# version 03
# baseline (speedup 1.0000x reference)
import contextlib
import numpy as np

import concourse.bass as bass
import concourse.mybir as mybir
from concourse.bass_utils import run_bass_kernel_spmd

F32 = mybir.dt.float32
BF16 = mybir.dt.bfloat16
I32 = mybir.dt.int32
ALU = mybir.AluOpType
ACTF = mybir.ActivationFunctionType
AX = mybir.AxisListType

NCORES = 8
B, T, C = 8, 4096, 1024
DI, DS = 2048, 16
I0, I1 = 2816, 3072
H, L, DH, CS = 16, 64, 64, 128
NB0, NB1 = I0 // 128, I1 // 128
LN_EPS = 1e-5
EXP_NEG20 = float(np.float32(np.exp(np.float32(-20.0))))
NK = DI // 128
TCH = 256
TS = 32
NSUB = TCH // TS
NCH2 = T // TCH
PP = 128
ENGINES = ["sync", "vector", "scalar", "tensor", "gpsimd"]
MASKVAL = -1e30
MAGIC = 12582912.0
SKIP_STAGES = set()


class Sem:
    def __init__(self, nc, name):
        self.h = nc.semaphore(name).__enter__()
        self.n = 0


class Builder:
    def __init__(self, nc):
        self.nc = nc
        self.items = []
        self._slots = {}
        for nm in ["dm", "dmb", "mm", "ev", "vv", "gg", "cc", "bar"]:
            setattr(self, nm, Sem(nc, "s_" + nm))

    def slot(self, name):
        if name not in self._slots:
            self._slots[name] = Sem(self.nc, "sl_" + name)
        return self._slots[name]

    def tick(self, eng, sem):
        sem.n += 1

        def fn(e, sem=sem):
            e.nop().then_inc(sem.h, 1)
        self.items.append((eng, fn))

    def raw(self, eng, fn):
        self.items.append((eng, fn))

    def wait(self, eng, sem, at=None):
        v = sem.n if at is None else at
        if v > 0:
            self.items.append((eng, lambda e: e.wait_ge(sem.h, v)))

    def op(self, eng, meth, *args, sem=None, inc=1, **kw):
        s = sem
        if s is not None:
            s.n += inc

        def fn(e):
            i = getattr(e, meth)(*args, **kw)
            if s is not None:
                i.then_inc(s.h, inc)
        self.items.append((eng, fn))

    def dma(self, eng, out, in_, sem=None, **kw):
        s = sem if sem is not None else self.dm
        s.n += 16

        def fn(e):
            e.dma_start(out=out, in_=in_, **kw).then_inc(s.h, 16)
        self.items.append((eng, fn))

    def flush(self):
        nc = self.nc
        items, self.items = self.items, []
        if not items:
            return
        per = {e: [fn for (en, fn) in items if en == e] for e in ENGINES}
        with nc.Block() as blk:
            for e in ENGINES:
                fns = per[e]
                if not fns:
                    continue

                def mk(fns):
                    def emit(eng):
                        for fn in fns:
                            fn(eng)
                    return emit
                getattr(blk, e)(mk(fns))

    def barrier(self):
        self.bar.n += len(ENGINES)
        tgt = self.bar.n
        for e in ENGINES:
            def fn(eng, tgt=tgt):
                eng.nop().then_inc(self.bar.h, 1)
                eng.wait_ge(self.bar.h, tgt)
            self.items.append((e, fn))
        self.flush()


def build_program(phase_limit=99, debug_scr=()):
    nc = bass.Bass("TRN2", target_bir_lowering=False, detect_race_conditions=False)
    D, S = {}, {}

    def din(name, shape):
        D[name] = nc.dram_tensor(name, list(shape), F32, kind="ExternalInput").ap()

    din("x_in", (T, C))
    din("w_inT_cb", (2 * DI // 128, C, 128))
    din("conv_w", (128, NK * 4))
    din("conv_b", (128, NK))
    din("w_xT", (128, NK * 33))
    din("w_dt", (128, NK))
    din("b_dt", (128, NK))
    din("a_log", (1, DS))
    din("d_param", (128, NK))
    din("ln_g", (128, NK))
    din("ln_b", (128, NK))
    din("w_outT", (DI, C))
    din("clip_x", (1, 1))
    din("clip_down", (1, 1))
    din("w_gate0", (I0, C))
    din("w_up0", (I0, C))
    din("w_down0", (C, I0))
    din("q_hT", (64, H * L))
    din("w_kT", (C, C))
    din("w_vT", (C, C))
    din("w_gT", (C, C))
    din("w_pT", (C, C))
    din("w_gate1", (I1, C))
    din("w_up1", (I1, C))
    din("w_down1", (C, I1))
    din("identf", (128, 128))
    din("cos_t", (T, DH // 2))
    din("sin_t", (T, DH // 2))
    din("cmask", (L, CS))

    out_t = nc.dram_tensor("out", [T, C], F32, kind="ExternalOutput").ap()
    dbg = {}
    for nm in debug_scr:
        pass

    def scr(name, shape, dtype=F32):
        S[name] = nc.dram_tensor("scr_" + name, list(shape), dtype)

    scr("xn", (T, C))
    scr("xnT_bf", (C, T), BF16)
    scr("xc", (DI, T))
    scr("z", (DI, T))
    scr("prod_bf", (DI, T), BF16)
    scr("s0", (T, C))
    scr("gu_bf", (I0, T), BF16)
    scr("x1", (T, C))
    scr("xn1", (T, C))
    scr("xn1T_bf", (C, T), BF16)
    scr("gatesT_bf", (C, T), BF16)
    scr("s1", (T, C))
    scr("upr_bf", (I1, T), BF16)
    scr("gater_bf", (I1, T), BF16)
    scr("t2_bf", (I1, T), BF16)
    scr("t3_bf", (I1, T), BF16)
    scr("y_dbg", (DI, T))
    scr("xq_dbg", (T, C))
    scr("qs_dbg", (128, 32))
    scr("cc_dbg", (4, 128))
    scr("lmax_ph0c", (128, 32))
    scr("g0_dbg", (T, C))
    scr("qs2_dbg", (128, 32))
    scr("w0_dbg", (128, 2048))
    scr("winT", (2 * DI // 128, C, 128), BF16)
    scr("woutT", (DI, C), BF16)
    scr("wkT", (C, C), BF16)
    scr("wvT", (C, C), BF16)
    scr("wgT", (C, C), BF16)
    scr("wpT", (C, C), BF16)
    scr("wq_g0T", (NB0, C, 128), BF16)
    scr("wq_u0T", (NB0, C, 128), BF16)
    scr("wq_d0T", (I0, C), BF16)
    scr("wq_g1T", (NB1, C, 128), BF16)
    scr("wq_u1T", (NB1, C, 128), BF16)
    scr("wq_d1T", (I1, C), BF16)
    NCC = 7
    scr("cc_in", (NCC, 128))
    scr("cc_out", (NCC, 128))

    for nm in debug_scr:
        shp = list(S[nm].shape) if nm in S else None
        dt_ = S[nm].dtype if nm in S else F32
        dbg[nm] = nc.dram_tensor("dbg_" + nm, shp, dt_, kind="ExternalOutput").ap()

    bld = Builder(nc)
    es = contextlib.ExitStack()
    with es:
        def sb(name, shape, dtype=F32, st=None):
            return (st or es).enter_context(nc.sbuf_tensor(name, list(shape), dtype))

        def psb(name, shape, dtype=F32, st=None):
            return (st or es).enter_context(nc.psum_tensor(name, list(shape), dtype))

        ones_f = sb("ones_f", [PP, 1])
        ident_b = sb("ident_b", [PP, 128], BF16)
        conv_w_t = sb("conv_w_t", [PP, NK * 4])
        conv_b_t = sb("conv_b_t", [PP, NK])
        w_dt_t = sb("w_dt_t", [PP, NK])
        b_dt_t = sb("b_dt_t", [PP, NK])
        d_par_t = sb("d_par_t", [PP, NK])
        ln_g_t = sb("ln_g_t", [PP, NK])
        ln_b_t = sb("ln_b_t", [PP, NK])
        a_cols = sb("a_cols", [PP, DS])
        w_xT_b = sb("w_xT_b", [PP, NK * 33], BF16)
        qh_b = sb("qh_b", [64, H * L], BF16)
        cmask_t = sb("cmask_t", [L, CS])
        gmax = sb("gmax", [PP, 8])
        lmax = sb("lmax", [PP, 8])
        qs = sb("qs", [PP, 16])
        row128 = sb("row128", [1, 136])
        ps_util = psb("ps_util", [PP, 512])

        def BO(e, meth, *a, **kw):
            bld.op(e, meth, *a, **kw)

        def global_max(slot, cc_id):
            cin, cout = S["cc_in"].ap(), S["cc_out"].ap()
            bld.wait("gpsimd", bld.vv)
            bld.dma("gpsimd", cin[cc_id, :].unsqueeze(0), lmax[:, slot:slot + 1],
                    sem=bld.gg, allow_slow_non_contiguous=True)
            bld.wait("gpsimd", bld.gg)
            bld.op("gpsimd", "collective_compute", "AllReduce", ALU.max,
                   replica_groups=[list(range(NCORES))],
                   ins=[cin[cc_id:cc_id + 1, :]],
                   outs=[cout[cc_id:cc_id + 1, :]], sem=bld.cc)
            bld.wait("gpsimd", bld.cc)
            bld.dma("gpsimd", row128[0:1, 0:128], cout[cc_id:cc_id + 1, :],
                    sem=bld.gg)
            bld.wait("vector", bld.gg)
            BO("vector", "tensor_reduce", row128[0:1, 128:129], row128[0:1, 0:128],
               AX.X, ALU.max, sem=bld.vv)
            bld.wait("tensor", bld.vv)
            BO("tensor", "matmul", ps_util[:, 0:1],
               ones_f[0:1, 0:1].broadcast_to([1, 128]), row128[0:1, 128:129],
               start=True, stop=True, sem=bld.mm)
            bld.wait("scalar", bld.mm)
            BO("scalar", "copy", gmax[:, slot:slot + 1], ps_util[:, 0:1], sem=bld.ev)

        def derive_scale(slot, qmaxv, sig_col, inv_col, s_col, floor=None):
            bld.wait("vector", bld.ev)
            if sig_col is not None:
                BO("vector", "tensor_scalar", qs[:, s_col:s_col + 1],
                   gmax[:, slot:slot + 1], qs[:, sig_col:sig_col + 1], 1.0 / qmaxv,
                   ALU.mult, ALU.mult, sem=bld.vv)
            else:
                BO("vector", "tensor_scalar", qs[:, s_col:s_col + 1],
                   gmax[:, slot:slot + 1], 1.0 / qmaxv, None, ALU.mult, sem=bld.vv)
            bld.wait("vector", bld.vv)
            if floor is not None:
                BO("vector", "tensor_scalar", qs[:, s_col:s_col + 1],
                   qs[:, s_col:s_col + 1], floor, None, ALU.max, sem=bld.vv)
                bld.wait("vector", bld.vv)
            BO("vector", "reciprocal", qs[:, inv_col:inv_col + 1],
               qs[:, s_col:s_col + 1], sem=bld.vv)

        def emit_quant(dst_bf, src, inv_col, s_col, qhi, qlo, itmp, ftmp):
            BO("vector", "tensor_scalar", ftmp, src, qs[:, inv_col:inv_col + 1],
               float(qhi), ALU.mult, ALU.min)
            BO("vector", "tensor_scalar", itmp, ftmp, float(qlo), None, ALU.max)
            BO("vector", "tensor_scalar", dst_bf, itmp, qs[:, s_col:s_col + 1],
               None, ALU.mult)

        def emit_ln_tile(xin, xout_f32, xout_bf, stat, dump, F, absmax_slot=None):
            bld.wait("scalar", bld.vv)
            BO("scalar", "activation", xout_f32, xin, ACTF.Copy,
               accum_out=stat[:, 0:1], sem=bld.ev)
            bld.wait("vector", bld.ev)
            BO("vector", "tensor_scalar", stat[:, 0:1], stat[:, 0:1], 1.0 / F, None,
               ALU.mult, sem=bld.vv)
            bld.wait("vector", bld.vv)
            BO("vector", "tensor_scalar", xout_f32, xout_f32, stat[:, 0:1], None,
               ALU.subtract, sem=bld.vv)
            bld.wait("scalar", bld.vv)
            BO("scalar", "activation", dump, xout_f32, ACTF.Square,
               accum_out=stat[:, 1:2], sem=bld.ev)
            bld.wait("vector", bld.ev)
            BO("vector", "tensor_scalar", stat[:, 1:2], stat[:, 1:2], 1.0 / F, LN_EPS,
               ALU.mult, ALU.add, sem=bld.vv)
            bld.wait("scalar", bld.vv)
            BO("scalar", "activation", stat[:, 2:3], stat[:, 1:2], ACTF.Sqrt, sem=bld.ev)
            bld.wait("vector", bld.ev)
            BO("vector", "reciprocal", stat[:, 3:4], stat[:, 2:3], sem=bld.vv)
            bld.wait("vector", bld.vv)
            BO("vector", "tensor_scalar", xout_f32, xout_f32, stat[:, 3:4], None,
               ALU.mult, sem=bld.vv)
            if xout_bf is not None:
                BO("vector", "tensor_copy", xout_bf, xout_f32, sem=bld.vv)
            if absmax_slot is not None:
                BO("vector", "tensor_reduce", stat[:, 5:6], xout_f32, AX.X, ALU.max,
                   apply_absolute_value=True, sem=bld.vv)
                bld.wait("vector", bld.vv)
                BO("vector", "tensor_tensor", lmax[:, absmax_slot:absmax_slot + 1],
                   lmax[:, absmax_slot:absmax_slot + 1], stat[:, 5:6], ALU.max,
                   sem=bld.vv)

        ph0 = contextlib.ExitStack()
        idf = sb("idf", [PP, 128], F32, ph0)
        arow = sb("arow", [1, DS], F32, ph0)
        crow = sb("crow", [1, 2], F32, ph0)
        wx_st = sb("wx_st", [PP, NK * 33], F32, ph0)
        qh_st = sb("qh_st", [64, H * L], F32, ph0)

        bld.dma("sync", conv_w_t[:], D["conv_w"])
        bld.dma("sync", conv_b_t[:], D["conv_b"])
        bld.dma("sync", w_dt_t[:], D["w_dt"])
        bld.dma("sync", b_dt_t[:], D["b_dt"])
        bld.dma("sync", d_par_t[:], D["d_param"])
        bld.dma("sync", ln_g_t[:], D["ln_g"])
        bld.dma("sync", ln_b_t[:], D["ln_b"])
        bld.dma("sync", wx_st[:], D["w_xT"])
        bld.dma("sync", qh_st[:], D["q_hT"])
        bld.dma("sync", cmask_t[:], D["cmask"])
        bld.dma("sync", idf[:], D["identf"])
        bld.dma("sync", arow[:], D["a_log"])
        bld.dma("sync", crow[0:1, 0:1], D["clip_x"])
        bld.dma("sync", crow[0:1, 1:2], D["clip_down"])
        bld.wait("vector", bld.dm)
        BO("vector", "memset", ones_f[:], 1.0)
        BO("vector", "memset", lmax[:], 0.0)
        BO("vector", "tensor_scalar", arow[:], arow[:], 5.0, None, ALU.min, sem=bld.vv)
        bld.wait("scalar", bld.vv)
        bld.wait("scalar", bld.dm)
        BO("scalar", "copy", ident_b[:], idf[:])
        BO("scalar", "copy", w_xT_b[:], wx_st[:])
        BO("scalar", "copy", qh_b[:], qh_st[:])
        BO("scalar", "activation", arow[:], arow[:], ACTF.Exp)
        BO("scalar", "activation", crow[:], crow[:], ACTF.Sigmoid, sem=bld.ev)
        bld.wait("vector", bld.ev)
        BO("vector", "tensor_scalar", arow[:], arow[:], -1.0, None, ALU.mult, sem=bld.vv)
        bld.wait("tensor", bld.vv)
        BO("tensor", "matmul", ps_util[:, 0:DS],
           ones_f[0:1, 0:1].broadcast_to([1, 128]), arow[0:1, :],
           start=True, stop=True)
        BO("tensor", "matmul", ps_util[:, 32:34],
           ones_f[0:1, 0:1].broadcast_to([1, 128]), crow[0:1, :],
           start=True, stop=True, sem=bld.mm)
        bld.wait("scalar", bld.mm)
        BO("scalar", "copy", a_cols[:], ps_util[:, 0:DS])
        BO("scalar", "copy", qs[:, 14:16], ps_util[:, 32:34], sem=bld.ev)
        bld.barrier()
        ph0.close()

        ph0b = contextlib.ExitStack()
        wst = [sb(f"wst{i}", [PP, 512], F32, ph0b) for i in range(2)]
        wbt = [sb(f"wbt{i}", [PP, 512], BF16, ph0b) for i in range(2)]
        s_wc = [bld.slot("wc0"), bld.slot("wc1")]
        step = 0

        def cast_stream(pairs):
            nonlocal step
            for (srcv, dstv) in pairs:
                sl = step % 2
                bld.wait("sync", s_wc[sl])
                bld.dma("sync", wst[sl][:, 0:512], srcv)
                bld.wait("scalar", bld.dm)
                BO("scalar", "copy", wbt[sl][:, 0:512], wst[sl][:, 0:512], sem=bld.ev)
                bld.wait("sync", bld.ev)
                bld.dma("sync", dstv, wbt[sl][:, 0:512], sem=s_wc[sl])
                step += 1

        pairs = []
        for eb in range(2 * DI // 128):
            for q in range(2):
                pairs.append((
                    D["w_inT_cb"][eb, q * 512:(q + 1) * 512, :].rearrange(
                        "(a p) m -> p a m", p=128),
                    S["winT"].ap()[eb, q * 512:(q + 1) * 512, :].rearrange(
                        "(a p) m -> p a m", p=128)))
        for (src2, dst2, K2, M2) in [("w_outT", "woutT", DI, C), ("w_kT", "wkT", C, C),
                                     ("w_vT", "wvT", C, C), ("w_gT", "wgT", C, C),
                                     ("w_pT", "wpT", C, C)]:
            for kt in range(K2 // 128):
                for mh in range(M2 // 512):
                    pairs.append((
                        D[src2][kt * 128:(kt + 1) * 128, mh * 512:(mh + 1) * 512],
                        S[dst2].ap()[kt * 128:(kt + 1) * 128, mh * 512:(mh + 1) * 512]))
        if phase_limit >= 2 and "ph0b" not in SKIP_STAGES:
            cast_stream(pairs)
        bld.barrier()
        ph0b.close()

        ph0c = contextlib.ExitStack()
        wq_st = sb("wq_st", [PP, I1], F32, ph0c)
        wq_i = sb("wq_i", [PP, I1], I32, ph0c)
        wq_b = sb("wq_b", [PP, I1], BF16, ph0c)
        wq_m = sb("wq_m", [PP, 96], F32, ph0c)
        wq_inv = sb("wq_inv", [PP, 96], F32, ph0c)
        tr_o = sb("tr_o", [PP, 512], BF16, ph0c)
        ps_tr = psb("ps_tr", [PP, 512], BF16, ph0c)
        s_q_store = bld.slot("q_store")

        def quant_weight(src2, dst2, rows, K2, lpbq, dst_cb):
            qmaxv = 7.0 if lpbq else 127.0
            qlov = -8.0 if lpbq else -128.0
            for rt in range(rows // 128):
                bld.wait("sync", bld.mm)
                bld.wait("sync", bld.vv)
                bld.dma("sync", wq_st[:, 0:K2], D[src2][rt * 128:(rt + 1) * 128, :])
                bld.wait("vector", bld.dm)
                if lpbq:
                    nb = K2 // 32
                    wv3 = wq_st[:, 0:K2].rearrange("p (b i) -> p b i", i=32)
                    if "qv_red" not in SKIP_STAGES:
                        BO("vector", "tensor_reduce", wq_m[:, 0:nb], wv3, AX.X,
                           ALU.max, apply_absolute_value=True, sem=bld.vv)
                        bld.wait("vector", bld.vv)
                        BO("vector", "tensor_scalar", wq_m[:, 0:nb], wq_m[:, 0:nb],
                           1.0 / qmaxv, 1e-12, ALU.mult, ALU.max, sem=bld.vv)
                        bld.wait("vector", bld.vv)
                        BO("vector", "reciprocal", wq_inv[:, 0:nb], wq_m[:, 0:nb],
                           sem=bld.vv)
                        bld.wait("vector", bld.vv)
                    iexp = wq_inv[:, 0:nb].unsqueeze(2).broadcast_to([PP, nb, 32])
                    mexp = wq_m[:, 0:nb].unsqueeze(2).broadcast_to([PP, nb, 32])
                    if "qv_tt1" not in SKIP_STAGES:
                        BO("vector", "tensor_tensor", wv3, wv3, iexp, ALU.mult)
                    if "qv_clip" not in SKIP_STAGES:
                        BO("vector", "tensor_scalar", wq_st[:, 0:K2], wq_st[:, 0:K2],
                           qmaxv, qlov, ALU.min, ALU.max)
                        BO("vector", "tensor_scalar", wq_st[:, 0:K2], wq_st[:, 0:K2],
                           MAGIC, MAGIC, ALU.add, ALU.subtract)
                    if "qv_tt2" not in SKIP_STAGES:
                        BO("vector", "tensor_tensor",
                           wq_b[:, 0:K2].rearrange("p (b i) -> p b i", i=32),
                           wv3, mexp, ALU.mult, sem=bld.vv)
                    else:
                        BO("vector", "tensor_copy", wq_b[:, 0:K2], wq_st[:, 0:K2],
                           sem=bld.vv)
                else:
                    BO("vector", "tensor_reduce", wq_m[:, 0:1], wq_st[:, 0:K2],
                       AX.X, ALU.max, apply_absolute_value=True, sem=bld.vv)
                    bld.wait("vector", bld.vv)
                    BO("vector", "tensor_scalar", wq_m[:, 0:1], wq_m[:, 0:1],
                       1.0 / qmaxv, None, ALU.mult, sem=bld.vv)
                    bld.wait("vector", bld.vv)
                    BO("vector", "reciprocal", wq_inv[:, 0:1], wq_m[:, 0:1],
                       sem=bld.vv)
                    bld.wait("vector", bld.vv)
                    BO("vector", "tensor_scalar", wq_st[:, 0:K2], wq_st[:, 0:K2],
                       wq_inv[:, 0:1], qmaxv, ALU.mult, ALU.min)
                    BO("vector", "tensor_scalar", wq_st[:, 0:K2], wq_st[:, 0:K2],
                       qlov, MAGIC, ALU.max, ALU.add)
                    BO("vector", "tensor_scalar", wq_st[:, 0:K2], wq_st[:, 0:K2],
                       MAGIC, None, ALU.subtract)
                    BO("vector", "tensor_scalar", wq_b[:, 0:K2], wq_st[:, 0:K2],
                       wq_m[:, 0:1], None, ALU.mult, sem=bld.vv)
                for kt in range(K2 // 128):
                    q = kt % 4
                    bld.wait("tensor", bld.vv)
                    bld.wait("tensor", bld.ev)
                    BO("tensor", "transpose", ps_tr[:, q * 128:(q + 1) * 128],
                       wq_b[:, kt * 128:(kt + 1) * 128], ident_b[:], sem=bld.mm)
                    bld.wait("scalar", bld.mm)
                    bld.wait("scalar", s_q_store)
                    BO("scalar", "copy", tr_o[:, q * 128:(q + 1) * 128],
                       ps_tr[:, q * 128:(q + 1) * 128], sem=bld.ev)
                    bld.wait("sync", bld.ev)
                    if dst_cb:
                        dview = S[dst2].ap()[rt, kt * 128:(kt + 1) * 128, :]
                    else:
                        dview = S[dst2].ap()[kt * 128:(kt + 1) * 128,
                                             rt * 128:(rt + 1) * 128]
                    bld.dma("sync", dview, tr_o[:, q * 128:(q + 1) * 128],
                            sem=s_q_store)

        if phase_limit >= 2 and "ph0c" not in SKIP_STAGES:
            qw_calls = [("w_gate0", "wq_g0T", I0, C, False, True),
                        ("w_up0", "wq_u0T", I0, C, False, True),
                        ("w_down0", "wq_d0T", C, I0, False, False),
                        ("w_gate1", "wq_g1T", I1, C, True, True),
                        ("w_up1", "wq_u1T", I1, C, True, True),
                        ("w_down1", "wq_d1T", C, I1, True, False)]
            for qi, qargs in enumerate(qw_calls):
                if f"qw{qi}" not in SKIP_STAGES:
                    quant_weight(*qargs)
        bld.barrier()
        if "lmax_ph0c" in dbg:
            sd = S["lmax_ph0c"].ap()
            bld.dma("sync", sd[:, 0:8], gmax[:, 0:8])
            bld.dma("sync", sd[:, 8:16], lmax[:, 0:8])
            bld.dma("sync", sd[:, 16:32], qs[:, 0:16], sem=bld.dm)
            bld.wait("sync", bld.dm)
        ph0c.close()

        if phase_limit >= 1:
            ph1 = contextlib.ExitStack()
            xt = sb("p1x", [PP, C], F32, ph1)
            xnt = sb("p1n", [PP, C], F32, ph1)
            xnb = sb("p1b", [PP, C], BF16, ph1)
            stat = sb("p1s", [PP, 8], F32, ph1)
            trb = sb("p1t", [PP, C], BF16, ph1)
            dmp = sb("p1d", [PP, C], F32, ph1)
            ps_t1 = psb("p1p", [PP, C], BF16, ph1)
            s_p1 = bld.slot("p1st")
            for it in range(T // 128):
                bld.wait("sync", s_p1)
                bld.wait("sync", bld.vv)
                bld.wait("sync", bld.ev)
                bld.dma("sync", xt[:], D["x_in"][it * 128:(it + 1) * 128, :])
                bld.wait("scalar", bld.dm)
                bld.wait("vector", bld.dm)
                emit_ln_tile(xt[:], xnt[:], xnb[:], stat, dmp[:], C, absmax_slot=0)
                bld.wait("sync", bld.vv)
                bld.dma("sync", S["xn"].ap()[it * 128:(it + 1) * 128, :], xnt[:],
                        sem=s_p1)
                bld.wait("tensor", bld.vv)
                bld.wait("tensor", bld.ev)
                for cb in range(8):
                    BO("tensor", "transpose", ps_t1[:, cb * 128:(cb + 1) * 128],
                       xnb[:, cb * 128:(cb + 1) * 128], ident_b[:],
                       sem=bld.mm if cb == 7 else None)
                bld.wait("scalar", bld.mm)
                BO("scalar", "copy", trb[:], ps_t1[:], sem=bld.ev)
                bld.wait("sync", bld.ev)
                for cb in range(8):
                    bld.dma("sync", S["xnT_bf"].ap()[cb * 128:(cb + 1) * 128,
                                                     it * 128:(it + 1) * 128],
                            trb[:, cb * 128:(cb + 1) * 128], sem=s_p1)
            global_max(0, 0)
            derive_scale(0, 127.0, 14, 0, 1)
            if "qs_dbg" in dbg:
                bld.wait("sync", bld.vv)
                bld.wait("sync", bld.ev)
                bld.dma("sync", S["qs_dbg"].ap()[:, 0:16], qs[:, 0:16])
                bld.dma("sync", S["qs_dbg"].ap()[:, 16:24], gmax[:, 0:8])
                bld.dma("sync", S["qs_dbg"].ap()[:, 24:32], lmax[:, 0:8])
            if "cc_dbg" in dbg:
                bld.wait("sync", bld.vv)
                bld.dma("sync", S["cc_dbg"].ap()[0:1, :], S["cc_in"].ap()[0:1, :])
                bld.dma("sync", S["cc_dbg"].ap()[1:2, :], S["cc_out"].ap()[0:1, :])
                bld.dma("sync", S["cc_dbg"].ap()[2:3, :], row128[0:1, 0:128])
            bld.barrier()
            ph1.close()


        if phase_limit >= 2 and "ph2a" not in SKIP_STAGES:
            ph2a = contextlib.ExitStack()
            wbuf = [sb(f"a2w{i}", [PP, 8 * 128], BF16, ph2a) for i in range(2)]
            xbuf = [sb(f"a2x{i}", [PP, 8 * 512], BF16, ph2a) for i in range(2)]
            est = [sb(f"a2e{i}", [PP, 512], F32, ph2a) for i in range(2)]
            ps2 = [psb(f"a2p{i}", [PP, 512], F32, ph2a) for i in range(2)]
            s_wb = [bld.slot("a2wb0"), bld.slot("a2wb1")]
            s_xb = [bld.slot("a2xb0"), bld.slot("a2xb1")]
            s_es = [bld.slot("a2es0"), bld.slot("a2es1")]
            s_pb = [bld.slot("a2pb0"), bld.slot("a2pb1")]
            for tt in range(T // 512):
                xsl = tt % 2
                bld.wait("sync", s_xb[xsl])
                bld.dma("sync", xbuf[xsl][:],
                        S["xnT_bf"].ap()[:, tt * 512:(tt + 1) * 512].rearrange(
                            "(q p) m -> p q m", p=128))
                for eb in range(2 * DI // 128):
                    wsl = eb % 2
                    psl = eb % 2
                    bld.wait("sync", s_wb[wsl])
                    bld.dma("sync", wbuf[wsl][:],
                            S["winT"].ap()[eb].rearrange("(q p) m -> p q m", p=128))
                    bld.wait("tensor", bld.dm)
                    bld.wait("tensor", s_pb[psl])
                    for cb in range(8):
                        BO("tensor", "matmul", ps2[psl][:],
                           wbuf[wsl][:, cb * 128:(cb + 1) * 128],
                           xbuf[xsl][:, cb * 512:(cb + 1) * 512],
                           start=(cb == 0), stop=(cb == 7),
                           sem=bld.mm if cb == 7 else None)
                    bld.tick("tensor", s_wb[wsl])
                    if eb == 2 * DI // 128 - 1:
                        bld.tick("tensor", s_xb[xsl])
                    esl = eb % 2
                    bld.wait("scalar", bld.mm)
                    bld.wait("scalar", s_es[esl])
                    BO("scalar", "copy", est[esl][:], ps2[psl][:], sem=bld.ev)
                    bld.tick("scalar", s_pb[psl])
                    bld.wait("sync", bld.ev)
                    dstb = S["xc"] if eb < NK else S["z"]
                    er = eb if eb < NK else eb - NK
                    bld.dma("sync", dstb.ap()[er * 128:(er + 1) * 128,
                                              tt * 512:(tt + 1) * 512],
                            est[esl][:], sem=s_es[esl])
            bld.barrier()
            ph2a.close()


        if phase_limit >= 3:
            ph2b = contextlib.ExitStack()
            xc_t = sb("xc_t", [PP, NK * (TCH + 3)], F32, ph2b)
            u_t = sb("u_t", [PP, NK * TCH], F32, ph2b)
            ub_t = sb("ub_t", [PP, NK * TCH], BF16, ph2b)
            zs_t = sb("zs_t", [PP, NK * TCH], F32, ph2b)
            dt_t = sb("dt_t", [PP, NK * TCH], F32, ph2b)
            xdT = sb("xdT", [33, TCH], F32, ph2b)
            xdf0 = sb("xdf0", [1, TCH], F32, ph2b)
            xdfs = sb("xdfs", [1, 33 * TS], F32, ph2b)
            dA = sb("dA", [PP, TS * 256], F32, ph2b)
            dBu = sb("dBu", [PP, TS * 256], F32, ph2b)
            hist = sb("hist", [PP, (TS + 1) * 256], F32, ph2b)
            hcs = sb("hcs", [PP, 8 * 256], F32, ph2b)
            brep = sb("brep", [PP, DS * TS], F32, ph2b)
            crep = sb("crep", [PP, DS * TS], F32, ph2b)
            dus = sb("dus", [PP, NK * TS], F32, ph2b)
            ysub = sb("ysub", [PP, TS * NK], F32, ph2b)
            ysq = sb("ysq", [PP, TS * NK], F32, ph2b)
            yrow = sb("yrow", [1, 4 * TS], F32, ph2b)
            prodb = sb("prodb", [PP, NK * TCH], BF16, ph2b)
            cm100 = sb("cm100", [PP, 4], F32, ph2b)
            ps_xd = psb("ps_xd", [33, TCH], F32, ph2b)
            ps_dr = psb("ps_dr", [PP, TCH], F32, ph2b)
            ps_bc2 = psb("ps_bc2", [PP, DS * TS], F32, ph2b)
            ps_row = psb("ps_row", [1, 2 * TS], F32, ph2b)
            ps_mu = psb("ps_mu", [PP, 2 * TS], F32, ph2b)
            hist_last = hist[:, TS * 256:(TS + 1) * 256]

            s_gd = bld.slot("b2_dbu")
            s_vs = bld.slot("b2_stepsdone")
            s_chv = bld.slot("b2_chv")
            s_chs = bld.slot("b2_chs")
            s_prod = bld.slot("b2_prod")

            BO("vector", "memset", hist[:], 0.0)
            BO("vector", "memset", cm100[:, 0:1], -100.0)
            BO("vector", "memset", cm100[:, 1:2], 100.0)
            BO("vector", "memset", cm100[:, 2:3], EXP_NEG20, sem=bld.vv)
            cm100b = cm100[:, 0:1].broadcast_to([PP, 256])
            c100b = cm100[:, 1:2].broadcast_to([PP, 256])
            e20b = cm100[:, 2:3].broadcast_to([PP, TS * 256])

            xcv = lambda k: xc_t[:, k * (TCH + 3):(k + 1) * (TCH + 3)]
            uv = lambda k: u_t[:, k * TCH:(k + 1) * TCH]
            ubv = lambda k: ub_t[:, k * TCH:(k + 1) * TCH]
            zv = lambda k: zs_t[:, k * TCH:(k + 1) * TCH]
            dtv = lambda k: dt_t[:, k * TCH:(k + 1) * TCH]
            dt3 = dt_t[:].rearrange("p (c t) -> p t c", c=NK)
            u3 = u_t[:].rearrange("p (c t) -> p t c", c=NK)
            ysub3 = ysub[:].rearrange("p (t c) -> p t c", t=TS)
            ysq3 = ysq[:].rearrange("p (t c) -> p t c", t=TS)

            for ch in range(NCH2):
                t0 = ch * TCH
                bld.wait("sync", s_chv)
                bld.wait("sync", s_chs)
                for k in range(NK):
                    if ch == 0:
                        bld.dma("sync", xcv(k)[:, 3:],
                                S["xc"].ap()[k * 128:(k + 1) * 128, t0:t0 + TCH])
                    else:
                        bld.dma("sync", xcv(k)[:, :],
                                S["xc"].ap()[k * 128:(k + 1) * 128, t0 - 3:t0 + TCH])
                    bld.dma("sync", zv(k)[:, :],
                            S["z"].ap()[k * 128:(k + 1) * 128, t0:t0 + TCH])
                bld.wait("vector", bld.dm)
                if ch == 0:
                    for k in range(NK):
                        BO("vector", "memset", xcv(k)[:, 0:3], 0.0)
                for k in range(NK):
                    w = lambda j: conv_w_t[:, k * 4 + j:k * 4 + j + 1]
                    BO("vector", "tensor_scalar", uv(k), xcv(k)[:, 0:TCH], w(0), None,
                       ALU.mult)
                    for j in range(1, 4):
                        BO("vector", "scalar_tensor_tensor", uv(k), xcv(k)[:, j:j + TCH],
                           w(j), uv(k), ALU.mult, ALU.add,
                           sem=bld.vv if j == 3 else None)
                bld.wait("scalar", bld.vv)
                bld.wait("scalar", bld.dm)
                for k in range(NK):
                    BO("scalar", "activation", uv(k), uv(k), ACTF.Silu,
                       bias=conv_b_t[:, k:k + 1], scale=1.0)
                    BO("scalar", "copy", ubv(k), uv(k))
                    BO("scalar", "activation", zv(k), zv(k), ACTF.Silu,
                       sem=bld.ev if k == NK - 1 else None)
                bld.wait("tensor", bld.ev)
                for k in range(NK):
                    BO("tensor", "matmul", ps_xd[:, :],
                       w_xT_b[:, k * 33:(k + 1) * 33], ubv(k),
                       start=(k == 0), stop=(k == NK - 1),
                       sem=bld.mm if k == NK - 1 else None)
                bld.wait("scalar", bld.mm)
                BO("scalar", "copy", xdT[:, :], ps_xd[:, :], sem=bld.ev)
                bld.wait("sync", bld.ev)
                bld.dma("sync", xdf0[:], xdT[0:1, :])
                bld.wait("tensor", bld.dm)
                BO("tensor", "matmul", ps_dr[:, :],
                   ones_f[0:1, 0:1].broadcast_to([1, 128]), xdf0[0:1, :],
                   start=True, stop=True, sem=bld.mm)
                bld.wait("scalar", bld.mm)
                for k in range(NK):
                    BO("scalar", "activation", dtv(k), ps_dr[:, :], ACTF.Exp,
                       bias=b_dt_t[:, k:k + 1], scale=w_dt_t[:, k:k + 1])
                    BO("scalar", "activation", dtv(k), dtv(k), ACTF.Ln, bias=1.0,
                       sem=bld.ev if k == NK - 1 else None)
                bld.wait("vector", bld.ev)
                for k in range(NK):
                    BO("vector", "tensor_scalar", dtv(k), dtv(k), 1e-4, 10.0,
                       ALU.max, ALU.min, sem=bld.vv if k == NK - 1 else None)

                for s in range(NSUB):
                    ts0 = s * TS
                    bld.wait("sync", bld.mm)
                    bld.dma("sync", xdfs[0:1, :], xdT[:, ts0:ts0 + TS],
                            allow_slow_non_contiguous=True)
                    bld.wait("tensor", bld.vv)
                    bld.wait("tensor", bld.dm)
                    bld.wait("tensor", bld.ev)
                    for n in range(DS):
                        BO("tensor", "matmul", ps_bc2[:, n * TS:(n + 1) * TS],
                           ones_f[0:1, 0:1].broadcast_to([1, 128]),
                           xdfs[0:1, (1 + n) * TS:(2 + n) * TS],
                           start=True, stop=True,
                           sem=bld.mm if n == DS - 1 else None)
                    bld.wait("scalar", bld.mm)
                    bld.wait("scalar", s_gd)
                    bld.wait("scalar", bld.vv)
                    BO("scalar", "copy", brep[:], ps_bc2[:], sem=bld.ev)
                    bld.wait("tensor", bld.ev)
                    for n in range(DS):
                        BO("tensor", "matmul", ps_bc2[:, n * TS:(n + 1) * TS],
                           ones_f[0:1, 0:1].broadcast_to([1, 128]),
                           xdfs[0:1, (17 + n) * TS:(18 + n) * TS],
                           start=True, stop=True,
                           sem=bld.mm if n == DS - 1 else None)
                    bld.wait("scalar", bld.mm)
                    BO("scalar", "copy", crep[:], ps_bc2[:], sem=bld.ev)
                    bld.wait("scalar", s_vs)
                    dA4 = dA[:].rearrange("p (t c n) -> p t c n", t=TS, c=NK)
                    for n in range(DS):
                        BO("scalar", "activation", dA4[:, :, :, n],
                           dt3[:, ts0:ts0 + TS, :], ACTF.Exp,
                           scale=a_cols[:, n:n + 1],
                           sem=bld.ev if n == DS - 1 else None)
                    if s == NSUB - 1:
                        bld.tick("scalar", s_chs)
                    BO("vector", "tensor_tensor",
                       dus[:].rearrange("p (t c) -> p t c", t=TS),
                       dt3[:, ts0:ts0 + TS, :], u3[:, ts0:ts0 + TS, :], ALU.mult,
                       sem=bld.vv)
                    bld.wait("gpsimd", bld.vv)
                    bld.wait("gpsimd", bld.ev)
                    bld.wait("gpsimd", s_vs)
                    BO("gpsimd", "tensor_tensor",
                       dBu[:].rearrange("p (t c n) -> p t c n", t=TS, c=NK),
                       dus[:].rearrange("p (t c) -> p t c", t=TS).unsqueeze(3)
                           .broadcast_to([PP, TS, NK, DS]),
                       brep[:].rearrange("p (n t) -> p t n", n=DS).unsqueeze(2)
                           .broadcast_to([PP, TS, NK, DS]), ALU.mult)
                    BO("gpsimd", "tensor_scalar", dBu[:], dBu[:], 10.0, -10.0,
                       ALU.min, ALU.max)
                    bld.tick("gpsimd", s_gd)
                    BO("vector", "tensor_copy", hist[:, 0:256], hist_last)
                    bld.wait("vector", bld.ev)
                    BO("vector", "tensor_tensor", dA[:], dA[:], e20b, ALU.max)
                    bld.wait("vector", s_gd)
                    for grp in range(TS // 8):
                        for tl in range(grp * 8, grp * 8 + 8):
                            hp = hist[:, tl * 256:(tl + 1) * 256]
                            hc = hist[:, (tl + 1) * 256:(tl + 2) * 256]
                            av = dA[:, tl * 256:(tl + 1) * 256]
                            bv = dBu[:, tl * 256:(tl + 1) * 256]
                            BO("vector", "tensor_tensor", hc, av, hp, ALU.mult)
                            BO("vector", "tensor_tensor", hc, hc, bv, ALU.add)
                            BO("vector", "tensor_tensor", hc, hc, c100b, ALU.min)
                            BO("vector", "tensor_tensor", hc, hc, cm100b, ALU.max)
                        hv = hist[:, (1 + grp * 8) * 256:(1 + grp * 8 + 8) * 256]
                        BO("vector", "tensor_tensor",
                           hcs[:].rearrange("p (t c n) -> p t c n", t=8, c=NK),
                           hv.rearrange("p (t c n) -> p t c n", t=8, c=NK),
                           crep[:].rearrange("p (n t) -> p t n", n=DS)[
                               :, grp * 8:grp * 8 + 8, :].unsqueeze(2)
                               .broadcast_to([PP, 8, NK, DS]), ALU.mult)
                        BO("vector", "tensor_reduce",
                           ysub3[:, grp * 8:grp * 8 + 8, :],
                           hcs[:].rearrange("p (t c n) -> p t c n", t=8, c=NK),
                           AX.X, ALU.add)
                    bld.tick("vector", s_vs)
                    if "y_dbg" in dbg:
                        bld.vv.n += 1
                        bld.raw("vector", lambda e: e.engine_nop().then_inc(bld.vv.h, 1))
                        bld.wait("sync", bld.vv)
                        for k in range(NK):
                            bld.dma("sync", S["y_dbg"].ap()[
                                k * 128:(k + 1) * 128,
                                t0 + ts0:t0 + ts0 + TS],
                                ysub3[:, :, k], sem=bld.dmb,
                                allow_slow_non_contiguous=True)
                        bld.wait("vector", bld.dmb)
                    BO("vector", "tensor_reduce", ysq[:, 0:TS], ysub3, AX.X, ALU.add,
                       sem=bld.vv)
                    bld.wait("tensor", bld.vv)
                    BO("tensor", "matmul", ps_row[0:1, 0:TS], ones_f[:, 0:1],
                       ysq[:, 0:TS], start=True, stop=True, sem=bld.mm)
                    bld.wait("vector", bld.mm)
                    BO("vector", "tensor_scalar", yrow[0:1, 0:TS], ps_row[0:1, 0:TS],
                       1.0 / DI, None, ALU.mult, sem=bld.vv)
                    bld.wait("tensor", bld.vv)
                    BO("tensor", "matmul", ps_mu[:, 0:TS],
                       ones_f[0:1, 0:1].broadcast_to([1, 128]), yrow[0:1, 0:TS],
                       start=True, stop=True, sem=bld.mm)
                    bld.wait("vector", bld.mm)
                    mu3 = ps_mu[:, 0:TS].unsqueeze(2).broadcast_to([PP, TS, NK])
                    BO("vector", "tensor_tensor", ysub3, ysub3, mu3, ALU.subtract,
                       sem=bld.vv)
                    bld.wait("scalar", bld.vv)
                    BO("scalar", "activation", ysq[:, 0:TS * NK], ysub[:], ACTF.Square,
                       sem=bld.ev)
                    bld.wait("vector", bld.ev)
                    BO("vector", "tensor_reduce", ysq[:, 0:TS], ysq3, AX.X, ALU.add,
                       sem=bld.vv)
                    bld.wait("tensor", bld.vv)
                    BO("tensor", "matmul", ps_row[0:1, TS:2 * TS], ones_f[:, 0:1],
                       ysq[:, 0:TS], start=True, stop=True, sem=bld.mm)
                    bld.wait("vector", bld.mm)
                    yv = yrow[0:1, TS:2 * TS]
                    BO("vector", "tensor_scalar", yv, ps_row[0:1, TS:2 * TS],
                       1.0 / DI, LN_EPS, ALU.mult, ALU.add, sem=bld.vv)
                    bld.wait("scalar", bld.vv)
                    BO("scalar", "activation", yrow[0:1, 2 * TS:3 * TS], yv, ACTF.Sqrt,
                       sem=bld.ev)
                    bld.wait("vector", bld.ev)
                    s0v = yrow[0:1, 2 * TS:3 * TS]
                    r0v = yrow[0:1, 3 * TS:4 * TS]
                    BO("vector", "reciprocal", r0v, s0v, sem=bld.vv)
                    bld.wait("vector", bld.vv)
                    bld.wait("tensor", bld.vv)
                    BO("tensor", "matmul", ps_mu[:, TS:2 * TS],
                       ones_f[0:1, 0:1].broadcast_to([1, 128]), r0v,
                       start=True, stop=True, sem=bld.mm)
                    bld.wait("vector", bld.mm)
                    rst3 = ps_mu[:, TS:2 * TS].unsqueeze(2).broadcast_to([PP, TS, NK])
                    lng3 = ln_g_t[:, 0:NK].unsqueeze(1).broadcast_to([PP, TS, NK])
                    lnb3 = ln_b_t[:, 0:NK].unsqueeze(1).broadcast_to([PP, TS, NK])
                    dpr3 = d_par_t[:, 0:NK].unsqueeze(1).broadcast_to([PP, TS, NK])
                    BO("vector", "tensor_tensor", ysub3, ysub3, rst3, ALU.mult)
                    BO("vector", "tensor_tensor", ysub3, ysub3, lng3, ALU.mult)
                    BO("vector", "tensor_tensor", ysub3, ysub3, lnb3, ALU.add)
                    usl = u3[:, ts0:ts0 + TS, :]
                    BO("vector", "tensor_tensor", ysq3, usl, dpr3, ALU.mult)
                    BO("vector", "tensor_tensor", ysub3, ysub3, ysq3, ALU.add)
                    if s == 0:
                        bld.wait("vector", s_prod)
                    zsl = zs_t[:].rearrange("p (c t) -> p t c", c=NK)[
                        :, ts0:ts0 + TS, :]
                    BO("vector", "tensor_tensor",
                       prodb[:].rearrange("p (c t) -> p t c", c=NK)[
                           :, ts0:ts0 + TS, :],
                       ysub3, zsl, ALU.mult, sem=bld.vv)
                bld.tick("vector", s_chv)
                bld.wait("sync", bld.vv)
                for k in range(NK):
                    bld.dma("sync", S["prod_bf"].ap()[k * 128:(k + 1) * 128,
                                                      t0:t0 + TCH],
                            prodb[:, k * TCH:(k + 1) * TCH], sem=s_prod)
            bld.barrier()
            ph2b.close()


        if phase_limit >= 4:
            ph2c = contextlib.ExitStack()
            pbuf = [sb(f"c2p{i}", [PP, NK * 512], BF16, ph2c) for i in range(2)]
            wob = sb("c2w", [PP, NK * 1024], BF16, ph2c)
            sst = [sb(f"c2s{i}", [PP, 1024], F32, ph2c) for i in range(2)]
            ps2c = [psb(f"c2ps{i}", [PP, 512], F32, ph2c) for i in range(4)]
            s_pb2 = [bld.slot("c2pb0"), bld.slot("c2pb1")]
            s_ss = [bld.slot("c2ss0"), bld.slot("c2ss1")]
            s_pe = [bld.slot(f"c2pe{i}") for i in range(4)]
            for k in range(NK):
                bld.dma("sync", wob[:, k * 1024:(k + 1) * 1024],
                        S["woutT"].ap()[k * 128:(k + 1) * 128, :])
            for tt in range(T // 512):
                psl = tt % 2
                bld.wait("sync", s_pb2[psl])
                for k in range(NK):
                    bld.dma("sync", pbuf[psl][:, k * 512:(k + 1) * 512],
                            S["prod_bf"].ap()[k * 128:(k + 1) * 128,
                                              tt * 512:(tt + 1) * 512])
                for ts4 in range(4):
                    for oh in range(2):
                        pb = 2 * (ts4 % 2) + oh
                        bld.wait("tensor", bld.dm)
                        bld.wait("tensor", s_pe[pb])
                        for k in range(NK):
                            BO("tensor", "matmul", ps2c[pb][:],
                               pbuf[psl][:, k * 512 + ts4 * 128:
                                         k * 512 + (ts4 + 1) * 128],
                               wob[:, k * 1024 + oh * 512:k * 1024 + (oh + 1) * 512],
                               start=(k == 0), stop=(k == NK - 1),
                               sem=bld.mm if k == NK - 1 else None)
                        if ts4 == 3 and oh == 1:
                            bld.tick("tensor", s_pb2[psl])
                        ssl = ts4 % 2
                        bld.wait("scalar", bld.mm)
                        bld.wait("scalar", s_ss[ssl])
                        BO("scalar", "copy", sst[ssl][:, oh * 512:(oh + 1) * 512],
                           ps2c[pb][:], sem=bld.ev)
                        bld.tick("scalar", s_pe[pb])
                    bld.wait("sync", bld.ev)
                    ssl = ts4 % 2
                    bld.dma("sync",
                            S["s0"].ap()[tt * 512 + ts4 * 128:
                                         tt * 512 + (ts4 + 1) * 128, :],
                            sst[ssl][:], sem=s_ss[ssl])
            bld.barrier()
            ph2c.close()

        if phase_limit >= 5:
            ph3a = contextlib.ExitStack()
            xqT = [sb(f"d3xq{i}", [PP, T], BF16, ph3a) for i in range(8)]
            x3t = sb("d3x", [PP, C], F32, ph3a)
            q3b = sb("d3q", [PP, C], BF16, ph3a)
            q3i = sb("d3i", [PP, C], I32, ph3a)
            trb3 = sb("d3t", [PP, C], BF16, ph3a)
            wg3 = [sb(f"d3wg{i}", [PP, 1024], BF16, ph3a) for i in range(2)]
            wu3 = [sb(f"d3wu{i}", [PP, 1024], BF16, ph3a) for i in range(2)]
            gsil = sb("d3gs", [PP, 512], F32, ph3a)
            guf = sb("d3gu", [PP, 512], F32, ph3a)
            gub = sb("d3gb", [PP, 512], BF16, ph3a)
            st3 = sb("d3st", [PP, 8], F32, ph3a)
            ps_g3 = psb("d3pg", [PP, 512], F32, ph3a)
            ps_u3 = psb("d3pu", [PP, 512], F32, ph3a)
            ps_t3 = psb("d3pt", [PP, C], BF16, ph3a)
            s_w3 = [bld.slot("d3w0"), bld.slot("d3w1")]
            s_gu = bld.slot("d3gu")
            for it in range(T // 128):
                bld.wait("sync", bld.vv)
                bld.wait("sync", bld.ev)
                bld.dma("sync", x3t[:], S["xn"].ap()[it * 128:(it + 1) * 128, :])
                bld.wait("vector", bld.dm)
                BO("vector", "tensor_scalar", x3t[:], x3t[:], qs[:, 0:1], 127.0,
                   ALU.mult, ALU.min)
                BO("vector", "tensor_scalar", x3t[:], x3t[:], -128.0, MAGIC,
                   ALU.max, ALU.add)
                BO("vector", "tensor_scalar", q3b[:], x3t[:], MAGIC, qs[:, 1:2],
                   ALU.subtract, ALU.mult, sem=bld.vv)
                if "xq_dbg" in dbg:
                    BO("vector", "tensor_scalar", x3t[:], x3t[:], MAGIC, None,
                       ALU.subtract, sem=bld.vv)
                    bld.wait("sync", bld.vv)
                    bld.dma("sync", S["xq_dbg"].ap()[it * 128:(it + 1) * 128, :],
                            x3t[:])
                bld.wait("tensor", bld.vv)
                bld.wait("tensor", bld.ev)
                for cb in range(8):
                    BO("tensor", "transpose", ps_t3[:, cb * 128:(cb + 1) * 128],
                       q3b[:, cb * 128:(cb + 1) * 128], ident_b[:],
                       sem=bld.mm if cb == 7 else None)
                bld.wait("scalar", bld.mm)
                BO("scalar", "copy", trb3[:], ps_t3[:], sem=bld.ev)
                bld.wait("vector", bld.ev)
                for cb in range(8):
                    BO("vector", "tensor_copy", xqT[cb][:, it * 128:(it + 1) * 128],
                       trb3[:, cb * 128:(cb + 1) * 128],
                       sem=bld.vv if cb == 7 else None)
            for ib in range(NB0):
                wsl = ib % 2
                bld.wait("sync", s_w3[wsl])
                bld.dma("sync", wg3[wsl][:],
                        S["wq_g0T"].ap()[ib].rearrange("(q p) m -> p q m", p=128))
                bld.dma("sync", wu3[wsl][:],
                        S["wq_u0T"].ap()[ib].rearrange("(q p) m -> p q m", p=128))
                if ib == 0 and "w0_dbg" in dbg:
                    bld.wait("vector", bld.dm)
                    BO("vector", "tensor_copy", gsil[:, 0:512], wg3[0][:, 0:512])
                    BO("vector", "tensor_copy", guf[:, 0:512], wg3[0][:, 512:1024],
                       sem=bld.vv)
                    bld.wait("sync", bld.vv)
                    bld.dma("sync", S["w0_dbg"].ap()[:, 0:512], gsil[:, 0:512])
                    bld.dma("sync", S["w0_dbg"].ap()[:, 512:1024], guf[:, 0:512])
                    bld.dma("sync", trb3[:, 0:128],
                            S["wq_g0T"].ap()[0, 0:128, :], sem=bld.dm)
                    bld.wait("vector", bld.dm)
                    BO("vector", "tensor_copy", gsil[:, 0:128], trb3[:, 0:128],
                       sem=bld.vv)
                    bld.wait("sync", bld.vv)
                    bld.dma("sync", S["w0_dbg"].ap()[:, 1024:1152], gsil[:, 0:128])
                for tt in range(T // 512):
                    bld.wait("tensor", bld.dm)
                    bld.wait("tensor", bld.vv)
                    if tt == 0 and ib == 0:
                        pass
                    bld.wait("tensor", bld.ev)
                    for cb in range(8):
                        BO("tensor", "matmul", ps_g3[:],
                           wg3[wsl][:, cb * 128:(cb + 1) * 128],
                           xqT[cb][:, tt * 512:(tt + 1) * 512],
                           start=(cb == 0), stop=(cb == 7))
                    for cb in range(8):
                        BO("tensor", "matmul", ps_u3[:],
                           wu3[wsl][:, cb * 128:(cb + 1) * 128],
                           xqT[cb][:, tt * 512:(tt + 1) * 512],
                           start=(cb == 0), stop=(cb == 7),
                           sem=bld.mm if cb == 7 else None)
                    if tt == T // 512 - 1:
                        bld.tick("tensor", s_w3[wsl])
                    bld.wait("scalar", bld.mm)
                    BO("scalar", "activation", gsil[:], ps_g3[:], ACTF.Silu,
                       sem=bld.ev)
                    bld.wait("vector", bld.ev)
                    bld.wait("vector", s_gu)
                    BO("vector", "tensor_tensor", guf[:], gsil[:], ps_u3[:], ALU.mult)
                    BO("vector", "tensor_reduce", st3[:, 0:1], guf[:], AX.X, ALU.max,
                       apply_absolute_value=True, sem=bld.vv)
                    bld.wait("vector", bld.vv)
                    BO("vector", "tensor_tensor", lmax[:, 1:2], lmax[:, 1:2],
                       st3[:, 0:1], ALU.max)
                    BO("vector", "tensor_copy", gub[:], guf[:], sem=bld.vv)
                    bld.tick("vector", bld.ev)
                    bld.wait("sync", bld.vv)
                    bld.dma("sync", S["gu_bf"].ap()[ib * 128:(ib + 1) * 128,
                                                    tt * 512:(tt + 1) * 512],
                            gub[:], sem=s_gu)
            global_max(1, 1)
            derive_scale(1, 127.0, 15, 2, 3)
            bld.barrier()
            ph3a.close()

        if phase_limit >= 6:
            ph3b = contextlib.ExitStack()
            wd3 = [sb(f"e3wd{i}", [PP, 1024], BF16, ph3b) for i in range(NB0)]
            gu3 = sb("e3gu", [PP, NB0 * 512], BF16, ph3b)
            hq3 = sb("e3hq", [PP, NB0 * 512], BF16, ph3b)
            hqi = sb("e3hi", [PP, 512], I32, ph3b)
            hqf = sb("e3hf", [PP, 512], F32, ph3b)
            g3 = sb("e3g", [PP, C], F32, ph3b)
            xx3 = sb("e3x", [PP, C], F32, ph3b)
            ss3 = sb("e3s", [PP, C], F32, ph3b)
            xn1t = sb("e3n", [PP, C], F32, ph3b)
            xn1b = sb("e3nb", [PP, C], BF16, ph3b)
            dmp3 = sb("e3d", [PP, C], F32, ph3b)
            st3b = sb("e3st", [PP, 8], F32, ph3b)
            trb3b = sb("e3t", [PP, C], BF16, ph3b)
            ps_d3 = [psb(f"e3p{i}", [PP, 512], F32, ph3b) for i in range(2)]
            ps_t3b = psb("e3pt", [PP, C], BF16, ph3b)
            for k in range(NB0):
                bld.dma("sync", wd3[k][:],
                        S["wq_d0T"].ap()[k * 128:(k + 1) * 128, :])
            for tt in range(T // 512):
                bld.wait("sync", bld.vv)
                for k in range(NB0):
                    bld.dma("sync", gu3[:, k * 512:(k + 1) * 512],
                            S["gu_bf"].ap()[k * 128:(k + 1) * 128,
                                            tt * 512:(tt + 1) * 512])
                bld.wait("vector", bld.dm)
                for k in range(NB0):
                    BO("vector", "tensor_scalar", hqf[:], gu3[:, k * 512:(k + 1) * 512],
                       qs[:, 2:3], 127.0, ALU.mult, ALU.min)
                    BO("vector", "tensor_scalar", hqf[:], hqf[:], -128.0, MAGIC,
                       ALU.max, ALU.add)
                    BO("vector", "tensor_scalar", hq3[:, k * 512:(k + 1) * 512],
                       hqf[:], MAGIC, qs[:, 3:4], ALU.subtract, ALU.mult,
                       sem=bld.vv if k == NB0 - 1 else None)
                for t4 in range(4):
                    it = tt * 4 + t4
                    bld.wait("tensor", bld.vv)
                    bld.wait("tensor", bld.ev)
                    for oh in range(2):
                        for k in range(NB0):
                            BO("tensor", "matmul", ps_d3[oh][:],
                               hq3[:, k * 512 + t4 * 128:k * 512 + (t4 + 1) * 128],
                               wd3[k][:, oh * 512:(oh + 1) * 512],
                               start=(k == 0), stop=(k == NB0 - 1),
                               sem=bld.mm if (k == NB0 - 1 and oh == 1) else None)
                    bld.wait("scalar", bld.mm)
                    bld.wait("scalar", bld.vv)
                    BO("scalar", "activation", g3[:, 0:512], ps_d3[0][:], ACTF.Silu)
                    BO("scalar", "activation", g3[:, 512:1024], ps_d3[1][:], ACTF.Silu,
                       sem=bld.ev)
                    bld.wait("sync", bld.ev)
                    if "g0_dbg" in dbg:
                        bld.dma("sync", S["g0_dbg"].ap()[it * 128:(it + 1) * 128, :],
                                g3[:])
                    bld.dma("sync", xx3[:], D["x_in"][it * 128:(it + 1) * 128, :])
                    bld.dma("sync", ss3[:], S["s0"].ap()[it * 128:(it + 1) * 128, :])
                    bld.wait("vector", bld.dm)
                    bld.wait("vector", bld.ev)
                    BO("vector", "tensor_tensor", g3[:], g3[:], ss3[:], ALU.mult)
                    BO("vector", "tensor_tensor", xx3[:], xx3[:], g3[:], ALU.add,
                       sem=bld.vv)
                    bld.tick("vector", bld.ev)
                    bld.wait("sync", bld.vv)
                    bld.dma("sync", S["x1"].ap()[it * 128:(it + 1) * 128, :], xx3[:])
                    bld.wait("scalar", bld.vv)
                    emit_ln_tile(xx3[:], xn1t[:], xn1b[:], st3b, dmp3[:], C,
                                 absmax_slot=2)
                    bld.wait("sync", bld.vv)
                    bld.dma("sync", S["xn1"].ap()[it * 128:(it + 1) * 128, :],
                            xn1t[:])
                    bld.wait("tensor", bld.vv)
                    bld.wait("tensor", bld.ev)
                    for cb in range(8):
                        BO("tensor", "transpose", ps_t3b[:, cb * 128:(cb + 1) * 128],
                           xn1b[:, cb * 128:(cb + 1) * 128], ident_b[:],
                           sem=bld.mm if cb == 7 else None)
                    bld.wait("scalar", bld.mm)
                    BO("scalar", "copy", trb3b[:], ps_t3b[:], sem=bld.ev)
                    bld.wait("sync", bld.ev)
                    for cb in range(8):
                        bld.dma("sync", S["xn1T_bf"].ap()[cb * 128:(cb + 1) * 128,
                                                          it * 128:(it + 1) * 128],
                                trb3b[:, cb * 128:(cb + 1) * 128])
            global_max(2, 2)
            derive_scale(2, 32767.0, None, 4, 5, floor=1e-12)
            if "qs2_dbg" in dbg:
                bld.wait("sync", bld.vv)
                bld.wait("sync", bld.ev)
                bld.dma("sync", S["qs2_dbg"].ap()[:, 0:16], qs[:, 0:16])
                bld.dma("sync", S["qs2_dbg"].ap()[:, 16:24], gmax[:, 0:8])
                bld.dma("sync", S["qs2_dbg"].ap()[:, 24:32], lmax[:, 0:8])
            bld.barrier()
            ph3b.close()

        attn_sb = sb("attn_sb", [64, H * DH], BF16)
        if phase_limit >= 7:
            ph4a = contextlib.ExitStack()
            wk4 = [sb(f"f4wk{i}", [PP, C], BF16, ph4a) for i in range(8)]
            wv4 = [sb(f"f4wv{i}", [PP, C], BF16, ph4a) for i in range(8)]
            wg4 = [sb(f"f4wg{i}", [PP, C], BF16, ph4a) for i in range(8)]
            xT4 = sb("f4x", [PP, 8 * 128], BF16, ph4a)
            kst = sb("f4k", [PP, C], F32, ph4a)
            krp = sb("f4kr", [PP, C], BF16, ph4a)
            rt1 = sb("f4r1", [PP, 512], F32, ph4a)
            rt2 = sb("f4r2", [PP, 512], F32, ph4a)
            cosT = sb("f4c", [PP, 32], F32, ph4a)
            sinT = sb("f4s", [PP, 32], F32, ph4a)
            kTh = sb("f4kt", [64, H * 128], BF16, ph4a)
            vb4 = sb("f4v", [PP, C], BF16, ph4a)
            gst = sb("f4g", [PP, C], F32, ph4a)
            gsb = sb("f4gb", [PP, C], BF16, ph4a)
            gmx = sb("f4gm", [PP, 20], F32, ph4a)
            trg = sb("f4tg", [PP, C], BF16, ph4a)
            sc4 = sb("f4sc", [64, CS], F32, ph4a)
            e4 = sb("f4e", [64, CS], F32, ph4a)
            pm4 = sb("f4pm", [64, 4], F32, ph4a)
            pb4 = sb("f4pb", [64, CS], BF16, ph4a)
            pT4 = sb("f4pt", [PP, 64], BF16, ph4a)
            attn_acc = sb("f4aa", [64, H * DH], F32, ph4a)
            ps_kv = psb("f4p", [PP, 512], F32, ph4a)
            ps_sc = psb("f4ps", [64, CS], F32, ph4a)
            ps_at = psb("f4pa", [64, DH], F32, ph4a)
            ps_kT = psb("f4pk", [64, H * 128], BF16, ph4a)
            ps_tg = psb("f4ptg", [PP, C], BF16, ph4a)
            ps_pt = ps_tg[:, 0:64]
            BO("vector", "memset", attn_acc[:, :], 0.0, sem=bld.vv)
            for cb in range(8):
                bld.dma("sync", wk4[cb][:], S["wkT"].ap()[cb * 128:(cb + 1) * 128, :])
                bld.dma("sync", wv4[cb][:], S["wvT"].ap()[cb * 128:(cb + 1) * 128, :])
                bld.dma("sync", wg4[cb][:], S["wgT"].ap()[cb * 128:(cb + 1) * 128, :])
            for it in range(T // 128):
                bld.wait("sync", bld.mm)
                bld.wait("sync", bld.vv)
                bld.wait("sync", bld.ev)
                for cb in range(8):
                    bld.dma("sync", xT4[:, cb * 128:(cb + 1) * 128],
                            S["xn1T_bf"].ap()[cb * 128:(cb + 1) * 128,
                                              it * 128:(it + 1) * 128])
                bld.dma("sync", cosT[:, :], D["cos_t"][it * 128:(it + 1) * 128, :])
                bld.dma("sync", sinT[:, :], D["sin_t"][it * 128:(it + 1) * 128, :])
                for (dst, wsrc) in [(kst, wk4), (vb4, wv4), (gst, wg4)]:
                    for oh in range(2):
                        bld.wait("tensor", bld.dm)
                        bld.wait("tensor", bld.ev)
                        for cb in range(8):
                            BO("tensor", "matmul", ps_kv[:],
                               xT4[:, cb * 128:(cb + 1) * 128],
                               wsrc[cb][:, oh * 512:(oh + 1) * 512],
                               start=(cb == 0), stop=(cb == 7),
                               sem=bld.mm if cb == 7 else None)
                        bld.wait("scalar", bld.mm)
                        BO("scalar", "copy", dst[:, oh * 512:(oh + 1) * 512],
                           ps_kv[:], sem=bld.ev)
                bld.wait("vector", bld.ev)
                av2 = kst[:].rearrange("p (h i two) -> p h i two", h=H, two=2)
                a4 = av2[:, :, :, 0]
                b4 = av2[:, :, :, 1]
                cexp = cosT[:].unsqueeze(1).broadcast_to([PP, H, 32])
                sexp = sinT[:].unsqueeze(1).broadcast_to([PP, H, 32])
                r13 = rt1[:].rearrange("p (h i) -> p h i", h=H)
                r23 = rt2[:].rearrange("p (h i) -> p h i", h=H)
                kv4 = krp[:].rearrange("p (h i two) -> p h i two", h=H, two=2)
                BO("vector", "tensor_tensor", r13, a4, cexp, ALU.mult)
                BO("vector", "tensor_tensor", r23, b4, sexp, ALU.mult)
                BO("vector", "tensor_tensor", kv4[:, :, :, 0], r13, r23, ALU.subtract)
                BO("vector", "tensor_tensor", r13, a4, sexp, ALU.mult)
                BO("vector", "tensor_tensor", r23, b4, cexp, ALU.mult)
                BO("vector", "tensor_tensor", kv4[:, :, :, 1], r13, r23, ALU.add,
                   sem=bld.vv)
                bld.wait("tensor", bld.vv)
                bld.wait("tensor", bld.ev)
                for h in range(H):
                    BO("tensor", "transpose", ps_kT[:, h * 128:(h + 1) * 128],
                       krp[:, h * 64:(h + 1) * 64], ident_b[:],
                       sem=bld.mm if h == H - 1 else None)
                bld.wait("scalar", bld.mm)
                BO("scalar", "copy", kTh[:, :], ps_kT[:, :], sem=bld.ev)
                bld.wait("vector", bld.ev)
                g3v = gst[:].rearrange("p (h l) -> p h l", h=H)
                BO("vector", "tensor_reduce", gmx[:, 0:16], g3v, AX.X, ALU.max,
                   sem=bld.vv)
                bld.wait("vector", bld.vv)
                mexp4 = gmx[:, 0:16].unsqueeze(2).broadcast_to([PP, H, L])
                BO("vector", "tensor_tensor", g3v, g3v, mexp4, ALU.subtract,
                   sem=bld.vv)
                bld.wait("scalar", bld.vv)
                BO("scalar", "activation", gst[:], gst[:], ACTF.Exp, scale=0.125,
                   sem=bld.ev)
                bld.wait("vector", bld.ev)
                BO("vector", "tensor_reduce", gmx[:, 0:16], g3v, AX.X, ALU.add,
                   sem=bld.vv)
                bld.wait("vector", bld.vv)
                BO("vector", "reciprocal", gmx[:, 0:16], gmx[:, 0:16], sem=bld.vv)
                bld.wait("vector", bld.vv)
                BO("vector", "tensor_tensor",
                   gsb[:].rearrange("p (h l) -> p h l", h=H), g3v,
                   gmx[:, 0:16].unsqueeze(2).broadcast_to([PP, H, L]), ALU.mult,
                   sem=bld.vv)
                bld.wait("tensor", bld.vv)
                for cb in range(8):
                    BO("tensor", "transpose", ps_tg[:, cb * 128:(cb + 1) * 128],
                       gsb[:, cb * 128:(cb + 1) * 128], ident_b[:],
                       sem=bld.mm if cb == 7 else None)
                bld.wait("scalar", bld.mm)
                BO("scalar", "copy", trg[:], ps_tg[:], sem=bld.ev)
                bld.wait("sync", bld.ev)
                for cb in range(8):
                    bld.dma("sync", S["gatesT_bf"].ap()[cb * 128:(cb + 1) * 128,
                                                        it * 128:(it + 1) * 128],
                            trg[:, cb * 128:(cb + 1) * 128])
                for h in range(H):
                    bld.wait("tensor", bld.vv)
                    BO("tensor", "matmul", ps_sc[:, :],
                       qh_b[:, h * L:(h + 1) * L],
                       kTh[:, h * 128:(h + 1) * 128],
                       start=True, stop=True, sem=bld.mm)
                    bld.wait("vector", bld.mm)
                    BO("vector", "tensor_tensor", sc4[:], ps_sc[:, :], cmask_t[:],
                       ALU.add)
                    BO("vector", "tensor_reduce", pm4[:, 0:1], sc4[:], AX.X, ALU.max,
                       sem=bld.vv)
                    bld.wait("vector", bld.vv)
                    BO("vector", "tensor_scalar", pm4[:, 1:2], pm4[:, 0:1], -0.125,
                       None, ALU.mult, sem=bld.vv)
                    bld.wait("scalar", bld.vv)
                    BO("scalar", "activation", e4[:], sc4[:], ACTF.Exp, scale=0.125,
                       bias=pm4[:, 1:2], sem=bld.ev)
                    bld.wait("vector", bld.ev)
                    BO("vector", "tensor_reduce", pm4[:, 2:3], e4[:], AX.X, ALU.add,
                       sem=bld.vv)
                    bld.wait("vector", bld.vv)
                    BO("vector", "reciprocal", pm4[:, 3:4], pm4[:, 2:3], sem=bld.vv)
                    bld.wait("vector", bld.vv)
                    BO("vector", "tensor_scalar", pb4[:], e4[:], pm4[:, 3:4], None,
                       ALU.mult, sem=bld.vv)
                    bld.wait("tensor", bld.vv)
                    BO("tensor", "transpose", ps_pt, pb4[:], ident_b[0:64, 0:64],
                       sem=bld.mm)
                    bld.wait("scalar", bld.mm)
                    BO("scalar", "copy", pT4[:, :], ps_pt, sem=bld.ev)
                    bld.wait("tensor", bld.ev)
                    bld.wait("tensor", bld.vv)
                    BO("tensor", "matmul", ps_at[:, :],
                       pT4[:, :], vb4[:, h * DH:(h + 1) * DH],
                       start=True, stop=True, sem=bld.mm)
                    bld.wait("vector", bld.mm)
                    BO("vector", "tensor_tensor", attn_acc[:, h * DH:(h + 1) * DH],
                       attn_acc[:, h * DH:(h + 1) * DH], ps_at[:, :], ALU.add,
                       sem=bld.vv)
            bld.wait("scalar", bld.vv)
            BO("scalar", "copy", attn_sb[:, :], attn_acc[:, :], sem=bld.ev)
            bld.barrier()
            ph4a.close()

        if phase_limit >= 8:
            ph4b = contextlib.ExitStack()
            wp4 = [sb(f"g4wp{i}", [PP, C], BF16, ph4b) for i in range(8)]
            gT4 = sb("g4g", [64, H * 128], BF16, ph4b)
            wgt = sb("g4w", [PP, C], BF16, ph4b)
            wgtT = sb("g4wt", [PP, 8 * 128], BF16, ph4b)
            s1st = sb("g4s", [PP, C], F32, ph4b)
            ps_w4 = psb("g4pw", [PP, C], F32, ph4b)
            ps_s4 = [psb(f"g4ps{i}", [PP, 512], F32, ph4b) for i in range(2)]
            ps_t4 = psb("g4pt", [PP, C], BF16, ph4b)
            for cb in range(8):
                bld.dma("sync", wp4[cb][:], S["wpT"].ap()[cb * 128:(cb + 1) * 128, :])
            for it in range(T // 128):
                bld.wait("sync", bld.mm)
                bld.wait("sync", bld.ev)
                for h in range(H):
                    bld.dma("sync", gT4[:, h * 128:(h + 1) * 128],
                            S["gatesT_bf"].ap()[h * 64:(h + 1) * 64,
                                                it * 128:(it + 1) * 128])
                bld.wait("tensor", bld.dm)
                bld.wait("tensor", bld.ev)
                for h in range(H):
                    BO("tensor", "matmul", ps_w4[:, h * DH:(h + 1) * DH],
                       gT4[:, h * 128:(h + 1) * 128],
                       attn_sb[:, h * DH:(h + 1) * DH],
                       start=True, stop=True,
                       sem=bld.mm if h == H - 1 else None)
                bld.wait("scalar", bld.mm)
                BO("scalar", "copy", wgt[:], ps_w4[:], sem=bld.ev)
                bld.wait("tensor", bld.ev)
                for cb in range(8):
                    BO("tensor", "transpose", ps_t4[:, cb * 128:(cb + 1) * 128],
                       wgt[:, cb * 128:(cb + 1) * 128], ident_b[:],
                       sem=bld.mm if cb == 7 else None)
                bld.wait("scalar", bld.mm)
                BO("scalar", "copy", wgtT[:], ps_t4[:], sem=bld.ev)
                bld.wait("tensor", bld.ev)
                for oh in range(2):
                    for cb in range(8):
                        BO("tensor", "matmul", ps_s4[oh][:],
                           wgtT[:, cb * 128:(cb + 1) * 128],
                           wp4[cb][:, oh * 512:(oh + 1) * 512],
                           start=(cb == 0), stop=(cb == 7),
                           sem=bld.mm if cb == 7 else None)
                bld.wait("scalar", bld.mm)
                BO("scalar", "copy", s1st[:, 0:512], ps_s4[0][:])
                BO("scalar", "copy", s1st[:, 512:1024], ps_s4[1][:], sem=bld.ev)
                bld.wait("sync", bld.ev)
                bld.dma("sync", S["s1"].ap()[it * 128:(it + 1) * 128, :], s1st[:])
            bld.barrier()
            ph4b.close()

        if phase_limit >= 9:
            ph5a = contextlib.ExitStack()
            xqT5 = [sb(f"h5xq{i}", [PP, T], BF16, ph5a) for i in range(8)]
            x5t = sb("h5x", [PP, C], F32, ph5a)
            q5b = sb("h5q", [PP, C], BF16, ph5a)
            q5i = sb("h5i", [PP, C], I32, ph5a)
            trb5 = sb("h5t", [PP, C], BF16, ph5a)
            wg5 = [sb(f"h5wg{i}", [PP, 1024], BF16, ph5a) for i in range(2)]
            wu5 = [sb(f"h5wu{i}", [PP, 1024], BF16, ph5a) for i in range(2)]
            ub5 = sb("h5ub", [PP, 512], BF16, ph5a)
            gb5 = sb("h5gb", [PP, 512], BF16, ph5a)
            st5 = sb("h5st", [PP, 8], F32, ph5a)
            ps_g5 = psb("h5pg", [PP, 512], F32, ph5a)
            ps_u5 = psb("h5pu", [PP, 512], F32, ph5a)
            ps_t5 = psb("h5pt", [PP, C], BF16, ph5a)
            s_w5 = [bld.slot("h5w0"), bld.slot("h5w1")]
            s_st5 = bld.slot("h5st")
            for it in range(T // 128):
                bld.wait("sync", bld.vv)
                bld.wait("sync", bld.ev)
                bld.dma("sync", x5t[:], S["xn1"].ap()[it * 128:(it + 1) * 128, :])
                bld.wait("vector", bld.dm)
                BO("vector", "tensor_scalar", x5t[:], x5t[:], qs[:, 4:5], 32767.0,
                   ALU.mult, ALU.min)
                BO("vector", "tensor_scalar", x5t[:], x5t[:], -32768.0, MAGIC,
                   ALU.max, ALU.add)
                BO("vector", "tensor_scalar", q5b[:], x5t[:], MAGIC, qs[:, 5:6],
                   ALU.subtract, ALU.mult, sem=bld.vv)
                bld.wait("tensor", bld.vv)
                bld.wait("tensor", bld.ev)
                for cb in range(8):
                    BO("tensor", "transpose", ps_t5[:, cb * 128:(cb + 1) * 128],
                       q5b[:, cb * 128:(cb + 1) * 128], ident_b[:],
                       sem=bld.mm if cb == 7 else None)
                bld.wait("scalar", bld.mm)
                BO("scalar", "copy", trb5[:], ps_t5[:], sem=bld.ev)
                bld.wait("vector", bld.ev)
                for cb in range(8):
                    BO("vector", "tensor_copy", xqT5[cb][:, it * 128:(it + 1) * 128],
                       trb5[:, cb * 128:(cb + 1) * 128],
                       sem=bld.vv if cb == 7 else None)
            for ib in range(NB1):
                wsl = ib % 2
                bld.wait("sync", s_w5[wsl])
                bld.dma("sync", wg5[wsl][:],
                        S["wq_g1T"].ap()[ib].rearrange("(q p) m -> p q m", p=128))
                bld.dma("sync", wu5[wsl][:],
                        S["wq_u1T"].ap()[ib].rearrange("(q p) m -> p q m", p=128))
                for tt in range(T // 512):
                    bld.wait("tensor", bld.dm)
                    bld.wait("tensor", bld.vv)
                    bld.wait("tensor", bld.ev)
                    for cb in range(8):
                        BO("tensor", "matmul", ps_g5[:],
                           wg5[wsl][:, cb * 128:(cb + 1) * 128],
                           xqT5[cb][:, tt * 512:(tt + 1) * 512],
                           start=(cb == 0), stop=(cb == 7))
                    for cb in range(8):
                        BO("tensor", "matmul", ps_u5[:],
                           wu5[wsl][:, cb * 128:(cb + 1) * 128],
                           xqT5[cb][:, tt * 512:(tt + 1) * 512],
                           start=(cb == 0), stop=(cb == 7),
                           sem=bld.mm if cb == 7 else None)
                    if tt == T // 512 - 1:
                        bld.tick("tensor", s_w5[wsl])
                    bld.wait("vector", bld.mm)
                    bld.wait("vector", s_st5)
                    BO("vector", "tensor_reduce", st5[:, 0:1], ps_g5[:], AX.X,
                       ALU.max, apply_absolute_value=True)
                    BO("vector", "tensor_reduce", st5[:, 1:2], ps_u5[:], AX.X,
                       ALU.max, apply_absolute_value=True, sem=bld.vv)
                    bld.wait("vector", bld.vv)
                    BO("vector", "tensor_tensor", lmax[:, 4:5], lmax[:, 4:5],
                       st5[:, 0:1], ALU.max)
                    BO("vector", "tensor_tensor", lmax[:, 3:4], lmax[:, 3:4],
                       st5[:, 1:2], ALU.max, sem=bld.vv)
                    bld.wait("scalar", bld.vv)
                    BO("scalar", "copy", gb5[:], ps_g5[:])
                    BO("scalar", "copy", ub5[:], ps_u5[:], sem=bld.ev)
                    bld.tick("scalar", bld.ev)
                    bld.wait("sync", bld.ev)
                    bld.dma("sync", S["gater_bf"].ap()[ib * 128:(ib + 1) * 128,
                                                       tt * 512:(tt + 1) * 512],
                            gb5[:], sem=s_st5)
                    bld.dma("sync", S["upr_bf"].ap()[ib * 128:(ib + 1) * 128,
                                                     tt * 512:(tt + 1) * 512],
                            ub5[:], sem=s_st5)
            global_max(3, 3)
            derive_scale(3, 32767.0, None, 6, 7, floor=1e-12)
            global_max(4, 4)
            derive_scale(4, 32767.0, None, 8, 9, floor=1e-12)
            bld.barrier()
            ph5a.close()

        if phase_limit >= 10:
            ph5b = contextlib.ExitStack()
            gr5 = sb("i5g", [PP, 512], BF16, ph5b)
            gq5 = sb("i5gq", [PP, 512], F32, ph5b)
            sg5 = sb("i5s", [PP, 512], F32, ph5b)
            t25 = sb("i5t", [PP, 512], F32, ph5b)
            t2b5 = sb("i5tb", [PP, 512], BF16, ph5b)
            i5 = sb("i5i", [PP, 512], I32, ph5b)
            st5b = sb("i5st", [PP, 4], F32, ph5b)
            s_st5b = bld.slot("i5st")
            for tt in range(T // 512):
                for ib in range(NB1):
                    bld.wait("sync", bld.vv)
                    bld.dma("sync", gr5[:], S["gater_bf"].ap()[
                        ib * 128:(ib + 1) * 128, tt * 512:(tt + 1) * 512])
                    bld.wait("vector", bld.dm)
                    BO("vector", "tensor_scalar", gq5[:], gr5[:], qs[:, 8:9], 32767.0,
                       ALU.mult, ALU.min)
                    BO("vector", "tensor_scalar", gq5[:], gq5[:], -32768.0, MAGIC,
                       ALU.max, ALU.add)
                    BO("vector", "tensor_scalar", gq5[:], gq5[:], MAGIC,
                       qs[:, 9:10], ALU.subtract, ALU.mult, sem=bld.vv)
                    bld.wait("scalar", bld.vv)
                    BO("scalar", "activation", sg5[:], gq5[:], ACTF.Sigmoid,
                       sem=bld.ev)
                    bld.wait("vector", bld.ev)
                    BO("vector", "tensor_scalar", sg5[:], sg5[:], 65536.0, 65535.0,
                       ALU.mult, ALU.min)
                    BO("vector", "tensor_scalar", sg5[:], sg5[:], 0.0, MAGIC,
                       ALU.max, ALU.add)
                    BO("vector", "tensor_scalar", sg5[:], sg5[:], MAGIC,
                       1.0 / 65536.0, ALU.subtract, ALU.mult)
                    BO("vector", "tensor_tensor", t25[:], gq5[:], sg5[:], ALU.mult)
                    BO("vector", "tensor_reduce", st5b[:, 0:1], t25[:], AX.X,
                       ALU.max, apply_absolute_value=True, sem=bld.vv)
                    bld.wait("vector", bld.vv)
                    BO("vector", "tensor_tensor", lmax[:, 5:6], lmax[:, 5:6],
                       st5b[:, 0:1], ALU.max)
                    bld.wait("vector", s_st5b)
                    BO("vector", "tensor_copy", t2b5[:], t25[:], sem=bld.vv)
                    bld.wait("sync", bld.vv)
                    bld.dma("sync", S["t2_bf"].ap()[ib * 128:(ib + 1) * 128,
                                                    tt * 512:(tt + 1) * 512],
                            t2b5[:], sem=s_st5b)
            global_max(5, 5)
            derive_scale(5, 32767.0, None, 10, 11, floor=1e-12)
            bld.barrier()
            ph5b.close()

        if phase_limit >= 11:
            ph5c = contextlib.ExitStack()
            a5 = sb("j5a", [PP, 512], BF16, ph5c)
            b5 = sb("j5b", [PP, 512], BF16, ph5c)
            aq5 = sb("j5aq", [PP, 512], F32, ph5c)
            bq5 = sb("j5bq", [PP, 512], F32, ph5c)
            t35 = sb("j5t", [PP, 512], F32, ph5c)
            t3b5 = sb("j5tb", [PP, 512], BF16, ph5c)
            i5c = sb("j5i", [PP, 512], I32, ph5c)
            st5c = sb("j5st", [PP, 4], F32, ph5c)
            s_st5c = bld.slot("j5st")
            for tt in range(T // 512):
                for ib in range(NB1):
                    bld.wait("sync", bld.vv)
                    bld.dma("sync", a5[:], S["t2_bf"].ap()[
                        ib * 128:(ib + 1) * 128, tt * 512:(tt + 1) * 512])
                    bld.dma("sync", b5[:], S["upr_bf"].ap()[
                        ib * 128:(ib + 1) * 128, tt * 512:(tt + 1) * 512])
                    bld.wait("vector", bld.dm)
                    BO("vector", "tensor_scalar", aq5[:], a5[:], qs[:, 10:11],
                       32767.0, ALU.mult, ALU.min)
                    BO("vector", "tensor_scalar", aq5[:], aq5[:], -32768.0, MAGIC,
                       ALU.max, ALU.add)
                    BO("vector", "tensor_scalar", aq5[:], aq5[:], MAGIC,
                       qs[:, 11:12], ALU.subtract, ALU.mult)
                    BO("vector", "tensor_scalar", bq5[:], b5[:], qs[:, 6:7],
                       32767.0, ALU.mult, ALU.min)
                    BO("vector", "tensor_scalar", bq5[:], bq5[:], -32768.0, MAGIC,
                       ALU.max, ALU.add)
                    BO("vector", "tensor_scalar", bq5[:], bq5[:], MAGIC,
                       qs[:, 7:8], ALU.subtract, ALU.mult)
                    BO("vector", "tensor_tensor", t35[:], aq5[:], bq5[:], ALU.mult)
                    BO("vector", "tensor_reduce", st5c[:, 0:1], t35[:], AX.X,
                       ALU.max, apply_absolute_value=True, sem=bld.vv)
                    bld.wait("vector", bld.vv)
                    BO("vector", "tensor_tensor", lmax[:, 6:7], lmax[:, 6:7],
                       st5c[:, 0:1], ALU.max)
                    bld.wait("vector", s_st5c)
                    BO("vector", "tensor_copy", t3b5[:], t35[:], sem=bld.vv)
                    bld.wait("sync", bld.vv)
                    bld.dma("sync", S["t3_bf"].ap()[ib * 128:(ib + 1) * 128,
                                                    tt * 512:(tt + 1) * 512],
                            t3b5[:], sem=s_st5c)
            global_max(6, 6)
            derive_scale(6, 32767.0, None, 12, 13, floor=1e-12)
            bld.barrier()
            ph5c.close()

        if phase_limit >= 12:
            ph5d = contextlib.ExitStack()
            wd5 = [sb(f"k5wd{i}", [PP, 1024], BF16, ph5d) for i in range(NB1)]
            t3l = sb("k5t", [PP, NB1 * 512], BF16, ph5d)
            oq5 = sb("k5o", [PP, NB1 * 512], BF16, ph5d)
            of5 = sb("k5f", [PP, 512], F32, ph5d)
            oi5 = sb("k5i", [PP, 512], I32, ph5d)
            g5d = sb("k5g", [PP, C], F32, ph5d)
            x5d = sb("k5x", [PP, C], F32, ph5d)
            s5d = sb("k5s", [PP, C], F32, ph5d)
            ps_d5 = [psb(f"k5p{i}", [PP, 512], F32, ph5d) for i in range(2)]
            s_out = bld.slot("k5out")
            for k in range(NB1):
                bld.dma("sync", wd5[k][:],
                        S["wq_d1T"].ap()[k * 128:(k + 1) * 128, :])
            for tt in range(T // 512):
                bld.wait("sync", bld.vv)
                for k in range(NB1):
                    bld.dma("sync", t3l[:, k * 512:(k + 1) * 512],
                            S["t3_bf"].ap()[k * 128:(k + 1) * 128,
                                            tt * 512:(tt + 1) * 512])
                bld.wait("vector", bld.dm)
                for k in range(NB1):
                    BO("vector", "tensor_scalar", of5[:], t3l[:, k * 512:(k + 1) * 512],
                       qs[:, 12:13], 32767.0, ALU.mult, ALU.min)
                    BO("vector", "tensor_scalar", of5[:], of5[:], -32768.0, MAGIC,
                       ALU.max, ALU.add)
                    BO("vector", "tensor_scalar", oq5[:, k * 512:(k + 1) * 512],
                       of5[:], MAGIC, qs[:, 13:14], ALU.subtract, ALU.mult,
                       sem=bld.vv if k == NB1 - 1 else None)
                for t4 in range(4):
                    it = tt * 4 + t4
                    bld.wait("tensor", bld.vv)
                    bld.wait("tensor", bld.ev)
                    for oh in range(2):
                        for k in range(NB1):
                            BO("tensor", "matmul", ps_d5[oh][:],
                               oq5[:, k * 512 + t4 * 128:k * 512 + (t4 + 1) * 128],
                               wd5[k][:, oh * 512:(oh + 1) * 512],
                               start=(k == 0), stop=(k == NB1 - 1),
                               sem=bld.mm if (k == NB1 - 1 and oh == 1) else None)
                    bld.wait("scalar", bld.mm)
                    BO("scalar", "activation", g5d[:, 0:512], ps_d5[0][:], ACTF.Silu)
                    BO("scalar", "activation", g5d[:, 512:1024], ps_d5[1][:],
                       ACTF.Silu, sem=bld.ev)
                    bld.wait("sync", bld.ev)
                    bld.wait("sync", s_out)
                    bld.dma("sync", x5d[:], S["x1"].ap()[it * 128:(it + 1) * 128, :])
                    bld.dma("sync", s5d[:], S["s1"].ap()[it * 128:(it + 1) * 128, :])
                    bld.wait("vector", bld.dm)
                    bld.wait("vector", bld.ev)
                    BO("vector", "tensor_tensor", g5d[:], g5d[:], s5d[:], ALU.mult)
                    BO("vector", "tensor_tensor", x5d[:], x5d[:], g5d[:], ALU.add,
                       sem=bld.vv)
                    bld.tick("vector", bld.ev)
                    bld.wait("sync", bld.vv)
                    bld.dma("sync", out_t[it * 128:(it + 1) * 128, :], x5d[:],
                            sem=s_out)
            bld.wait("sync", s_out)
            bld.barrier()
            ph5d.close()



        bld.flush()

        if dbg:
            for nm, dv in dbg.items():
                bld.dma("sync", dv, S[nm].ap(), sem=bld.dmb)
            bld.wait("sync", bld.dmb)
            bld.flush()

        if phase_limit < 12:
            bld.dma("sync", out_t, S["xn"].ap(), sem=bld.dmb)
            bld.wait("sync", bld.dmb)
            bld.flush()

    return nc


def _host_layout(inputs):
    g = lambda k: np.ascontiguousarray(np.asarray(inputs[k], dtype=np.float32))
    W_in = g("L0_attn_W_in")
    com = {}
    com["w_inT_cb"] = np.ascontiguousarray(
        W_in.reshape(32, 128, C).transpose(0, 2, 1))
    def perpart(v, f=1):
        return np.ascontiguousarray(
            v.reshape(NK, 128, f).transpose(1, 0, 2).reshape(128, NK * f))
    com["conv_w"] = perpart(g("L0_attn_conv_w").reshape(DI, 4), 4)
    com["conv_b"] = perpart(g("L0_attn_conv_b"))
    com["w_xT"] = perpart(np.ascontiguousarray(g("L0_attn_W_x").T), 33)
    com["w_dt"] = perpart(g("L0_attn_W_dt"))
    com["b_dt"] = perpart(g("L0_attn_b_dt"))
    com["a_log"] = g("L0_attn_A_log").reshape(1, DS)
    com["d_param"] = perpart(g("L0_attn_D_param"))
    com["ln_g"] = perpart(g("L0_attn_ln_g"))
    com["ln_b"] = perpart(g("L0_attn_ln_b"))
    com["w_outT"] = np.ascontiguousarray(g("L0_attn_W_out").T)
    com["clip_x"] = g("L0_ffn_clip_x").reshape(1, 1)
    com["clip_down"] = g("L0_ffn_clip_down").reshape(1, 1)
    com["w_gate0"] = g("L0_ffn_w_gate")
    com["w_up0"] = g("L0_ffn_w_up")
    com["w_down0"] = g("L0_ffn_w_down")
    com["q_hT"] = np.ascontiguousarray(
        g("L1_attn_latent_queries").reshape(L, H, DH).transpose(2, 1, 0).reshape(
            DH, H * L))
    com["w_kT"] = np.ascontiguousarray(g("L1_attn_W_k").T)
    com["w_vT"] = np.ascontiguousarray(g("L1_attn_W_v").T)
    com["w_gT"] = np.ascontiguousarray(g("L1_attn_W_g").T)
    com["w_pT"] = np.ascontiguousarray(g("L1_attn_W_p").T)
    com["w_gate1"] = g("L1_ffn_w_gate")
    com["w_up1"] = g("L1_ffn_w_up")
    com["w_down1"] = g("L1_ffn_w_down")
    com["identf"] = np.eye(128, dtype=np.float32)
    inv = (1.0 / (10000.0 ** (np.arange(0, DH, 2, dtype=np.float32) / DH)))
    ang = np.arange(T, dtype=np.float32)[:, None] * inv[None, :]
    com["cos_t"] = np.cos(ang).astype(np.float32)
    com["sin_t"] = np.sin(ang).astype(np.float32)
    causal = (np.arange(CS)[None, :] <= np.arange(L)[:, None])
    com["cmask"] = np.where(causal, 0.0, MASKVAL).astype(np.float32)
    return com


def kernel(**inputs):
    com = _host_layout(inputs)
    x = np.ascontiguousarray(np.asarray(inputs["L0_attn_x"], dtype=np.float32))
    nc = build_program()
    in_maps = []
    for c in range(NCORES):
        m = dict(com)
        m["x_in"] = np.ascontiguousarray(x[c])
        in_maps.append(m)
    res = run_bass_kernel_spmd(nc, in_maps, core_ids=list(range(NCORES)))
    return np.stack([np.asarray(r["out"], dtype=np.float32) for r in res.results])


if __name__ == "__main__":
    build_program(phase_limit=1)
    print("built")


# version 05
# speedup vs baseline: 1.0769x; 1.0769x over previous
import contextlib
import numpy as np

import concourse.bass as bass
import concourse.mybir as mybir
from concourse.bass_utils import run_bass_kernel_spmd

F32 = mybir.dt.float32
BF16 = mybir.dt.bfloat16
I32 = mybir.dt.int32
ALU = mybir.AluOpType
ACTF = mybir.ActivationFunctionType
AX = mybir.AxisListType

NCORES = 8
B, T, C = 8, 4096, 1024
DI, DS = 2048, 16
I0, I1 = 2816, 3072
H, L, DH, CS = 16, 64, 64, 128
NB0, NB1 = I0 // 128, I1 // 128
LN_EPS = 1e-5
EXP_NEG20 = float(np.float32(np.exp(np.float32(-20.0))))
NK = DI // 128
TCH = 256
TS = 32
NSUB = TCH // TS
NCH2 = T // TCH
PP = 128
ENGINES = ["sync", "vector", "scalar", "tensor", "gpsimd"]
MASKVAL = -1e30
MAGIC = 12582912.0
SKIP_STAGES = set()


class Sem:
    def __init__(self, nc, name):
        self.h = nc.semaphore(name).__enter__()
        self.n = 0


class Builder:
    def __init__(self, nc):
        self.nc = nc
        self.items = []
        self._slots = {}
        for nm in ["dm", "dmb", "mm", "ev", "vv", "gg", "cc", "bar"]:
            setattr(self, nm, Sem(nc, "s_" + nm))

    def slot(self, name):
        if name not in self._slots:
            self._slots[name] = Sem(self.nc, "sl_" + name)
        return self._slots[name]

    def tick(self, eng, sem):
        sem.n += 1

        def fn(e, sem=sem):
            e.nop().then_inc(sem.h, 1)
        self.items.append((eng, fn))

    def raw(self, eng, fn):
        self.items.append((eng, fn))

    def wait(self, eng, sem, at=None):
        v = sem.n if at is None else at
        if v > 0:
            self.items.append((eng, lambda e: e.wait_ge(sem.h, v)))

    def op(self, eng, meth, *args, sem=None, inc=1, **kw):
        s = sem
        if s is not None:
            s.n += inc

        def fn(e):
            i = getattr(e, meth)(*args, **kw)
            if s is not None:
                i.then_inc(s.h, inc)
        self.items.append((eng, fn))

    def dma(self, eng, out, in_, sem=None, **kw):
        s = sem if sem is not None else self.dm
        s.n += 16

        def fn(e):
            e.dma_start(out=out, in_=in_, **kw).then_inc(s.h, 16)
        self.items.append((eng, fn))

    def flush(self):
        nc = self.nc
        items, self.items = self.items, []
        if not items:
            return
        per = {e: [fn for (en, fn) in items if en == e] for e in ENGINES}
        with nc.Block() as blk:
            for e in ENGINES:
                fns = per[e]
                if not fns:
                    continue

                def mk(fns):
                    def emit(eng):
                        for fn in fns:
                            fn(eng)
                    return emit
                getattr(blk, e)(mk(fns))

    def barrier(self):
        self.bar.n += len(ENGINES)
        tgt = self.bar.n
        for e in ENGINES:
            def fn(eng, tgt=tgt):
                eng.nop().then_inc(self.bar.h, 1)
                eng.wait_ge(self.bar.h, tgt)
            self.items.append((e, fn))
        self.flush()


def build_program(phase_limit=99, debug_scr=()):
    nc = bass.Bass("TRN2", target_bir_lowering=False, detect_race_conditions=False)
    D, S = {}, {}

    def din(name, shape):
        D[name] = nc.dram_tensor(name, list(shape), F32, kind="ExternalInput").ap()

    din("x_in", (T, C))
    din("w_inT_cb", (2 * DI // 128, C, 128))
    din("conv_w", (128, NK * 4))
    din("conv_b", (128, NK))
    din("w_xT", (128, NK * 33))
    din("w_dt", (128, NK))
    din("b_dt", (128, NK))
    din("a_log", (1, DS))
    din("d_param", (128, NK))
    din("ln_g", (128, NK))
    din("ln_b", (128, NK))
    din("w_outT", (DI, C))
    din("clip_x", (1, 1))
    din("clip_down", (1, 1))
    din("w_gate0", (I0, C))
    din("w_up0", (I0, C))
    din("w_down0", (C, I0))
    din("q_hT", (64, H * L))
    din("w_kT", (C, C))
    din("w_vT", (C, C))
    din("w_gT", (C, C))
    din("w_pT", (C, C))
    din("w_gate1", (I1, C))
    din("w_up1", (I1, C))
    din("w_down1", (C, I1))
    din("identf", (128, 128))
    din("cos_t", (T, DH // 2))
    din("sin_t", (T, DH // 2))
    din("cmask", (L, CS))

    out_t = nc.dram_tensor("out", [T, C], F32, kind="ExternalOutput").ap()
    dbg = {}
    for nm in debug_scr:
        pass

    def scr(name, shape, dtype=F32):
        S[name] = nc.dram_tensor("scr_" + name, list(shape), dtype)

    scr("xn", (T, C))
    scr("xnT_bf", (C, T), BF16)
    scr("xc", (DI, T))
    scr("z", (DI, T))
    scr("prod_bf", (DI, T), BF16)
    scr("s0", (T, C))
    scr("gu_bf", (I0, T), BF16)
    scr("x1", (T, C))
    scr("xn1", (T, C))
    scr("xn1T_bf", (C, T), BF16)
    scr("gatesT_bf", (C, T), BF16)
    scr("s1", (T, C))
    scr("upr_bf", (I1, T), BF16)
    scr("gater_bf", (I1, T), BF16)
    scr("t2_bf", (I1, T), BF16)
    scr("t3_bf", (I1, T), BF16)
    scr("y_dbg", (DI, T))
    scr("xq_dbg", (T, C))
    scr("qs_dbg", (128, 32))
    scr("cc_dbg", (4, 128))
    scr("lmax_ph0c", (128, 32))
    scr("g0_dbg", (T, C))
    scr("qs2_dbg", (128, 32))
    scr("w0_dbg", (128, 2048))
    scr("winT", (2 * DI // 128, C, 128), BF16)
    scr("woutT", (DI, C), BF16)
    scr("wkT", (C, C), BF16)
    scr("wvT", (C, C), BF16)
    scr("wgT", (C, C), BF16)
    scr("wpT", (C, C), BF16)
    scr("wq_g0T", (NB0, C, 128), BF16)
    scr("wq_u0T", (NB0, C, 128), BF16)
    scr("wq_d0T", (I0, C), BF16)
    scr("wq_g1T", (NB1, C, 128), BF16)
    scr("wq_u1T", (NB1, C, 128), BF16)
    scr("wq_d1T", (I1, C), BF16)
    NCC = 7
    scr("cc_in", (NCC, 128))
    scr("cc_out", (NCC, 128))

    for nm in debug_scr:
        shp = list(S[nm].shape) if nm in S else None
        dt_ = S[nm].dtype if nm in S else F32
        dbg[nm] = nc.dram_tensor("dbg_" + nm, shp, dt_, kind="ExternalOutput").ap()

    bld = Builder(nc)
    es = contextlib.ExitStack()
    with es:
        def sb(name, shape, dtype=F32, st=None):
            return (st or es).enter_context(nc.sbuf_tensor(name, list(shape), dtype))

        def psb(name, shape, dtype=F32, st=None):
            return (st or es).enter_context(nc.psum_tensor(name, list(shape), dtype))

        ones_f = sb("ones_f", [PP, 1])
        ident_b = sb("ident_b", [PP, 128], BF16)
        conv_w_t = sb("conv_w_t", [PP, NK * 4])
        conv_b_t = sb("conv_b_t", [PP, NK])
        w_dt_t = sb("w_dt_t", [PP, NK])
        b_dt_t = sb("b_dt_t", [PP, NK])
        d_par_t = sb("d_par_t", [PP, NK])
        ln_g_t = sb("ln_g_t", [PP, NK])
        ln_b_t = sb("ln_b_t", [PP, NK])
        a_cols = sb("a_cols", [PP, DS])
        w_xT_b = sb("w_xT_b", [PP, NK * 33], BF16)
        qh_b = sb("qh_b", [64, H * L], BF16)
        cmask_t = sb("cmask_t", [L, CS])
        gmax = sb("gmax", [PP, 8])
        lmax = sb("lmax", [PP, 8])
        qs = sb("qs", [PP, 16])
        row128 = sb("row128", [1, 136])
        ps_util = psb("ps_util", [PP, 512])

        def BO(e, meth, *a, **kw):
            bld.op(e, meth, *a, **kw)

        def global_max(slot, cc_id):
            cin, cout = S["cc_in"].ap(), S["cc_out"].ap()
            bld.wait("gpsimd", bld.vv)
            bld.dma("gpsimd", cin[cc_id, :].unsqueeze(0), lmax[:, slot:slot + 1],
                    sem=bld.gg, allow_slow_non_contiguous=True)
            bld.wait("gpsimd", bld.gg)
            bld.op("gpsimd", "collective_compute", "AllReduce", ALU.max,
                   replica_groups=[list(range(NCORES))],
                   ins=[cin[cc_id:cc_id + 1, :]],
                   outs=[cout[cc_id:cc_id + 1, :]], sem=bld.cc)
            bld.wait("gpsimd", bld.cc)
            bld.dma("gpsimd", row128[0:1, 0:128], cout[cc_id:cc_id + 1, :],
                    sem=bld.gg)
            bld.wait("vector", bld.gg)
            BO("vector", "tensor_reduce", row128[0:1, 128:129], row128[0:1, 0:128],
               AX.X, ALU.max, sem=bld.vv)
            bld.wait("tensor", bld.vv)
            BO("tensor", "matmul", ps_util[:, 0:1],
               ones_f[0:1, 0:1].broadcast_to([1, 128]), row128[0:1, 128:129],
               start=True, stop=True, sem=bld.mm)
            bld.wait("scalar", bld.mm)
            BO("scalar", "copy", gmax[:, slot:slot + 1], ps_util[:, 0:1], sem=bld.ev)

        def derive_scale(slot, qmaxv, sig_col, inv_col, s_col, floor=None):
            bld.wait("vector", bld.ev)
            if sig_col is not None:
                BO("vector", "tensor_scalar", qs[:, s_col:s_col + 1],
                   gmax[:, slot:slot + 1], qs[:, sig_col:sig_col + 1], 1.0 / qmaxv,
                   ALU.mult, ALU.mult, sem=bld.vv)
            else:
                BO("vector", "tensor_scalar", qs[:, s_col:s_col + 1],
                   gmax[:, slot:slot + 1], 1.0 / qmaxv, None, ALU.mult, sem=bld.vv)
            bld.wait("vector", bld.vv)
            if floor is not None:
                BO("vector", "tensor_scalar", qs[:, s_col:s_col + 1],
                   qs[:, s_col:s_col + 1], floor, None, ALU.max, sem=bld.vv)
                bld.wait("vector", bld.vv)
            BO("vector", "reciprocal", qs[:, inv_col:inv_col + 1],
               qs[:, s_col:s_col + 1], sem=bld.vv)

        def emit_quant(dst_bf, src, inv_col, s_col, qhi, qlo, itmp, ftmp):
            BO("vector", "tensor_scalar", ftmp, src, qs[:, inv_col:inv_col + 1],
               float(qhi), ALU.mult, ALU.min)
            BO("vector", "tensor_scalar", itmp, ftmp, float(qlo), None, ALU.max)
            BO("vector", "tensor_scalar", dst_bf, itmp, qs[:, s_col:s_col + 1],
               None, ALU.mult)

        def emit_ln_tile(xin, xout_f32, xout_bf, stat, dump, F, absmax_slot=None):
            bld.wait("scalar", bld.vv)
            BO("scalar", "activation", xout_f32, xin, ACTF.Copy,
               accum_out=stat[:, 0:1], sem=bld.ev)
            bld.wait("vector", bld.ev)
            BO("vector", "tensor_scalar", stat[:, 0:1], stat[:, 0:1], 1.0 / F, None,
               ALU.mult, sem=bld.vv)
            bld.wait("vector", bld.vv)
            BO("vector", "tensor_scalar", xout_f32, xout_f32, stat[:, 0:1], None,
               ALU.subtract, sem=bld.vv)
            bld.wait("scalar", bld.vv)
            BO("scalar", "activation", dump, xout_f32, ACTF.Square,
               accum_out=stat[:, 1:2], sem=bld.ev)
            bld.wait("vector", bld.ev)
            BO("vector", "tensor_scalar", stat[:, 1:2], stat[:, 1:2], 1.0 / F, LN_EPS,
               ALU.mult, ALU.add, sem=bld.vv)
            bld.wait("scalar", bld.vv)
            BO("scalar", "activation", stat[:, 2:3], stat[:, 1:2], ACTF.Sqrt, sem=bld.ev)
            bld.wait("vector", bld.ev)
            BO("vector", "reciprocal", stat[:, 3:4], stat[:, 2:3], sem=bld.vv)
            bld.wait("vector", bld.vv)
            BO("vector", "tensor_scalar", xout_f32, xout_f32, stat[:, 3:4], None,
               ALU.mult, sem=bld.vv)
            if xout_bf is not None:
                BO("vector", "tensor_copy", xout_bf, xout_f32, sem=bld.vv)
            if absmax_slot is not None:
                BO("vector", "tensor_reduce", stat[:, 5:6], xout_f32, AX.X, ALU.max,
                   apply_absolute_value=True, sem=bld.vv)
                bld.wait("vector", bld.vv)
                BO("vector", "tensor_tensor", lmax[:, absmax_slot:absmax_slot + 1],
                   lmax[:, absmax_slot:absmax_slot + 1], stat[:, 5:6], ALU.max,
                   sem=bld.vv)

        ph0 = contextlib.ExitStack()
        idf = sb("idf", [PP, 128], F32, ph0)
        arow = sb("arow", [1, DS], F32, ph0)
        crow = sb("crow", [1, 2], F32, ph0)
        wx_st = sb("wx_st", [PP, NK * 33], F32, ph0)
        qh_st = sb("qh_st", [64, H * L], F32, ph0)

        bld.dma("sync", conv_w_t[:], D["conv_w"])
        bld.dma("sync", conv_b_t[:], D["conv_b"])
        bld.dma("sync", w_dt_t[:], D["w_dt"])
        bld.dma("sync", b_dt_t[:], D["b_dt"])
        bld.dma("sync", d_par_t[:], D["d_param"])
        bld.dma("sync", ln_g_t[:], D["ln_g"])
        bld.dma("sync", ln_b_t[:], D["ln_b"])
        bld.dma("sync", wx_st[:], D["w_xT"])
        bld.dma("sync", qh_st[:], D["q_hT"])
        bld.dma("sync", cmask_t[:], D["cmask"])
        bld.dma("sync", idf[:], D["identf"])
        bld.dma("sync", arow[:], D["a_log"])
        bld.dma("sync", crow[0:1, 0:1], D["clip_x"])
        bld.dma("sync", crow[0:1, 1:2], D["clip_down"])
        bld.wait("vector", bld.dm)
        BO("vector", "memset", ones_f[:], 1.0)
        BO("vector", "memset", lmax[:], 0.0)
        BO("vector", "tensor_scalar", arow[:], arow[:], 5.0, None, ALU.min, sem=bld.vv)
        bld.wait("scalar", bld.vv)
        bld.wait("scalar", bld.dm)
        BO("scalar", "copy", ident_b[:], idf[:])
        BO("scalar", "copy", w_xT_b[:], wx_st[:])
        BO("scalar", "copy", qh_b[:], qh_st[:])
        BO("scalar", "activation", arow[:], arow[:], ACTF.Exp)
        BO("scalar", "activation", crow[:], crow[:], ACTF.Sigmoid, sem=bld.ev)
        bld.wait("vector", bld.ev)
        BO("vector", "tensor_scalar", arow[:], arow[:], -1.0, None, ALU.mult, sem=bld.vv)
        bld.wait("tensor", bld.vv)
        BO("tensor", "matmul", ps_util[:, 0:DS],
           ones_f[0:1, 0:1].broadcast_to([1, 128]), arow[0:1, :],
           start=True, stop=True)
        BO("tensor", "matmul", ps_util[:, 32:34],
           ones_f[0:1, 0:1].broadcast_to([1, 128]), crow[0:1, :],
           start=True, stop=True, sem=bld.mm)
        bld.wait("scalar", bld.mm)
        BO("scalar", "copy", a_cols[:], ps_util[:, 0:DS])
        BO("scalar", "copy", qs[:, 14:16], ps_util[:, 32:34], sem=bld.ev)
        bld.barrier()
        ph0.close()

        ph0b = contextlib.ExitStack()
        wst = [sb(f"wst{i}", [PP, 512], F32, ph0b) for i in range(2)]
        wbt = [sb(f"wbt{i}", [PP, 512], BF16, ph0b) for i in range(2)]
        s_wc = [bld.slot("wc0"), bld.slot("wc1")]
        step = 0

        def cast_stream(pairs):
            nonlocal step
            for (srcv, dstv) in pairs:
                sl = step % 2
                bld.wait("sync", s_wc[sl])
                bld.dma("sync", wst[sl][:, 0:512], srcv)
                bld.wait("scalar", bld.dm)
                BO("scalar", "copy", wbt[sl][:, 0:512], wst[sl][:, 0:512], sem=bld.ev)
                bld.wait("sync", bld.ev)
                bld.dma("sync", dstv, wbt[sl][:, 0:512], sem=s_wc[sl])
                step += 1

        pairs = []
        for eb in range(2 * DI // 128):
            for q in range(2):
                pairs.append((
                    D["w_inT_cb"][eb, q * 512:(q + 1) * 512, :].rearrange(
                        "(a p) m -> p a m", p=128),
                    S["winT"].ap()[eb, q * 512:(q + 1) * 512, :].rearrange(
                        "(a p) m -> p a m", p=128)))
        for (src2, dst2, K2, M2) in [("w_outT", "woutT", DI, C), ("w_kT", "wkT", C, C),
                                     ("w_vT", "wvT", C, C), ("w_gT", "wgT", C, C),
                                     ("w_pT", "wpT", C, C)]:
            for kt in range(K2 // 128):
                for mh in range(M2 // 512):
                    pairs.append((
                        D[src2][kt * 128:(kt + 1) * 128, mh * 512:(mh + 1) * 512],
                        S[dst2].ap()[kt * 128:(kt + 1) * 128, mh * 512:(mh + 1) * 512]))
        if phase_limit >= 2 and "ph0b" not in SKIP_STAGES:
            cast_stream(pairs)
        bld.barrier()
        ph0b.close()

        ph0c = contextlib.ExitStack()
        wq_st = sb("wq_st", [PP, I1], F32, ph0c)
        wq_i = sb("wq_i", [PP, I1], I32, ph0c)
        wq_b = sb("wq_b", [PP, I1], BF16, ph0c)
        wq_m = sb("wq_m", [PP, 96], F32, ph0c)
        wq_inv = sb("wq_inv", [PP, 96], F32, ph0c)
        tr_o = sb("tr_o", [PP, 512], BF16, ph0c)
        ps_tr = psb("ps_tr", [PP, 512], BF16, ph0c)
        s_q_store = bld.slot("q_store")

        def quant_weight(src2, dst2, rows, K2, lpbq, dst_cb):
            qmaxv = 7.0 if lpbq else 127.0
            qlov = -8.0 if lpbq else -128.0
            for rt in range(rows // 128):
                bld.wait("sync", bld.mm)
                bld.wait("sync", bld.vv)
                bld.dma("sync", wq_st[:, 0:K2], D[src2][rt * 128:(rt + 1) * 128, :])
                bld.wait("vector", bld.dm)
                if lpbq:
                    nb = K2 // 32
                    wv3 = wq_st[:, 0:K2].rearrange("p (b i) -> p b i", i=32)
                    if "qv_red" not in SKIP_STAGES:
                        BO("vector", "tensor_reduce", wq_m[:, 0:nb], wv3, AX.X,
                           ALU.max, apply_absolute_value=True, sem=bld.vv)
                        bld.wait("vector", bld.vv)
                        BO("vector", "tensor_scalar", wq_m[:, 0:nb], wq_m[:, 0:nb],
                           1.0 / qmaxv, 1e-12, ALU.mult, ALU.max, sem=bld.vv)
                        bld.wait("vector", bld.vv)
                        BO("vector", "reciprocal", wq_inv[:, 0:nb], wq_m[:, 0:nb],
                           sem=bld.vv)
                        bld.wait("vector", bld.vv)
                    iexp = wq_inv[:, 0:nb].unsqueeze(2).broadcast_to([PP, nb, 32])
                    mexp = wq_m[:, 0:nb].unsqueeze(2).broadcast_to([PP, nb, 32])
                    if "qv_tt1" not in SKIP_STAGES:
                        BO("vector", "tensor_tensor", wv3, wv3, iexp, ALU.mult)
                    if "qv_clip" not in SKIP_STAGES:
                        BO("vector", "tensor_scalar", wq_st[:, 0:K2], wq_st[:, 0:K2],
                           qmaxv, qlov, ALU.min, ALU.max)
                        BO("vector", "tensor_scalar", wq_st[:, 0:K2], wq_st[:, 0:K2],
                           MAGIC, MAGIC, ALU.add, ALU.subtract)
                    if "qv_tt2" not in SKIP_STAGES:
                        BO("vector", "tensor_tensor",
                           wq_b[:, 0:K2].rearrange("p (b i) -> p b i", i=32),
                           wv3, mexp, ALU.mult, sem=bld.vv)
                    else:
                        BO("vector", "tensor_copy", wq_b[:, 0:K2], wq_st[:, 0:K2],
                           sem=bld.vv)
                else:
                    BO("vector", "tensor_reduce", wq_m[:, 0:1], wq_st[:, 0:K2],
                       AX.X, ALU.max, apply_absolute_value=True, sem=bld.vv)
                    bld.wait("vector", bld.vv)
                    BO("vector", "tensor_scalar", wq_m[:, 0:1], wq_m[:, 0:1],
                       1.0 / qmaxv, None, ALU.mult, sem=bld.vv)
                    bld.wait("vector", bld.vv)
                    BO("vector", "reciprocal", wq_inv[:, 0:1], wq_m[:, 0:1],
                       sem=bld.vv)
                    bld.wait("vector", bld.vv)
                    BO("vector", "tensor_scalar", wq_st[:, 0:K2], wq_st[:, 0:K2],
                       wq_inv[:, 0:1], qmaxv, ALU.mult, ALU.min)
                    BO("vector", "tensor_scalar", wq_st[:, 0:K2], wq_st[:, 0:K2],
                       qlov, MAGIC, ALU.max, ALU.add)
                    BO("vector", "tensor_scalar", wq_st[:, 0:K2], wq_st[:, 0:K2],
                       MAGIC, None, ALU.subtract)
                    BO("vector", "tensor_scalar", wq_b[:, 0:K2], wq_st[:, 0:K2],
                       wq_m[:, 0:1], None, ALU.mult, sem=bld.vv)
                for kt in range(K2 // 128):
                    q = kt % 4
                    bld.wait("tensor", bld.vv)
                    bld.wait("tensor", bld.ev)
                    BO("tensor", "transpose", ps_tr[:, q * 128:(q + 1) * 128],
                       wq_b[:, kt * 128:(kt + 1) * 128], ident_b[:], sem=bld.mm)
                    bld.wait("scalar", bld.mm)
                    bld.wait("scalar", s_q_store)
                    BO("scalar", "copy", tr_o[:, q * 128:(q + 1) * 128],
                       ps_tr[:, q * 128:(q + 1) * 128], sem=bld.ev)
                    bld.wait("sync", bld.ev)
                    if dst_cb:
                        dview = S[dst2].ap()[rt, kt * 128:(kt + 1) * 128, :]
                    else:
                        dview = S[dst2].ap()[kt * 128:(kt + 1) * 128,
                                             rt * 128:(rt + 1) * 128]
                    bld.dma("sync", dview, tr_o[:, q * 128:(q + 1) * 128],
                            sem=s_q_store)

        if phase_limit >= 2 and "ph0c" not in SKIP_STAGES:
            qw_calls = [("w_gate0", "wq_g0T", I0, C, False, True),
                        ("w_up0", "wq_u0T", I0, C, False, True),
                        ("w_down0", "wq_d0T", C, I0, False, False),
                        ("w_gate1", "wq_g1T", I1, C, True, True),
                        ("w_up1", "wq_u1T", I1, C, True, True),
                        ("w_down1", "wq_d1T", C, I1, True, False)]
            for qi, qargs in enumerate(qw_calls):
                if f"qw{qi}" not in SKIP_STAGES:
                    quant_weight(*qargs)
        bld.barrier()
        if "lmax_ph0c" in dbg:
            sd = S["lmax_ph0c"].ap()
            bld.dma("sync", sd[:, 0:8], gmax[:, 0:8])
            bld.dma("sync", sd[:, 8:16], lmax[:, 0:8])
            bld.dma("sync", sd[:, 16:32], qs[:, 0:16], sem=bld.dm)
            bld.wait("sync", bld.dm)
        ph0c.close()

        if phase_limit >= 1:
            ph1 = contextlib.ExitStack()
            xt = sb("p1x", [PP, C], F32, ph1)
            xnt = sb("p1n", [PP, C], F32, ph1)
            xnb = sb("p1b", [PP, C], BF16, ph1)
            stat = sb("p1s", [PP, 8], F32, ph1)
            trb = sb("p1t", [PP, C], BF16, ph1)
            dmp = sb("p1d", [PP, C], F32, ph1)
            ps_t1 = psb("p1p", [PP, C], BF16, ph1)
            s_p1 = bld.slot("p1st")
            for it in range(T // 128):
                bld.wait("sync", s_p1)
                bld.wait("sync", bld.vv)
                bld.wait("sync", bld.ev)
                bld.dma("sync", xt[:], D["x_in"][it * 128:(it + 1) * 128, :])
                bld.wait("scalar", bld.dm)
                bld.wait("vector", bld.dm)
                emit_ln_tile(xt[:], xnt[:], xnb[:], stat, dmp[:], C, absmax_slot=0)
                bld.wait("sync", bld.vv)
                bld.dma("sync", S["xn"].ap()[it * 128:(it + 1) * 128, :], xnt[:],
                        sem=s_p1)
                bld.wait("tensor", bld.vv)
                bld.wait("tensor", bld.ev)
                for cb in range(8):
                    BO("tensor", "transpose", ps_t1[:, cb * 128:(cb + 1) * 128],
                       xnb[:, cb * 128:(cb + 1) * 128], ident_b[:],
                       sem=bld.mm if cb == 7 else None)
                bld.wait("scalar", bld.mm)
                BO("scalar", "copy", trb[:], ps_t1[:], sem=bld.ev)
                bld.wait("sync", bld.ev)
                for cb in range(8):
                    bld.dma("sync", S["xnT_bf"].ap()[cb * 128:(cb + 1) * 128,
                                                     it * 128:(it + 1) * 128],
                            trb[:, cb * 128:(cb + 1) * 128], sem=s_p1)
            global_max(0, 0)
            derive_scale(0, 127.0, 14, 0, 1)
            if "qs_dbg" in dbg:
                bld.wait("sync", bld.vv)
                bld.wait("sync", bld.ev)
                bld.dma("sync", S["qs_dbg"].ap()[:, 0:16], qs[:, 0:16])
                bld.dma("sync", S["qs_dbg"].ap()[:, 16:24], gmax[:, 0:8])
                bld.dma("sync", S["qs_dbg"].ap()[:, 24:32], lmax[:, 0:8])
            if "cc_dbg" in dbg:
                bld.wait("sync", bld.vv)
                bld.dma("sync", S["cc_dbg"].ap()[0:1, :], S["cc_in"].ap()[0:1, :])
                bld.dma("sync", S["cc_dbg"].ap()[1:2, :], S["cc_out"].ap()[0:1, :])
                bld.dma("sync", S["cc_dbg"].ap()[2:3, :], row128[0:1, 0:128])
            bld.barrier()
            ph1.close()


        if phase_limit >= 2 and "ph2a" not in SKIP_STAGES:
            ph2a = contextlib.ExitStack()
            wbuf = [sb(f"a2w{i}", [PP, 8 * 128], BF16, ph2a) for i in range(2)]
            xbuf = [sb(f"a2x{i}", [PP, 8 * 512], BF16, ph2a) for i in range(2)]
            est = [sb(f"a2e{i}", [PP, 512], F32, ph2a) for i in range(2)]
            ps2 = [psb(f"a2p{i}", [PP, 512], F32, ph2a) for i in range(2)]
            s_wb = [bld.slot("a2wb0"), bld.slot("a2wb1")]
            s_xb = [bld.slot("a2xb0"), bld.slot("a2xb1")]
            s_es = [bld.slot("a2es0"), bld.slot("a2es1")]
            s_pb = [bld.slot("a2pb0"), bld.slot("a2pb1")]
            for tt in range(T // 512):
                xsl = tt % 2
                bld.wait("sync", s_xb[xsl])
                bld.dma("sync", xbuf[xsl][:],
                        S["xnT_bf"].ap()[:, tt * 512:(tt + 1) * 512].rearrange(
                            "(q p) m -> p q m", p=128))
                for eb in range(2 * DI // 128):
                    wsl = eb % 2
                    psl = eb % 2
                    bld.wait("sync", s_wb[wsl])
                    bld.dma("sync", wbuf[wsl][:],
                            S["winT"].ap()[eb].rearrange("(q p) m -> p q m", p=128))
                    bld.wait("tensor", bld.dm)
                    bld.wait("tensor", s_pb[psl])
                    for cb in range(8):
                        BO("tensor", "matmul", ps2[psl][:],
                           wbuf[wsl][:, cb * 128:(cb + 1) * 128],
                           xbuf[xsl][:, cb * 512:(cb + 1) * 512],
                           start=(cb == 0), stop=(cb == 7),
                           sem=bld.mm if cb == 7 else None)
                    bld.tick("tensor", s_wb[wsl])
                    if eb == 2 * DI // 128 - 1:
                        bld.tick("tensor", s_xb[xsl])
                    esl = eb % 2
                    bld.wait("scalar", bld.mm)
                    bld.wait("scalar", s_es[esl])
                    BO("scalar", "copy", est[esl][:], ps2[psl][:], sem=bld.ev)
                    bld.tick("scalar", s_pb[psl])
                    bld.wait("sync", bld.ev)
                    dstb = S["xc"] if eb < NK else S["z"]
                    er = eb if eb < NK else eb - NK
                    bld.dma("sync", dstb.ap()[er * 128:(er + 1) * 128,
                                              tt * 512:(tt + 1) * 512],
                            est[esl][:], sem=s_es[esl])
            bld.barrier()
            ph2a.close()


        if phase_limit >= 3:
            ph2b = contextlib.ExitStack()
            xc_t = sb("xc_t", [PP, NK * (TCH + 3)], F32, ph2b)
            u_t = sb("u_t", [PP, NK * TCH], F32, ph2b)
            ub_t = sb("ub_t", [PP, NK * TCH], BF16, ph2b)
            zs_t = sb("zs_t", [PP, NK * TCH], F32, ph2b)
            dt_t = sb("dt_t", [PP, NK * TCH], F32, ph2b)
            xdT = sb("xdT", [33, TCH], F32, ph2b)
            xdf0 = sb("xdf0", [1, TCH], F32, ph2b)
            xdfs = sb("xdfs", [1, 33 * TS], F32, ph2b)
            dA = sb("dA", [PP, TS * 256], F32, ph2b)
            dBu = sb("dBu", [PP, TS * 256], F32, ph2b)
            hist = sb("hist", [PP, (TS + 1) * 256], F32, ph2b)
            hcs = sb("hcs", [PP, 8 * 256], F32, ph2b)
            brep = sb("brep", [PP, DS * TS], F32, ph2b)
            crep = sb("crep", [PP, DS * TS], F32, ph2b)
            dus = sb("dus", [PP, NK * TS], F32, ph2b)
            ysub = sb("ysub", [PP, TS * NK], F32, ph2b)
            ysq = sb("ysq", [PP, TS * NK], F32, ph2b)
            yrow = sb("yrow", [1, 4 * TS], F32, ph2b)
            prodb = sb("prodb", [PP, NK * TCH], BF16, ph2b)
            cm100 = sb("cm100", [PP, 4], F32, ph2b)
            ps_xd = psb("ps_xd", [33, TCH], F32, ph2b)
            ps_dr = psb("ps_dr", [PP, TCH], F32, ph2b)
            ps_bc2 = psb("ps_bc2", [PP, DS * TS], F32, ph2b)
            ps_row = psb("ps_row", [1, 2 * TS], F32, ph2b)
            ps_mu = psb("ps_mu", [PP, 2 * TS], F32, ph2b)
            hist_last = hist[:, TS * 256:(TS + 1) * 256]

            s_gd = bld.slot("b2_dbu")
            s_vs = bld.slot("b2_stepsdone")
            s_chv = bld.slot("b2_chv")
            s_chs = bld.slot("b2_chs")
            s_prod = bld.slot("b2_prod")

            BO("vector", "memset", hist[:], 0.0)
            BO("vector", "memset", cm100[:, 0:1], -100.0)
            BO("vector", "memset", cm100[:, 1:2], 100.0)
            BO("vector", "memset", cm100[:, 2:3], EXP_NEG20, sem=bld.vv)
            cm100b = cm100[:, 0:1].broadcast_to([PP, 256])
            c100b = cm100[:, 1:2].broadcast_to([PP, 256])
            e20b = cm100[:, 2:3].broadcast_to([PP, TS * 256])

            xcv = lambda k: xc_t[:, k * (TCH + 3):(k + 1) * (TCH + 3)]
            uv = lambda k: u_t[:, k * TCH:(k + 1) * TCH]
            ubv = lambda k: ub_t[:, k * TCH:(k + 1) * TCH]
            zv = lambda k: zs_t[:, k * TCH:(k + 1) * TCH]
            dtv = lambda k: dt_t[:, k * TCH:(k + 1) * TCH]
            dt3 = dt_t[:].rearrange("p (c t) -> p t c", c=NK)
            u3 = u_t[:].rearrange("p (c t) -> p t c", c=NK)
            ysub3 = ysub[:].rearrange("p (t c) -> p t c", t=TS)
            ysq3 = ysq[:].rearrange("p (t c) -> p t c", t=TS)

            for ch in range(NCH2):
                t0 = ch * TCH
                bld.wait("sync", s_chv)
                bld.wait("sync", s_chs)
                for k in range(NK):
                    if ch == 0:
                        bld.dma("sync", xcv(k)[:, 3:],
                                S["xc"].ap()[k * 128:(k + 1) * 128, t0:t0 + TCH])
                    else:
                        bld.dma("sync", xcv(k)[:, :],
                                S["xc"].ap()[k * 128:(k + 1) * 128, t0 - 3:t0 + TCH])
                    bld.dma("sync", zv(k)[:, :],
                            S["z"].ap()[k * 128:(k + 1) * 128, t0:t0 + TCH])
                bld.wait("vector", bld.dm)
                if ch == 0:
                    for k in range(NK):
                        BO("vector", "memset", xcv(k)[:, 0:3], 0.0)
                for k in range(NK):
                    w = lambda j: conv_w_t[:, k * 4 + j:k * 4 + j + 1]
                    BO("vector", "tensor_scalar", uv(k), xcv(k)[:, 0:TCH], w(0), None,
                       ALU.mult)
                    for j in range(1, 4):
                        BO("vector", "scalar_tensor_tensor", uv(k), xcv(k)[:, j:j + TCH],
                           w(j), uv(k), ALU.mult, ALU.add,
                           sem=bld.vv if j == 3 else None)
                bld.wait("scalar", bld.vv)
                bld.wait("scalar", bld.dm)
                for k in range(NK):
                    BO("scalar", "activation", uv(k), uv(k), ACTF.Silu,
                       bias=conv_b_t[:, k:k + 1], scale=1.0)
                    BO("scalar", "copy", ubv(k), uv(k))
                    BO("scalar", "activation", zv(k), zv(k), ACTF.Silu,
                       sem=bld.ev if k == NK - 1 else None)
                bld.wait("tensor", bld.ev)
                for k in range(NK):
                    BO("tensor", "matmul", ps_xd[:, :],
                       w_xT_b[:, k * 33:(k + 1) * 33], ubv(k),
                       start=(k == 0), stop=(k == NK - 1),
                       sem=bld.mm if k == NK - 1 else None)
                bld.wait("scalar", bld.mm)
                BO("scalar", "copy", xdT[:, :], ps_xd[:, :], sem=bld.ev)
                bld.wait("sync", bld.ev)
                bld.dma("sync", xdf0[:], xdT[0:1, :])
                bld.wait("tensor", bld.dm)
                BO("tensor", "matmul", ps_dr[:, :],
                   ones_f[0:1, 0:1].broadcast_to([1, 128]), xdf0[0:1, :],
                   start=True, stop=True, sem=bld.mm)
                bld.wait("scalar", bld.mm)
                for k in range(NK):
                    BO("scalar", "activation", dtv(k), ps_dr[:, :], ACTF.Exp,
                       bias=b_dt_t[:, k:k + 1], scale=w_dt_t[:, k:k + 1])
                    BO("scalar", "activation", dtv(k), dtv(k), ACTF.Ln, bias=1.0,
                       sem=bld.ev if k == NK - 1 else None)
                bld.wait("vector", bld.ev)
                for k in range(NK):
                    BO("vector", "tensor_scalar", dtv(k), dtv(k), 1e-4, 10.0,
                       ALU.max, ALU.min, sem=bld.vv if k == NK - 1 else None)

                for s in range(NSUB):
                    ts0 = s * TS
                    bld.wait("sync", bld.mm)
                    bld.dma("sync", xdfs[0:1, :], xdT[:, ts0:ts0 + TS],
                            allow_slow_non_contiguous=True)
                    bld.wait("tensor", bld.vv)
                    bld.wait("tensor", bld.dm)
                    bld.wait("tensor", bld.ev)
                    for n in range(DS):
                        BO("tensor", "matmul", ps_bc2[:, n * TS:(n + 1) * TS],
                           ones_f[0:1, 0:1].broadcast_to([1, 128]),
                           xdfs[0:1, (1 + n) * TS:(2 + n) * TS],
                           start=True, stop=True,
                           sem=bld.mm if n == DS - 1 else None)
                    bld.wait("scalar", bld.mm)
                    bld.wait("scalar", s_gd)
                    bld.wait("scalar", bld.vv)
                    BO("scalar", "copy", brep[:], ps_bc2[:], sem=bld.ev)
                    bld.wait("tensor", bld.ev)
                    for n in range(DS):
                        BO("tensor", "matmul", ps_bc2[:, n * TS:(n + 1) * TS],
                           ones_f[0:1, 0:1].broadcast_to([1, 128]),
                           xdfs[0:1, (17 + n) * TS:(18 + n) * TS],
                           start=True, stop=True,
                           sem=bld.mm if n == DS - 1 else None)
                    bld.wait("scalar", bld.mm)
                    BO("scalar", "copy", crep[:], ps_bc2[:], sem=bld.ev)
                    bld.wait("scalar", s_vs)
                    dA4 = dA[:].rearrange("p (t c n) -> p t c n", t=TS, c=NK)
                    for n in range(DS):
                        BO("scalar", "activation", dA4[:, :, :, n],
                           dt3[:, ts0:ts0 + TS, :], ACTF.Exp,
                           scale=a_cols[:, n:n + 1],
                           sem=bld.ev if n == DS - 1 else None)
                    if s == NSUB - 1:
                        bld.tick("scalar", s_chs)
                    BO("vector", "tensor_tensor",
                       dus[:].rearrange("p (t c) -> p t c", t=TS),
                       dt3[:, ts0:ts0 + TS, :], u3[:, ts0:ts0 + TS, :], ALU.mult)
                    bld.wait("vector", bld.ev)
                    bld.wait("vector", s_vs)
                    BO("vector", "tensor_tensor",
                       dBu[:].rearrange("p (t c n) -> p t c n", t=TS, c=NK),
                       dus[:].rearrange("p (t c) -> p t c", t=TS).unsqueeze(3)
                           .broadcast_to([PP, TS, NK, DS]),
                       brep[:].rearrange("p (n t) -> p t n", n=DS).unsqueeze(2)
                           .broadcast_to([PP, TS, NK, DS]), ALU.mult)
                    BO("vector", "tensor_scalar", dBu[:], dBu[:], 10.0, -10.0,
                       ALU.min, ALU.max)
                    bld.tick("vector", s_gd)
                    BO("vector", "tensor_copy", hist[:, 0:256], hist_last)
                    bld.wait("vector", bld.ev)
                    for grp in range(TS // 8):
                        for tl in range(grp * 8, grp * 8 + 8):
                            hp = hist[:, tl * 256:(tl + 1) * 256]
                            hc = hist[:, (tl + 1) * 256:(tl + 2) * 256]
                            av = dA[:, tl * 256:(tl + 1) * 256]
                            bv = dBu[:, tl * 256:(tl + 1) * 256]
                            BO("vector", "scalar_tensor_tensor", hc, av, EXP_NEG20,
                               hp, ALU.max, ALU.mult)
                            BO("vector", "tensor_tensor", hc, hc, bv, ALU.add)
                            BO("vector", "scalar_tensor_tensor", hc, hc, 100.0,
                               cm100b, ALU.min, ALU.max)
                        hv = hist[:, (1 + grp * 8) * 256:(1 + grp * 8 + 8) * 256]
                        BO("vector", "tensor_tensor",
                           hcs[:].rearrange("p (t c n) -> p t c n", t=8, c=NK),
                           hv.rearrange("p (t c n) -> p t c n", t=8, c=NK),
                           crep[:].rearrange("p (n t) -> p t n", n=DS)[
                               :, grp * 8:grp * 8 + 8, :].unsqueeze(2)
                               .broadcast_to([PP, 8, NK, DS]), ALU.mult)
                        BO("vector", "tensor_reduce",
                           ysub3[:, grp * 8:grp * 8 + 8, :],
                           hcs[:].rearrange("p (t c n) -> p t c n", t=8, c=NK),
                           AX.X, ALU.add)
                    bld.tick("vector", s_vs)
                    if "y_dbg" in dbg:
                        bld.vv.n += 1
                        bld.raw("vector", lambda e: e.engine_nop().then_inc(bld.vv.h, 1))
                        bld.wait("sync", bld.vv)
                        for k in range(NK):
                            bld.dma("sync", S["y_dbg"].ap()[
                                k * 128:(k + 1) * 128,
                                t0 + ts0:t0 + ts0 + TS],
                                ysub3[:, :, k], sem=bld.dmb,
                                allow_slow_non_contiguous=True)
                        bld.wait("vector", bld.dmb)
                    BO("vector", "tensor_reduce", ysq[:, 0:TS], ysub3, AX.X, ALU.add,
                       sem=bld.vv)
                    bld.wait("tensor", bld.vv)
                    BO("tensor", "matmul", ps_row[0:1, 0:TS], ones_f[:, 0:1],
                       ysq[:, 0:TS], start=True, stop=True, sem=bld.mm)
                    bld.wait("vector", bld.mm)
                    BO("vector", "tensor_scalar", yrow[0:1, 0:TS], ps_row[0:1, 0:TS],
                       1.0 / DI, None, ALU.mult, sem=bld.vv)
                    bld.wait("tensor", bld.vv)
                    BO("tensor", "matmul", ps_mu[:, 0:TS],
                       ones_f[0:1, 0:1].broadcast_to([1, 128]), yrow[0:1, 0:TS],
                       start=True, stop=True, sem=bld.mm)
                    bld.wait("vector", bld.mm)
                    mu3 = ps_mu[:, 0:TS].unsqueeze(2).broadcast_to([PP, TS, NK])
                    BO("vector", "tensor_tensor", ysub3, ysub3, mu3, ALU.subtract,
                       sem=bld.vv)
                    bld.wait("scalar", bld.vv)
                    BO("scalar", "activation", ysq[:, 0:TS * NK], ysub[:], ACTF.Square,
                       sem=bld.ev)
                    bld.wait("vector", bld.ev)
                    BO("vector", "tensor_reduce", ysq[:, 0:TS], ysq3, AX.X, ALU.add,
                       sem=bld.vv)
                    bld.wait("tensor", bld.vv)
                    BO("tensor", "matmul", ps_row[0:1, TS:2 * TS], ones_f[:, 0:1],
                       ysq[:, 0:TS], start=True, stop=True, sem=bld.mm)
                    bld.wait("vector", bld.mm)
                    yv = yrow[0:1, TS:2 * TS]
                    BO("vector", "tensor_scalar", yv, ps_row[0:1, TS:2 * TS],
                       1.0 / DI, LN_EPS, ALU.mult, ALU.add, sem=bld.vv)
                    bld.wait("scalar", bld.vv)
                    BO("scalar", "activation", yrow[0:1, 2 * TS:3 * TS], yv, ACTF.Sqrt,
                       sem=bld.ev)
                    bld.wait("vector", bld.ev)
                    s0v = yrow[0:1, 2 * TS:3 * TS]
                    r0v = yrow[0:1, 3 * TS:4 * TS]
                    BO("vector", "reciprocal", r0v, s0v, sem=bld.vv)
                    bld.wait("vector", bld.vv)
                    bld.wait("tensor", bld.vv)
                    BO("tensor", "matmul", ps_mu[:, TS:2 * TS],
                       ones_f[0:1, 0:1].broadcast_to([1, 128]), r0v,
                       start=True, stop=True, sem=bld.mm)
                    bld.wait("vector", bld.mm)
                    rst3 = ps_mu[:, TS:2 * TS].unsqueeze(2).broadcast_to([PP, TS, NK])
                    lng3 = ln_g_t[:, 0:NK].unsqueeze(1).broadcast_to([PP, TS, NK])
                    lnb3 = ln_b_t[:, 0:NK].unsqueeze(1).broadcast_to([PP, TS, NK])
                    dpr3 = d_par_t[:, 0:NK].unsqueeze(1).broadcast_to([PP, TS, NK])
                    BO("vector", "tensor_tensor", ysub3, ysub3, rst3, ALU.mult)
                    BO("vector", "tensor_tensor", ysub3, ysub3, lng3, ALU.mult)
                    BO("vector", "tensor_tensor", ysub3, ysub3, lnb3, ALU.add)
                    usl = u3[:, ts0:ts0 + TS, :]
                    BO("vector", "tensor_tensor", ysq3, usl, dpr3, ALU.mult)
                    BO("vector", "tensor_tensor", ysub3, ysub3, ysq3, ALU.add)
                    if s == 0:
                        bld.wait("vector", s_prod)
                    zsl = zs_t[:].rearrange("p (c t) -> p t c", c=NK)[
                        :, ts0:ts0 + TS, :]
                    BO("vector", "tensor_tensor",
                       prodb[:].rearrange("p (c t) -> p t c", c=NK)[
                           :, ts0:ts0 + TS, :],
                       ysub3, zsl, ALU.mult, sem=bld.vv)
                bld.tick("vector", s_chv)
                bld.wait("sync", bld.vv)
                for k in range(NK):
                    bld.dma("sync", S["prod_bf"].ap()[k * 128:(k + 1) * 128,
                                                      t0:t0 + TCH],
                            prodb[:, k * TCH:(k + 1) * TCH], sem=s_prod)
            bld.barrier()
            ph2b.close()


        if phase_limit >= 4:
            ph2c = contextlib.ExitStack()
            pbuf = [sb(f"c2p{i}", [PP, NK * 512], BF16, ph2c) for i in range(2)]
            wob = sb("c2w", [PP, NK * 1024], BF16, ph2c)
            sst = [sb(f"c2s{i}", [PP, 1024], F32, ph2c) for i in range(2)]
            ps2c = [psb(f"c2ps{i}", [PP, 512], F32, ph2c) for i in range(4)]
            s_pb2 = [bld.slot("c2pb0"), bld.slot("c2pb1")]
            s_ss = [bld.slot("c2ss0"), bld.slot("c2ss1")]
            s_pe = [bld.slot(f"c2pe{i}") for i in range(4)]
            for k in range(NK):
                bld.dma("sync", wob[:, k * 1024:(k + 1) * 1024],
                        S["woutT"].ap()[k * 128:(k + 1) * 128, :])
            for tt in range(T // 512):
                psl = tt % 2
                bld.wait("sync", s_pb2[psl])
                for k in range(NK):
                    bld.dma("sync", pbuf[psl][:, k * 512:(k + 1) * 512],
                            S["prod_bf"].ap()[k * 128:(k + 1) * 128,
                                              tt * 512:(tt + 1) * 512])
                for ts4 in range(4):
                    for oh in range(2):
                        pb = 2 * (ts4 % 2) + oh
                        bld.wait("tensor", bld.dm)
                        bld.wait("tensor", s_pe[pb])
                        for k in range(NK):
                            BO("tensor", "matmul", ps2c[pb][:],
                               pbuf[psl][:, k * 512 + ts4 * 128:
                                         k * 512 + (ts4 + 1) * 128],
                               wob[:, k * 1024 + oh * 512:k * 1024 + (oh + 1) * 512],
                               start=(k == 0), stop=(k == NK - 1),
                               sem=bld.mm if k == NK - 1 else None)
                        if ts4 == 3 and oh == 1:
                            bld.tick("tensor", s_pb2[psl])
                        ssl = ts4 % 2
                        bld.wait("scalar", bld.mm)
                        bld.wait("scalar", s_ss[ssl])
                        BO("scalar", "copy", sst[ssl][:, oh * 512:(oh + 1) * 512],
                           ps2c[pb][:], sem=bld.ev)
                        bld.tick("scalar", s_pe[pb])
                    bld.wait("sync", bld.ev)
                    ssl = ts4 % 2
                    bld.dma("sync",
                            S["s0"].ap()[tt * 512 + ts4 * 128:
                                         tt * 512 + (ts4 + 1) * 128, :],
                            sst[ssl][:], sem=s_ss[ssl])
            bld.barrier()
            ph2c.close()

        if phase_limit >= 5:
            ph3a = contextlib.ExitStack()
            xqT = [sb(f"d3xq{i}", [PP, T], BF16, ph3a) for i in range(8)]
            x3t = sb("d3x", [PP, C], F32, ph3a)
            q3b = sb("d3q", [PP, C], BF16, ph3a)
            q3i = sb("d3i", [PP, C], I32, ph3a)
            trb3 = sb("d3t", [PP, C], BF16, ph3a)
            wg3 = [sb(f"d3wg{i}", [PP, 1024], BF16, ph3a) for i in range(2)]
            wu3 = [sb(f"d3wu{i}", [PP, 1024], BF16, ph3a) for i in range(2)]
            gsil = sb("d3gs", [PP, 512], F32, ph3a)
            guf = sb("d3gu", [PP, 512], F32, ph3a)
            gub = sb("d3gb", [PP, 512], BF16, ph3a)
            st3 = sb("d3st", [PP, 8], F32, ph3a)
            ps_g3 = psb("d3pg", [PP, 512], F32, ph3a)
            ps_u3 = psb("d3pu", [PP, 512], F32, ph3a)
            ps_t3 = psb("d3pt", [PP, C], BF16, ph3a)
            s_w3 = [bld.slot("d3w0"), bld.slot("d3w1")]
            s_gu = bld.slot("d3gu")
            for it in range(T // 128):
                bld.wait("sync", bld.vv)
                bld.wait("sync", bld.ev)
                bld.dma("sync", x3t[:], S["xn"].ap()[it * 128:(it + 1) * 128, :])
                bld.wait("vector", bld.dm)
                BO("vector", "tensor_scalar", x3t[:], x3t[:], qs[:, 0:1], 127.0,
                   ALU.mult, ALU.min)
                BO("vector", "tensor_scalar", x3t[:], x3t[:], -128.0, MAGIC,
                   ALU.max, ALU.add)
                BO("vector", "tensor_scalar", q3b[:], x3t[:], MAGIC, qs[:, 1:2],
                   ALU.subtract, ALU.mult, sem=bld.vv)
                if "xq_dbg" in dbg:
                    BO("vector", "tensor_scalar", x3t[:], x3t[:], MAGIC, None,
                       ALU.subtract, sem=bld.vv)
                    bld.wait("sync", bld.vv)
                    bld.dma("sync", S["xq_dbg"].ap()[it * 128:(it + 1) * 128, :],
                            x3t[:])
                bld.wait("tensor", bld.vv)
                bld.wait("tensor", bld.ev)
                for cb in range(8):
                    BO("tensor", "transpose", ps_t3[:, cb * 128:(cb + 1) * 128],
                       q3b[:, cb * 128:(cb + 1) * 128], ident_b[:],
                       sem=bld.mm if cb == 7 else None)
                bld.wait("scalar", bld.mm)
                BO("scalar", "copy", trb3[:], ps_t3[:], sem=bld.ev)
                bld.wait("vector", bld.ev)
                for cb in range(8):
                    BO("vector", "tensor_copy", xqT[cb][:, it * 128:(it + 1) * 128],
                       trb3[:, cb * 128:(cb + 1) * 128],
                       sem=bld.vv if cb == 7 else None)
            for ib in range(NB0):
                wsl = ib % 2
                bld.wait("sync", s_w3[wsl])
                bld.dma("sync", wg3[wsl][:],
                        S["wq_g0T"].ap()[ib].rearrange("(q p) m -> p q m", p=128))
                bld.dma("sync", wu3[wsl][:],
                        S["wq_u0T"].ap()[ib].rearrange("(q p) m -> p q m", p=128))
                if ib == 0 and "w0_dbg" in dbg:
                    bld.wait("vector", bld.dm)
                    BO("vector", "tensor_copy", gsil[:, 0:512], wg3[0][:, 0:512])
                    BO("vector", "tensor_copy", guf[:, 0:512], wg3[0][:, 512:1024],
                       sem=bld.vv)
                    bld.wait("sync", bld.vv)
                    bld.dma("sync", S["w0_dbg"].ap()[:, 0:512], gsil[:, 0:512])
                    bld.dma("sync", S["w0_dbg"].ap()[:, 512:1024], guf[:, 0:512])
                    bld.dma("sync", trb3[:, 0:128],
                            S["wq_g0T"].ap()[0, 0:128, :], sem=bld.dm)
                    bld.wait("vector", bld.dm)
                    BO("vector", "tensor_copy", gsil[:, 0:128], trb3[:, 0:128],
                       sem=bld.vv)
                    bld.wait("sync", bld.vv)
                    bld.dma("sync", S["w0_dbg"].ap()[:, 1024:1152], gsil[:, 0:128])
                for tt in range(T // 512):
                    bld.wait("tensor", bld.dm)
                    bld.wait("tensor", bld.vv)
                    if tt == 0 and ib == 0:
                        pass
                    bld.wait("tensor", bld.ev)
                    for cb in range(8):
                        BO("tensor", "matmul", ps_g3[:],
                           wg3[wsl][:, cb * 128:(cb + 1) * 128],
                           xqT[cb][:, tt * 512:(tt + 1) * 512],
                           start=(cb == 0), stop=(cb == 7))
                    for cb in range(8):
                        BO("tensor", "matmul", ps_u3[:],
                           wu3[wsl][:, cb * 128:(cb + 1) * 128],
                           xqT[cb][:, tt * 512:(tt + 1) * 512],
                           start=(cb == 0), stop=(cb == 7),
                           sem=bld.mm if cb == 7 else None)
                    if tt == T // 512 - 1:
                        bld.tick("tensor", s_w3[wsl])
                    bld.wait("scalar", bld.mm)
                    BO("scalar", "activation", gsil[:], ps_g3[:], ACTF.Silu,
                       sem=bld.ev)
                    bld.wait("vector", bld.ev)
                    bld.wait("vector", s_gu)
                    BO("vector", "tensor_tensor", guf[:], gsil[:], ps_u3[:], ALU.mult)
                    BO("vector", "tensor_reduce", st3[:, 0:1], guf[:], AX.X, ALU.max,
                       apply_absolute_value=True, sem=bld.vv)
                    bld.wait("vector", bld.vv)
                    BO("vector", "tensor_tensor", lmax[:, 1:2], lmax[:, 1:2],
                       st3[:, 0:1], ALU.max)
                    BO("vector", "tensor_copy", gub[:], guf[:], sem=bld.vv)
                    bld.tick("vector", bld.ev)
                    bld.wait("sync", bld.vv)
                    bld.dma("sync", S["gu_bf"].ap()[ib * 128:(ib + 1) * 128,
                                                    tt * 512:(tt + 1) * 512],
                            gub[:], sem=s_gu)
            global_max(1, 1)
            derive_scale(1, 127.0, 15, 2, 3)
            bld.barrier()
            ph3a.close()

        if phase_limit >= 6:
            ph3b = contextlib.ExitStack()
            wd3 = [sb(f"e3wd{i}", [PP, 1024], BF16, ph3b) for i in range(NB0)]
            gu3 = sb("e3gu", [PP, NB0 * 512], BF16, ph3b)
            hq3 = sb("e3hq", [PP, NB0 * 512], BF16, ph3b)
            hqi = sb("e3hi", [PP, 512], I32, ph3b)
            hqf = sb("e3hf", [PP, 512], F32, ph3b)
            g3 = sb("e3g", [PP, C], F32, ph3b)
            xx3 = sb("e3x", [PP, C], F32, ph3b)
            ss3 = sb("e3s", [PP, C], F32, ph3b)
            xn1t = sb("e3n", [PP, C], F32, ph3b)
            xn1b = sb("e3nb", [PP, C], BF16, ph3b)
            dmp3 = sb("e3d", [PP, C], F32, ph3b)
            st3b = sb("e3st", [PP, 8], F32, ph3b)
            trb3b = sb("e3t", [PP, C], BF16, ph3b)
            ps_d3 = [psb(f"e3p{i}", [PP, 512], F32, ph3b) for i in range(2)]
            ps_t3b = psb("e3pt", [PP, C], BF16, ph3b)
            for k in range(NB0):
                bld.dma("sync", wd3[k][:],
                        S["wq_d0T"].ap()[k * 128:(k + 1) * 128, :])
            for tt in range(T // 512):
                bld.wait("sync", bld.vv)
                for k in range(NB0):
                    bld.dma("sync", gu3[:, k * 512:(k + 1) * 512],
                            S["gu_bf"].ap()[k * 128:(k + 1) * 128,
                                            tt * 512:(tt + 1) * 512])
                bld.wait("vector", bld.dm)
                for k in range(NB0):
                    BO("vector", "tensor_scalar", hqf[:], gu3[:, k * 512:(k + 1) * 512],
                       qs[:, 2:3], 127.0, ALU.mult, ALU.min)
                    BO("vector", "tensor_scalar", hqf[:], hqf[:], -128.0, MAGIC,
                       ALU.max, ALU.add)
                    BO("vector", "tensor_scalar", hq3[:, k * 512:(k + 1) * 512],
                       hqf[:], MAGIC, qs[:, 3:4], ALU.subtract, ALU.mult,
                       sem=bld.vv if k == NB0 - 1 else None)
                for t4 in range(4):
                    it = tt * 4 + t4
                    bld.wait("tensor", bld.vv)
                    bld.wait("tensor", bld.ev)
                    for oh in range(2):
                        for k in range(NB0):
                            BO("tensor", "matmul", ps_d3[oh][:],
                               hq3[:, k * 512 + t4 * 128:k * 512 + (t4 + 1) * 128],
                               wd3[k][:, oh * 512:(oh + 1) * 512],
                               start=(k == 0), stop=(k == NB0 - 1),
                               sem=bld.mm if (k == NB0 - 1 and oh == 1) else None)
                    bld.wait("scalar", bld.mm)
                    bld.wait("scalar", bld.vv)
                    BO("scalar", "activation", g3[:, 0:512], ps_d3[0][:], ACTF.Silu)
                    BO("scalar", "activation", g3[:, 512:1024], ps_d3[1][:], ACTF.Silu,
                       sem=bld.ev)
                    bld.wait("sync", bld.ev)
                    if "g0_dbg" in dbg:
                        bld.dma("sync", S["g0_dbg"].ap()[it * 128:(it + 1) * 128, :],
                                g3[:])
                    bld.dma("sync", xx3[:], D["x_in"][it * 128:(it + 1) * 128, :])
                    bld.dma("sync", ss3[:], S["s0"].ap()[it * 128:(it + 1) * 128, :])
                    bld.wait("vector", bld.dm)
                    bld.wait("vector", bld.ev)
                    BO("vector", "tensor_tensor", g3[:], g3[:], ss3[:], ALU.mult)
                    BO("vector", "tensor_tensor", xx3[:], xx3[:], g3[:], ALU.add,
                       sem=bld.vv)
                    bld.tick("vector", bld.ev)
                    bld.wait("sync", bld.vv)
                    bld.dma("sync", S["x1"].ap()[it * 128:(it + 1) * 128, :], xx3[:])
                    bld.wait("scalar", bld.vv)
                    emit_ln_tile(xx3[:], xn1t[:], xn1b[:], st3b, dmp3[:], C,
                                 absmax_slot=2)
                    bld.wait("sync", bld.vv)
                    bld.dma("sync", S["xn1"].ap()[it * 128:(it + 1) * 128, :],
                            xn1t[:])
                    bld.wait("tensor", bld.vv)
                    bld.wait("tensor", bld.ev)
                    for cb in range(8):
                        BO("tensor", "transpose", ps_t3b[:, cb * 128:(cb + 1) * 128],
                           xn1b[:, cb * 128:(cb + 1) * 128], ident_b[:],
                           sem=bld.mm if cb == 7 else None)
                    bld.wait("scalar", bld.mm)
                    BO("scalar", "copy", trb3b[:], ps_t3b[:], sem=bld.ev)
                    bld.wait("sync", bld.ev)
                    for cb in range(8):
                        bld.dma("sync", S["xn1T_bf"].ap()[cb * 128:(cb + 1) * 128,
                                                          it * 128:(it + 1) * 128],
                                trb3b[:, cb * 128:(cb + 1) * 128])
            global_max(2, 2)
            derive_scale(2, 32767.0, None, 4, 5, floor=1e-12)
            if "qs2_dbg" in dbg:
                bld.wait("sync", bld.vv)
                bld.wait("sync", bld.ev)
                bld.dma("sync", S["qs2_dbg"].ap()[:, 0:16], qs[:, 0:16])
                bld.dma("sync", S["qs2_dbg"].ap()[:, 16:24], gmax[:, 0:8])
                bld.dma("sync", S["qs2_dbg"].ap()[:, 24:32], lmax[:, 0:8])
            bld.barrier()
            ph3b.close()

        attn_sb = sb("attn_sb", [64, H * DH], BF16)
        if phase_limit >= 7:
            ph4a = contextlib.ExitStack()
            wk4 = [sb(f"f4wk{i}", [PP, C], BF16, ph4a) for i in range(8)]
            wv4 = [sb(f"f4wv{i}", [PP, C], BF16, ph4a) for i in range(8)]
            wg4 = [sb(f"f4wg{i}", [PP, C], BF16, ph4a) for i in range(8)]
            xT4 = sb("f4x", [PP, 8 * 128], BF16, ph4a)
            kst = sb("f4k", [PP, C], F32, ph4a)
            krp = sb("f4kr", [PP, C], BF16, ph4a)
            rt1 = sb("f4r1", [PP, 512], F32, ph4a)
            rt2 = sb("f4r2", [PP, 512], F32, ph4a)
            cosT = sb("f4c", [PP, 32], F32, ph4a)
            sinT = sb("f4s", [PP, 32], F32, ph4a)
            kTh = sb("f4kt", [64, H * 128], BF16, ph4a)
            vb4 = sb("f4v", [PP, C], BF16, ph4a)
            gst = sb("f4g", [PP, C], F32, ph4a)
            gsb = sb("f4gb", [PP, C], BF16, ph4a)
            gmx = sb("f4gm", [PP, 20], F32, ph4a)
            trg = sb("f4tg", [PP, C], BF16, ph4a)
            sc4 = sb("f4sc", [64, CS], F32, ph4a)
            e4 = sb("f4e", [64, CS], F32, ph4a)
            pm4 = sb("f4pm", [64, 4], F32, ph4a)
            pb4 = sb("f4pb", [64, CS], BF16, ph4a)
            pT4 = sb("f4pt", [PP, 64], BF16, ph4a)
            attn_acc = sb("f4aa", [64, H * DH], F32, ph4a)
            ps_kv = psb("f4p", [PP, 512], F32, ph4a)
            ps_sc = psb("f4ps", [64, CS], F32, ph4a)
            ps_at = psb("f4pa", [64, DH], F32, ph4a)
            ps_kT = psb("f4pk", [64, H * 128], BF16, ph4a)
            ps_tg = psb("f4ptg", [PP, C], BF16, ph4a)
            ps_pt = ps_tg[:, 0:64]
            BO("vector", "memset", attn_acc[:, :], 0.0, sem=bld.vv)
            for cb in range(8):
                bld.dma("sync", wk4[cb][:], S["wkT"].ap()[cb * 128:(cb + 1) * 128, :])
                bld.dma("sync", wv4[cb][:], S["wvT"].ap()[cb * 128:(cb + 1) * 128, :])
                bld.dma("sync", wg4[cb][:], S["wgT"].ap()[cb * 128:(cb + 1) * 128, :])
            for it in range(T // 128):
                bld.wait("sync", bld.mm)
                bld.wait("sync", bld.vv)
                bld.wait("sync", bld.ev)
                for cb in range(8):
                    bld.dma("sync", xT4[:, cb * 128:(cb + 1) * 128],
                            S["xn1T_bf"].ap()[cb * 128:(cb + 1) * 128,
                                              it * 128:(it + 1) * 128])
                bld.dma("sync", cosT[:, :], D["cos_t"][it * 128:(it + 1) * 128, :])
                bld.dma("sync", sinT[:, :], D["sin_t"][it * 128:(it + 1) * 128, :])
                for (dst, wsrc) in [(kst, wk4), (vb4, wv4), (gst, wg4)]:
                    for oh in range(2):
                        bld.wait("tensor", bld.dm)
                        bld.wait("tensor", bld.ev)
                        for cb in range(8):
                            BO("tensor", "matmul", ps_kv[:],
                               xT4[:, cb * 128:(cb + 1) * 128],
                               wsrc[cb][:, oh * 512:(oh + 1) * 512],
                               start=(cb == 0), stop=(cb == 7),
                               sem=bld.mm if cb == 7 else None)
                        bld.wait("scalar", bld.mm)
                        BO("scalar", "copy", dst[:, oh * 512:(oh + 1) * 512],
                           ps_kv[:], sem=bld.ev)
                bld.wait("vector", bld.ev)
                av2 = kst[:].rearrange("p (h i two) -> p h i two", h=H, two=2)
                a4 = av2[:, :, :, 0]
                b4 = av2[:, :, :, 1]
                cexp = cosT[:].unsqueeze(1).broadcast_to([PP, H, 32])
                sexp = sinT[:].unsqueeze(1).broadcast_to([PP, H, 32])
                r13 = rt1[:].rearrange("p (h i) -> p h i", h=H)
                r23 = rt2[:].rearrange("p (h i) -> p h i", h=H)
                kv4 = krp[:].rearrange("p (h i two) -> p h i two", h=H, two=2)
                BO("vector", "tensor_tensor", r13, a4, cexp, ALU.mult)
                BO("vector", "tensor_tensor", r23, b4, sexp, ALU.mult)
                BO("vector", "tensor_tensor", kv4[:, :, :, 0], r13, r23, ALU.subtract)
                BO("vector", "tensor_tensor", r13, a4, sexp, ALU.mult)
                BO("vector", "tensor_tensor", r23, b4, cexp, ALU.mult)
                BO("vector", "tensor_tensor", kv4[:, :, :, 1], r13, r23, ALU.add,
                   sem=bld.vv)
                bld.wait("tensor", bld.vv)
                bld.wait("tensor", bld.ev)
                for h in range(H):
                    BO("tensor", "transpose", ps_kT[:, h * 128:(h + 1) * 128],
                       krp[:, h * 64:(h + 1) * 64], ident_b[:],
                       sem=bld.mm if h == H - 1 else None)
                bld.wait("scalar", bld.mm)
                BO("scalar", "copy", kTh[:, :], ps_kT[:, :], sem=bld.ev)
                bld.wait("vector", bld.ev)
                g3v = gst[:].rearrange("p (h l) -> p h l", h=H)
                BO("vector", "tensor_reduce", gmx[:, 0:16], g3v, AX.X, ALU.max,
                   sem=bld.vv)
                bld.wait("vector", bld.vv)
                mexp4 = gmx[:, 0:16].unsqueeze(2).broadcast_to([PP, H, L])
                BO("vector", "tensor_tensor", g3v, g3v, mexp4, ALU.subtract,
                   sem=bld.vv)
                bld.wait("scalar", bld.vv)
                BO("scalar", "activation", gst[:], gst[:], ACTF.Exp, scale=0.125,
                   sem=bld.ev)
                bld.wait("vector", bld.ev)
                BO("vector", "tensor_reduce", gmx[:, 0:16], g3v, AX.X, ALU.add,
                   sem=bld.vv)
                bld.wait("vector", bld.vv)
                BO("vector", "reciprocal", gmx[:, 0:16], gmx[:, 0:16], sem=bld.vv)
                bld.wait("vector", bld.vv)
                BO("vector", "tensor_tensor",
                   gsb[:].rearrange("p (h l) -> p h l", h=H), g3v,
                   gmx[:, 0:16].unsqueeze(2).broadcast_to([PP, H, L]), ALU.mult,
                   sem=bld.vv)
                bld.wait("tensor", bld.vv)
                for cb in range(8):
                    BO("tensor", "transpose", ps_tg[:, cb * 128:(cb + 1) * 128],
                       gsb[:, cb * 128:(cb + 1) * 128], ident_b[:],
                       sem=bld.mm if cb == 7 else None)
                bld.wait("scalar", bld.mm)
                BO("scalar", "copy", trg[:], ps_tg[:], sem=bld.ev)
                bld.wait("sync", bld.ev)
                for cb in range(8):
                    bld.dma("sync", S["gatesT_bf"].ap()[cb * 128:(cb + 1) * 128,
                                                        it * 128:(it + 1) * 128],
                            trg[:, cb * 128:(cb + 1) * 128])
                for h in range(H):
                    bld.wait("tensor", bld.vv)
                    BO("tensor", "matmul", ps_sc[:, :],
                       qh_b[:, h * L:(h + 1) * L],
                       kTh[:, h * 128:(h + 1) * 128],
                       start=True, stop=True, sem=bld.mm)
                    bld.wait("vector", bld.mm)
                    BO("vector", "tensor_tensor", sc4[:], ps_sc[:, :], cmask_t[:],
                       ALU.add)
                    BO("vector", "tensor_reduce", pm4[:, 0:1], sc4[:], AX.X, ALU.max,
                       sem=bld.vv)
                    bld.wait("vector", bld.vv)
                    BO("vector", "tensor_scalar", pm4[:, 1:2], pm4[:, 0:1], -0.125,
                       None, ALU.mult, sem=bld.vv)
                    bld.wait("scalar", bld.vv)
                    BO("scalar", "activation", e4[:], sc4[:], ACTF.Exp, scale=0.125,
                       bias=pm4[:, 1:2], sem=bld.ev)
                    bld.wait("vector", bld.ev)
                    BO("vector", "tensor_reduce", pm4[:, 2:3], e4[:], AX.X, ALU.add,
                       sem=bld.vv)
                    bld.wait("vector", bld.vv)
                    BO("vector", "reciprocal", pm4[:, 3:4], pm4[:, 2:3], sem=bld.vv)
                    bld.wait("vector", bld.vv)
                    BO("vector", "tensor_scalar", pb4[:], e4[:], pm4[:, 3:4], None,
                       ALU.mult, sem=bld.vv)
                    bld.wait("tensor", bld.vv)
                    BO("tensor", "transpose", ps_pt, pb4[:], ident_b[0:64, 0:64],
                       sem=bld.mm)
                    bld.wait("scalar", bld.mm)
                    BO("scalar", "copy", pT4[:, :], ps_pt, sem=bld.ev)
                    bld.wait("tensor", bld.ev)
                    bld.wait("tensor", bld.vv)
                    BO("tensor", "matmul", ps_at[:, :],
                       pT4[:, :], vb4[:, h * DH:(h + 1) * DH],
                       start=True, stop=True, sem=bld.mm)
                    bld.wait("vector", bld.mm)
                    BO("vector", "tensor_tensor", attn_acc[:, h * DH:(h + 1) * DH],
                       attn_acc[:, h * DH:(h + 1) * DH], ps_at[:, :], ALU.add,
                       sem=bld.vv)
            bld.wait("scalar", bld.vv)
            BO("scalar", "copy", attn_sb[:, :], attn_acc[:, :], sem=bld.ev)
            bld.barrier()
            ph4a.close()

        if phase_limit >= 8:
            ph4b = contextlib.ExitStack()
            wp4 = [sb(f"g4wp{i}", [PP, C], BF16, ph4b) for i in range(8)]
            gT4 = sb("g4g", [64, H * 128], BF16, ph4b)
            wgt = sb("g4w", [PP, C], BF16, ph4b)
            wgtT = sb("g4wt", [PP, 8 * 128], BF16, ph4b)
            s1st = sb("g4s", [PP, C], F32, ph4b)
            ps_w4 = psb("g4pw", [PP, C], F32, ph4b)
            ps_s4 = [psb(f"g4ps{i}", [PP, 512], F32, ph4b) for i in range(2)]
            ps_t4 = psb("g4pt", [PP, C], BF16, ph4b)
            for cb in range(8):
                bld.dma("sync", wp4[cb][:], S["wpT"].ap()[cb * 128:(cb + 1) * 128, :])
            for it in range(T // 128):
                bld.wait("sync", bld.mm)
                bld.wait("sync", bld.ev)
                for h in range(H):
                    bld.dma("sync", gT4[:, h * 128:(h + 1) * 128],
                            S["gatesT_bf"].ap()[h * 64:(h + 1) * 64,
                                                it * 128:(it + 1) * 128])
                bld.wait("tensor", bld.dm)
                bld.wait("tensor", bld.ev)
                for h in range(H):
                    BO("tensor", "matmul", ps_w4[:, h * DH:(h + 1) * DH],
                       gT4[:, h * 128:(h + 1) * 128],
                       attn_sb[:, h * DH:(h + 1) * DH],
                       start=True, stop=True,
                       sem=bld.mm if h == H - 1 else None)
                bld.wait("scalar", bld.mm)
                BO("scalar", "copy", wgt[:], ps_w4[:], sem=bld.ev)
                bld.wait("tensor", bld.ev)
                for cb in range(8):
                    BO("tensor", "transpose", ps_t4[:, cb * 128:(cb + 1) * 128],
                       wgt[:, cb * 128:(cb + 1) * 128], ident_b[:],
                       sem=bld.mm if cb == 7 else None)
                bld.wait("scalar", bld.mm)
                BO("scalar", "copy", wgtT[:], ps_t4[:], sem=bld.ev)
                bld.wait("tensor", bld.ev)
                for oh in range(2):
                    for cb in range(8):
                        BO("tensor", "matmul", ps_s4[oh][:],
                           wgtT[:, cb * 128:(cb + 1) * 128],
                           wp4[cb][:, oh * 512:(oh + 1) * 512],
                           start=(cb == 0), stop=(cb == 7),
                           sem=bld.mm if cb == 7 else None)
                bld.wait("scalar", bld.mm)
                BO("scalar", "copy", s1st[:, 0:512], ps_s4[0][:])
                BO("scalar", "copy", s1st[:, 512:1024], ps_s4[1][:], sem=bld.ev)
                bld.wait("sync", bld.ev)
                bld.dma("sync", S["s1"].ap()[it * 128:(it + 1) * 128, :], s1st[:])
            bld.barrier()
            ph4b.close()

        if phase_limit >= 9:
            ph5a = contextlib.ExitStack()
            xqT5 = [sb(f"h5xq{i}", [PP, T], BF16, ph5a) for i in range(8)]
            x5t = sb("h5x", [PP, C], F32, ph5a)
            q5b = sb("h5q", [PP, C], BF16, ph5a)
            q5i = sb("h5i", [PP, C], I32, ph5a)
            trb5 = sb("h5t", [PP, C], BF16, ph5a)
            wg5 = [sb(f"h5wg{i}", [PP, 1024], BF16, ph5a) for i in range(2)]
            wu5 = [sb(f"h5wu{i}", [PP, 1024], BF16, ph5a) for i in range(2)]
            ub5 = sb("h5ub", [PP, 512], BF16, ph5a)
            gb5 = sb("h5gb", [PP, 512], BF16, ph5a)
            st5 = sb("h5st", [PP, 8], F32, ph5a)
            ps_g5 = psb("h5pg", [PP, 512], F32, ph5a)
            ps_u5 = psb("h5pu", [PP, 512], F32, ph5a)
            ps_t5 = psb("h5pt", [PP, C], BF16, ph5a)
            s_w5 = [bld.slot("h5w0"), bld.slot("h5w1")]
            s_st5 = bld.slot("h5st")
            for it in range(T // 128):
                bld.wait("sync", bld.vv)
                bld.wait("sync", bld.ev)
                bld.dma("sync", x5t[:], S["xn1"].ap()[it * 128:(it + 1) * 128, :])
                bld.wait("vector", bld.dm)
                BO("vector", "tensor_scalar", x5t[:], x5t[:], qs[:, 4:5], 32767.0,
                   ALU.mult, ALU.min)
                BO("vector", "tensor_scalar", x5t[:], x5t[:], -32768.0, MAGIC,
                   ALU.max, ALU.add)
                BO("vector", "tensor_scalar", q5b[:], x5t[:], MAGIC, qs[:, 5:6],
                   ALU.subtract, ALU.mult, sem=bld.vv)
                bld.wait("tensor", bld.vv)
                bld.wait("tensor", bld.ev)
                for cb in range(8):
                    BO("tensor", "transpose", ps_t5[:, cb * 128:(cb + 1) * 128],
                       q5b[:, cb * 128:(cb + 1) * 128], ident_b[:],
                       sem=bld.mm if cb == 7 else None)
                bld.wait("scalar", bld.mm)
                BO("scalar", "copy", trb5[:], ps_t5[:], sem=bld.ev)
                bld.wait("vector", bld.ev)
                for cb in range(8):
                    BO("vector", "tensor_copy", xqT5[cb][:, it * 128:(it + 1) * 128],
                       trb5[:, cb * 128:(cb + 1) * 128],
                       sem=bld.vv if cb == 7 else None)
            for ib in range(NB1):
                wsl = ib % 2
                bld.wait("sync", s_w5[wsl])
                bld.dma("sync", wg5[wsl][:],
                        S["wq_g1T"].ap()[ib].rearrange("(q p) m -> p q m", p=128))
                bld.dma("sync", wu5[wsl][:],
                        S["wq_u1T"].ap()[ib].rearrange("(q p) m -> p q m", p=128))
                for tt in range(T // 512):
                    bld.wait("tensor", bld.dm)
                    bld.wait("tensor", bld.vv)
                    bld.wait("tensor", bld.ev)
                    for cb in range(8):
                        BO("tensor", "matmul", ps_g5[:],
                           wg5[wsl][:, cb * 128:(cb + 1) * 128],
                           xqT5[cb][:, tt * 512:(tt + 1) * 512],
                           start=(cb == 0), stop=(cb == 7))
                    for cb in range(8):
                        BO("tensor", "matmul", ps_u5[:],
                           wu5[wsl][:, cb * 128:(cb + 1) * 128],
                           xqT5[cb][:, tt * 512:(tt + 1) * 512],
                           start=(cb == 0), stop=(cb == 7),
                           sem=bld.mm if cb == 7 else None)
                    if tt == T // 512 - 1:
                        bld.tick("tensor", s_w5[wsl])
                    bld.wait("vector", bld.mm)
                    bld.wait("vector", s_st5)
                    BO("vector", "tensor_reduce", st5[:, 0:1], ps_g5[:], AX.X,
                       ALU.max, apply_absolute_value=True)
                    BO("vector", "tensor_reduce", st5[:, 1:2], ps_u5[:], AX.X,
                       ALU.max, apply_absolute_value=True, sem=bld.vv)
                    bld.wait("vector", bld.vv)
                    BO("vector", "tensor_tensor", lmax[:, 4:5], lmax[:, 4:5],
                       st5[:, 0:1], ALU.max)
                    BO("vector", "tensor_tensor", lmax[:, 3:4], lmax[:, 3:4],
                       st5[:, 1:2], ALU.max, sem=bld.vv)
                    bld.wait("scalar", bld.vv)
                    BO("scalar", "copy", gb5[:], ps_g5[:])
                    BO("scalar", "copy", ub5[:], ps_u5[:], sem=bld.ev)
                    bld.tick("scalar", bld.ev)
                    bld.wait("sync", bld.ev)
                    bld.dma("sync", S["gater_bf"].ap()[ib * 128:(ib + 1) * 128,
                                                       tt * 512:(tt + 1) * 512],
                            gb5[:], sem=s_st5)
                    bld.dma("sync", S["upr_bf"].ap()[ib * 128:(ib + 1) * 128,
                                                     tt * 512:(tt + 1) * 512],
                            ub5[:], sem=s_st5)
            global_max(3, 3)
            derive_scale(3, 32767.0, None, 6, 7, floor=1e-12)
            global_max(4, 4)
            derive_scale(4, 32767.0, None, 8, 9, floor=1e-12)
            bld.barrier()
            ph5a.close()

        if phase_limit >= 10:
            ph5b = contextlib.ExitStack()
            gr5 = sb("i5g", [PP, 512], BF16, ph5b)
            gq5 = sb("i5gq", [PP, 512], F32, ph5b)
            sg5 = sb("i5s", [PP, 512], F32, ph5b)
            t25 = sb("i5t", [PP, 512], F32, ph5b)
            t2b5 = sb("i5tb", [PP, 512], BF16, ph5b)
            i5 = sb("i5i", [PP, 512], I32, ph5b)
            st5b = sb("i5st", [PP, 4], F32, ph5b)
            s_st5b = bld.slot("i5st")
            for tt in range(T // 512):
                for ib in range(NB1):
                    bld.wait("sync", bld.vv)
                    bld.dma("sync", gr5[:], S["gater_bf"].ap()[
                        ib * 128:(ib + 1) * 128, tt * 512:(tt + 1) * 512])
                    bld.wait("vector", bld.dm)
                    BO("vector", "tensor_scalar", gq5[:], gr5[:], qs[:, 8:9], 32767.0,
                       ALU.mult, ALU.min)
                    BO("vector", "tensor_scalar", gq5[:], gq5[:], -32768.0, MAGIC,
                       ALU.max, ALU.add)
                    BO("vector", "tensor_scalar", gq5[:], gq5[:], MAGIC,
                       qs[:, 9:10], ALU.subtract, ALU.mult, sem=bld.vv)
                    bld.wait("scalar", bld.vv)
                    BO("scalar", "activation", sg5[:], gq5[:], ACTF.Sigmoid,
                       sem=bld.ev)
                    bld.wait("vector", bld.ev)
                    BO("vector", "tensor_scalar", sg5[:], sg5[:], 65536.0, 65535.0,
                       ALU.mult, ALU.min)
                    BO("vector", "tensor_scalar", sg5[:], sg5[:], 0.0, MAGIC,
                       ALU.max, ALU.add)
                    BO("vector", "tensor_scalar", sg5[:], sg5[:], MAGIC,
                       1.0 / 65536.0, ALU.subtract, ALU.mult)
                    BO("vector", "tensor_tensor", t25[:], gq5[:], sg5[:], ALU.mult)
                    BO("vector", "tensor_reduce", st5b[:, 0:1], t25[:], AX.X,
                       ALU.max, apply_absolute_value=True, sem=bld.vv)
                    bld.wait("vector", bld.vv)
                    BO("vector", "tensor_tensor", lmax[:, 5:6], lmax[:, 5:6],
                       st5b[:, 0:1], ALU.max)
                    bld.wait("vector", s_st5b)
                    BO("vector", "tensor_copy", t2b5[:], t25[:], sem=bld.vv)
                    bld.wait("sync", bld.vv)
                    bld.dma("sync", S["t2_bf"].ap()[ib * 128:(ib + 1) * 128,
                                                    tt * 512:(tt + 1) * 512],
                            t2b5[:], sem=s_st5b)
            global_max(5, 5)
            derive_scale(5, 32767.0, None, 10, 11, floor=1e-12)
            bld.barrier()
            ph5b.close()

        if phase_limit >= 11:
            ph5c = contextlib.ExitStack()
            a5 = sb("j5a", [PP, 512], BF16, ph5c)
            b5 = sb("j5b", [PP, 512], BF16, ph5c)
            aq5 = sb("j5aq", [PP, 512], F32, ph5c)
            bq5 = sb("j5bq", [PP, 512], F32, ph5c)
            t35 = sb("j5t", [PP, 512], F32, ph5c)
            t3b5 = sb("j5tb", [PP, 512], BF16, ph5c)
            i5c = sb("j5i", [PP, 512], I32, ph5c)
            st5c = sb("j5st", [PP, 4], F32, ph5c)
            s_st5c = bld.slot("j5st")
            for tt in range(T // 512):
                for ib in range(NB1):
                    bld.wait("sync", bld.vv)
                    bld.dma("sync", a5[:], S["t2_bf"].ap()[
                        ib * 128:(ib + 1) * 128, tt * 512:(tt + 1) * 512])
                    bld.dma("sync", b5[:], S["upr_bf"].ap()[
                        ib * 128:(ib + 1) * 128, tt * 512:(tt + 1) * 512])
                    bld.wait("vector", bld.dm)
                    BO("vector", "tensor_scalar", aq5[:], a5[:], qs[:, 10:11],
                       32767.0, ALU.mult, ALU.min)
                    BO("vector", "tensor_scalar", aq5[:], aq5[:], -32768.0, MAGIC,
                       ALU.max, ALU.add)
                    BO("vector", "tensor_scalar", aq5[:], aq5[:], MAGIC,
                       qs[:, 11:12], ALU.subtract, ALU.mult)
                    BO("vector", "tensor_scalar", bq5[:], b5[:], qs[:, 6:7],
                       32767.0, ALU.mult, ALU.min)
                    BO("vector", "tensor_scalar", bq5[:], bq5[:], -32768.0, MAGIC,
                       ALU.max, ALU.add)
                    BO("vector", "tensor_scalar", bq5[:], bq5[:], MAGIC,
                       qs[:, 7:8], ALU.subtract, ALU.mult)
                    BO("vector", "tensor_tensor", t35[:], aq5[:], bq5[:], ALU.mult)
                    BO("vector", "tensor_reduce", st5c[:, 0:1], t35[:], AX.X,
                       ALU.max, apply_absolute_value=True, sem=bld.vv)
                    bld.wait("vector", bld.vv)
                    BO("vector", "tensor_tensor", lmax[:, 6:7], lmax[:, 6:7],
                       st5c[:, 0:1], ALU.max)
                    bld.wait("vector", s_st5c)
                    BO("vector", "tensor_copy", t3b5[:], t35[:], sem=bld.vv)
                    bld.wait("sync", bld.vv)
                    bld.dma("sync", S["t3_bf"].ap()[ib * 128:(ib + 1) * 128,
                                                    tt * 512:(tt + 1) * 512],
                            t3b5[:], sem=s_st5c)
            global_max(6, 6)
            derive_scale(6, 32767.0, None, 12, 13, floor=1e-12)
            bld.barrier()
            ph5c.close()

        if phase_limit >= 12:
            ph5d = contextlib.ExitStack()
            wd5 = [sb(f"k5wd{i}", [PP, 1024], BF16, ph5d) for i in range(NB1)]
            t3l = sb("k5t", [PP, NB1 * 512], BF16, ph5d)
            oq5 = sb("k5o", [PP, NB1 * 512], BF16, ph5d)
            of5 = sb("k5f", [PP, 512], F32, ph5d)
            oi5 = sb("k5i", [PP, 512], I32, ph5d)
            g5d = sb("k5g", [PP, C], F32, ph5d)
            x5d = sb("k5x", [PP, C], F32, ph5d)
            s5d = sb("k5s", [PP, C], F32, ph5d)
            ps_d5 = [psb(f"k5p{i}", [PP, 512], F32, ph5d) for i in range(2)]
            s_out = bld.slot("k5out")
            for k in range(NB1):
                bld.dma("sync", wd5[k][:],
                        S["wq_d1T"].ap()[k * 128:(k + 1) * 128, :])
            for tt in range(T // 512):
                bld.wait("sync", bld.vv)
                for k in range(NB1):
                    bld.dma("sync", t3l[:, k * 512:(k + 1) * 512],
                            S["t3_bf"].ap()[k * 128:(k + 1) * 128,
                                            tt * 512:(tt + 1) * 512])
                bld.wait("vector", bld.dm)
                for k in range(NB1):
                    BO("vector", "tensor_scalar", of5[:], t3l[:, k * 512:(k + 1) * 512],
                       qs[:, 12:13], 32767.0, ALU.mult, ALU.min)
                    BO("vector", "tensor_scalar", of5[:], of5[:], -32768.0, MAGIC,
                       ALU.max, ALU.add)
                    BO("vector", "tensor_scalar", oq5[:, k * 512:(k + 1) * 512],
                       of5[:], MAGIC, qs[:, 13:14], ALU.subtract, ALU.mult,
                       sem=bld.vv if k == NB1 - 1 else None)
                for t4 in range(4):
                    it = tt * 4 + t4
                    bld.wait("tensor", bld.vv)
                    bld.wait("tensor", bld.ev)
                    for oh in range(2):
                        for k in range(NB1):
                            BO("tensor", "matmul", ps_d5[oh][:],
                               oq5[:, k * 512 + t4 * 128:k * 512 + (t4 + 1) * 128],
                               wd5[k][:, oh * 512:(oh + 1) * 512],
                               start=(k == 0), stop=(k == NB1 - 1),
                               sem=bld.mm if (k == NB1 - 1 and oh == 1) else None)
                    bld.wait("scalar", bld.mm)
                    BO("scalar", "activation", g5d[:, 0:512], ps_d5[0][:], ACTF.Silu)
                    BO("scalar", "activation", g5d[:, 512:1024], ps_d5[1][:],
                       ACTF.Silu, sem=bld.ev)
                    bld.wait("sync", bld.ev)
                    bld.wait("sync", s_out)
                    bld.dma("sync", x5d[:], S["x1"].ap()[it * 128:(it + 1) * 128, :])
                    bld.dma("sync", s5d[:], S["s1"].ap()[it * 128:(it + 1) * 128, :])
                    bld.wait("vector", bld.dm)
                    bld.wait("vector", bld.ev)
                    BO("vector", "tensor_tensor", g5d[:], g5d[:], s5d[:], ALU.mult)
                    BO("vector", "tensor_tensor", x5d[:], x5d[:], g5d[:], ALU.add,
                       sem=bld.vv)
                    bld.tick("vector", bld.ev)
                    bld.wait("sync", bld.vv)
                    bld.dma("sync", out_t[it * 128:(it + 1) * 128, :], x5d[:],
                            sem=s_out)
            bld.wait("sync", s_out)
            bld.barrier()
            ph5d.close()



        bld.flush()

        if dbg:
            for nm, dv in dbg.items():
                bld.dma("sync", dv, S[nm].ap(), sem=bld.dmb)
            bld.wait("sync", bld.dmb)
            bld.flush()

        if phase_limit < 12:
            bld.dma("sync", out_t, S["xn"].ap(), sem=bld.dmb)
            bld.wait("sync", bld.dmb)
            bld.flush()

    return nc


def _host_layout(inputs):
    g = lambda k: np.ascontiguousarray(np.asarray(inputs[k], dtype=np.float32))
    W_in = g("L0_attn_W_in")
    com = {}
    com["w_inT_cb"] = np.ascontiguousarray(
        W_in.reshape(32, 128, C).transpose(0, 2, 1))
    def perpart(v, f=1):
        return np.ascontiguousarray(
            v.reshape(NK, 128, f).transpose(1, 0, 2).reshape(128, NK * f))
    com["conv_w"] = perpart(g("L0_attn_conv_w").reshape(DI, 4), 4)
    com["conv_b"] = perpart(g("L0_attn_conv_b"))
    com["w_xT"] = perpart(np.ascontiguousarray(g("L0_attn_W_x").T), 33)
    com["w_dt"] = perpart(g("L0_attn_W_dt"))
    com["b_dt"] = perpart(g("L0_attn_b_dt"))
    com["a_log"] = g("L0_attn_A_log").reshape(1, DS)
    com["d_param"] = perpart(g("L0_attn_D_param"))
    com["ln_g"] = perpart(g("L0_attn_ln_g"))
    com["ln_b"] = perpart(g("L0_attn_ln_b"))
    com["w_outT"] = np.ascontiguousarray(g("L0_attn_W_out").T)
    com["clip_x"] = g("L0_ffn_clip_x").reshape(1, 1)
    com["clip_down"] = g("L0_ffn_clip_down").reshape(1, 1)
    com["w_gate0"] = g("L0_ffn_w_gate")
    com["w_up0"] = g("L0_ffn_w_up")
    com["w_down0"] = g("L0_ffn_w_down")
    com["q_hT"] = np.ascontiguousarray(
        g("L1_attn_latent_queries").reshape(L, H, DH).transpose(2, 1, 0).reshape(
            DH, H * L))
    com["w_kT"] = np.ascontiguousarray(g("L1_attn_W_k").T)
    com["w_vT"] = np.ascontiguousarray(g("L1_attn_W_v").T)
    com["w_gT"] = np.ascontiguousarray(g("L1_attn_W_g").T)
    com["w_pT"] = np.ascontiguousarray(g("L1_attn_W_p").T)
    com["w_gate1"] = g("L1_ffn_w_gate")
    com["w_up1"] = g("L1_ffn_w_up")
    com["w_down1"] = g("L1_ffn_w_down")
    com["identf"] = np.eye(128, dtype=np.float32)
    inv = (1.0 / (10000.0 ** (np.arange(0, DH, 2, dtype=np.float32) / DH)))
    ang = np.arange(T, dtype=np.float32)[:, None] * inv[None, :]
    com["cos_t"] = np.cos(ang).astype(np.float32)
    com["sin_t"] = np.sin(ang).astype(np.float32)
    causal = (np.arange(CS)[None, :] <= np.arange(L)[:, None])
    com["cmask"] = np.where(causal, 0.0, MASKVAL).astype(np.float32)
    return com


def kernel(**inputs):
    com = _host_layout(inputs)
    x = np.ascontiguousarray(np.asarray(inputs["L0_attn_x"], dtype=np.float32))
    nc = build_program()
    in_maps = []
    for c in range(NCORES):
        m = dict(com)
        m["x_in"] = np.ascontiguousarray(x[c])
        in_maps.append(m)
    res = run_bass_kernel_spmd(nc, in_maps, core_ids=list(range(NCORES)))
    return np.stack([np.asarray(r["out"], dtype=np.float32) for r in res.results])


if __name__ == "__main__":
    build_program(phase_limit=1)
    print("built")


# version 07
# speedup vs baseline: 1.0986x; 1.0202x over previous
import contextlib
import numpy as np

import concourse.bass as bass
import concourse.mybir as mybir
from concourse.bass_utils import run_bass_kernel_spmd

F32 = mybir.dt.float32
BF16 = mybir.dt.bfloat16
I32 = mybir.dt.int32
ALU = mybir.AluOpType
ACTF = mybir.ActivationFunctionType
AX = mybir.AxisListType

NCORES = 8
B, T, C = 8, 4096, 1024
DI, DS = 2048, 16
I0, I1 = 2816, 3072
H, L, DH, CS = 16, 64, 64, 128
NB0, NB1 = I0 // 128, I1 // 128
LN_EPS = 1e-5
EXP_NEG20 = float(np.float32(np.exp(np.float32(-20.0))))
NK = DI // 128
TCH = 256
TS = 32
NSUB = TCH // TS
NCH2 = T // TCH
PP = 128
ENGINES = ["sync", "vector", "scalar", "tensor", "gpsimd"]
MASKVAL = -1e30
MAGIC = 12582912.0
SKIP_STAGES = set()


class Sem:
    def __init__(self, nc, name):
        self.h = nc.semaphore(name).__enter__()
        self.n = 0


class Builder:
    def __init__(self, nc):
        self.nc = nc
        self.items = []
        self._slots = {}
        for nm in ["dm", "dmb", "mm", "ev", "vv", "gg", "cc", "bar"]:
            setattr(self, nm, Sem(nc, "s_" + nm))

    def slot(self, name):
        if name not in self._slots:
            self._slots[name] = Sem(self.nc, "sl_" + name)
        return self._slots[name]

    def tick(self, eng, sem):
        sem.n += 1

        def fn(e, sem=sem):
            e.nop().then_inc(sem.h, 1)
        self.items.append((eng, fn))

    def raw(self, eng, fn):
        self.items.append((eng, fn))

    def wait(self, eng, sem, at=None):
        v = sem.n if at is None else at
        if v > 0:
            self.items.append((eng, lambda e: e.wait_ge(sem.h, v)))

    def op(self, eng, meth, *args, sem=None, inc=1, **kw):
        s = sem
        if s is not None:
            s.n += inc

        def fn(e):
            i = getattr(e, meth)(*args, **kw)
            if s is not None:
                i.then_inc(s.h, inc)
        self.items.append((eng, fn))

    def dma(self, eng, out, in_, sem=None, **kw):
        s = sem if sem is not None else self.dm
        s.n += 16

        def fn(e):
            e.dma_start(out=out, in_=in_, **kw).then_inc(s.h, 16)
        self.items.append((eng, fn))

    def flush(self):
        nc = self.nc
        items, self.items = self.items, []
        if not items:
            return
        per = {e: [fn for (en, fn) in items if en == e] for e in ENGINES}
        with nc.Block() as blk:
            for e in ENGINES:
                fns = per[e]
                if not fns:
                    continue

                def mk(fns):
                    def emit(eng):
                        for fn in fns:
                            fn(eng)
                    return emit
                getattr(blk, e)(mk(fns))

    def barrier(self):
        self.bar.n += len(ENGINES)
        tgt = self.bar.n
        for e in ENGINES:
            def fn(eng, tgt=tgt):
                eng.nop().then_inc(self.bar.h, 1)
                eng.wait_ge(self.bar.h, tgt)
            self.items.append((e, fn))
        self.flush()


def build_program(phase_limit=99, debug_scr=()):
    nc = bass.Bass("TRN2", target_bir_lowering=False, detect_race_conditions=False)
    D, S = {}, {}

    def din(name, shape):
        D[name] = nc.dram_tensor(name, list(shape), F32, kind="ExternalInput").ap()

    din("x_in", (T, C))
    din("w_inT_cb", (2 * DI // 128, C, 128))
    din("conv_w", (128, NK * 4))
    din("conv_b", (128, NK))
    din("w_xT", (128, NK * 33))
    din("w_dt", (128, NK))
    din("b_dt", (128, NK))
    din("a_log", (1, DS))
    din("d_param", (128, NK))
    din("ln_g", (128, NK))
    din("ln_b", (128, NK))
    din("w_outT", (DI, C))
    din("clip_x", (1, 1))
    din("clip_down", (1, 1))
    din("w_gate0", (I0, C))
    din("w_up0", (I0, C))
    din("w_down0", (C, I0))
    din("q_hT", (64, H * L))
    din("w_kT", (C, C))
    din("w_vT", (C, C))
    din("w_gT", (C, C))
    din("w_pT", (C, C))
    din("w_gate1", (I1, C))
    din("w_up1", (I1, C))
    din("w_down1", (C, I1))
    din("identf", (128, 128))
    din("cos_t", (T, DH // 2))
    din("sin_t", (T, DH // 2))
    din("cmask", (L, CS))

    out_t = nc.dram_tensor("out", [T, C], F32, kind="ExternalOutput").ap()
    dbg = {}
    for nm in debug_scr:
        pass

    def scr(name, shape, dtype=F32):
        S[name] = nc.dram_tensor("scr_" + name, list(shape), dtype)

    scr("xn", (T, C))
    scr("xnT_bf", (C, T), BF16)
    scr("xc", (DI, T))
    scr("z", (DI, T))
    scr("prod_bf", (DI, T), BF16)
    scr("s0", (T, C))
    scr("gu_bf", (I0, T), BF16)
    scr("x1", (T, C))
    scr("xn1", (T, C))
    scr("xn1T_bf", (C, T), BF16)
    scr("gatesT_bf", (C, T), BF16)
    scr("s1", (T, C))
    scr("upr_bf", (I1, T), BF16)
    scr("gater_bf", (I1, T), BF16)
    scr("t2_bf", (I1, T), BF16)
    scr("t3_bf", (I1, T), BF16)
    scr("y_dbg", (DI, T))
    scr("xq_dbg", (T, C))
    scr("qs_dbg", (128, 32))
    scr("cc_dbg", (4, 128))
    scr("lmax_ph0c", (128, 32))
    scr("g0_dbg", (T, C))
    scr("qs2_dbg", (128, 32))
    scr("w0_dbg", (128, 2048))
    scr("winT", (2 * DI // 128, C, 128), BF16)
    scr("woutT", (DI, C), BF16)
    scr("wkT", (C, C), BF16)
    scr("wvT", (C, C), BF16)
    scr("wgT", (C, C), BF16)
    scr("wpT", (C, C), BF16)
    scr("wq_g0T", (NB0, C, 128), BF16)
    scr("wq_u0T", (NB0, C, 128), BF16)
    scr("wq_d0T", (I0, C), BF16)
    scr("wq_g1T", (NB1, C, 128), BF16)
    scr("wq_u1T", (NB1, C, 128), BF16)
    scr("wq_d1T", (I1, C), BF16)
    NCC = 7
    scr("cc_in", (NCC, 128))
    scr("cc_out", (NCC, 128))

    for nm in debug_scr:
        shp = list(S[nm].shape) if nm in S else None
        dt_ = S[nm].dtype if nm in S else F32
        dbg[nm] = nc.dram_tensor("dbg_" + nm, shp, dt_, kind="ExternalOutput").ap()

    bld = Builder(nc)
    es = contextlib.ExitStack()
    with es:
        def sb(name, shape, dtype=F32, st=None):
            return (st or es).enter_context(nc.sbuf_tensor(name, list(shape), dtype))

        def psb(name, shape, dtype=F32, st=None):
            return (st or es).enter_context(nc.psum_tensor(name, list(shape), dtype))

        ones_f = sb("ones_f", [PP, 1])
        ident_b = sb("ident_b", [PP, 128], BF16)
        conv_w_t = sb("conv_w_t", [PP, NK * 4])
        conv_b_t = sb("conv_b_t", [PP, NK])
        w_dt_t = sb("w_dt_t", [PP, NK])
        b_dt_t = sb("b_dt_t", [PP, NK])
        d_par_t = sb("d_par_t", [PP, NK])
        ln_g_t = sb("ln_g_t", [PP, NK])
        ln_b_t = sb("ln_b_t", [PP, NK])
        a_cols = sb("a_cols", [PP, DS])
        w_xT_b = sb("w_xT_b", [PP, NK * 33], BF16)
        qh_b = sb("qh_b", [64, H * L], BF16)
        cmask_t = sb("cmask_t", [L, CS])
        gmax = sb("gmax", [PP, 8])
        lmax = sb("lmax", [PP, 8])
        qs = sb("qs", [PP, 16])
        row128 = sb("row128", [1, 136])
        ps_util = psb("ps_util", [PP, 512])

        def BO(e, meth, *a, **kw):
            bld.op(e, meth, *a, **kw)

        def global_max(slot, cc_id):
            cin, cout = S["cc_in"].ap(), S["cc_out"].ap()
            bld.wait("gpsimd", bld.vv)
            bld.dma("gpsimd", cin[cc_id, :].unsqueeze(0), lmax[:, slot:slot + 1],
                    sem=bld.gg, allow_slow_non_contiguous=True)
            bld.wait("gpsimd", bld.gg)
            bld.op("gpsimd", "collective_compute", "AllReduce", ALU.max,
                   replica_groups=[list(range(NCORES))],
                   ins=[cin[cc_id:cc_id + 1, :]],
                   outs=[cout[cc_id:cc_id + 1, :]], sem=bld.cc)
            bld.wait("gpsimd", bld.cc)
            bld.dma("gpsimd", row128[0:1, 0:128], cout[cc_id:cc_id + 1, :],
                    sem=bld.gg)
            bld.wait("vector", bld.gg)
            BO("vector", "tensor_reduce", row128[0:1, 128:129], row128[0:1, 0:128],
               AX.X, ALU.max, sem=bld.vv)
            bld.wait("tensor", bld.vv)
            BO("tensor", "matmul", ps_util[:, 0:1],
               ones_f[0:1, 0:1].broadcast_to([1, 128]), row128[0:1, 128:129],
               start=True, stop=True, sem=bld.mm)
            bld.wait("scalar", bld.mm)
            BO("scalar", "copy", gmax[:, slot:slot + 1], ps_util[:, 0:1], sem=bld.ev)

        def derive_scale(slot, qmaxv, sig_col, inv_col, s_col, floor=None):
            bld.wait("vector", bld.ev)
            if sig_col is not None:
                BO("vector", "tensor_scalar", qs[:, s_col:s_col + 1],
                   gmax[:, slot:slot + 1], qs[:, sig_col:sig_col + 1], 1.0 / qmaxv,
                   ALU.mult, ALU.mult, sem=bld.vv)
            else:
                BO("vector", "tensor_scalar", qs[:, s_col:s_col + 1],
                   gmax[:, slot:slot + 1], 1.0 / qmaxv, None, ALU.mult, sem=bld.vv)
            bld.wait("vector", bld.vv)
            if floor is not None:
                BO("vector", "tensor_scalar", qs[:, s_col:s_col + 1],
                   qs[:, s_col:s_col + 1], floor, None, ALU.max, sem=bld.vv)
                bld.wait("vector", bld.vv)
            BO("vector", "reciprocal", qs[:, inv_col:inv_col + 1],
               qs[:, s_col:s_col + 1], sem=bld.vv)

        def emit_quant(dst_bf, src, inv_col, s_col, qhi, qlo, itmp, ftmp):
            BO("vector", "tensor_scalar", ftmp, src, qs[:, inv_col:inv_col + 1],
               float(qhi), ALU.mult, ALU.min)
            BO("vector", "tensor_scalar", itmp, ftmp, float(qlo), None, ALU.max)
            BO("vector", "tensor_scalar", dst_bf, itmp, qs[:, s_col:s_col + 1],
               None, ALU.mult)

        def emit_ln_tile(xin, xout_f32, xout_bf, stat, dump, F, absmax_slot=None):
            bld.wait("scalar", bld.vv)
            BO("scalar", "activation", xout_f32, xin, ACTF.Copy,
               accum_out=stat[:, 0:1], sem=bld.ev)
            bld.wait("vector", bld.ev)
            BO("vector", "tensor_scalar", stat[:, 0:1], stat[:, 0:1], 1.0 / F, None,
               ALU.mult, sem=bld.vv)
            bld.wait("vector", bld.vv)
            BO("vector", "tensor_scalar", xout_f32, xout_f32, stat[:, 0:1], None,
               ALU.subtract, sem=bld.vv)
            bld.wait("scalar", bld.vv)
            BO("scalar", "activation", dump, xout_f32, ACTF.Square,
               accum_out=stat[:, 1:2], sem=bld.ev)
            bld.wait("vector", bld.ev)
            BO("vector", "tensor_scalar", stat[:, 1:2], stat[:, 1:2], 1.0 / F, LN_EPS,
               ALU.mult, ALU.add, sem=bld.vv)
            bld.wait("scalar", bld.vv)
            BO("scalar", "activation", stat[:, 2:3], stat[:, 1:2], ACTF.Sqrt, sem=bld.ev)
            bld.wait("vector", bld.ev)
            BO("vector", "reciprocal", stat[:, 3:4], stat[:, 2:3], sem=bld.vv)
            bld.wait("vector", bld.vv)
            BO("vector", "tensor_scalar", xout_f32, xout_f32, stat[:, 3:4], None,
               ALU.mult, sem=bld.vv)
            if xout_bf is not None:
                BO("vector", "tensor_copy", xout_bf, xout_f32, sem=bld.vv)
            if absmax_slot is not None:
                BO("vector", "tensor_reduce", stat[:, 5:6], xout_f32, AX.X, ALU.max,
                   apply_absolute_value=True, sem=bld.vv)
                bld.wait("vector", bld.vv)
                BO("vector", "tensor_tensor", lmax[:, absmax_slot:absmax_slot + 1],
                   lmax[:, absmax_slot:absmax_slot + 1], stat[:, 5:6], ALU.max,
                   sem=bld.vv)

        ph0 = contextlib.ExitStack()
        idf = sb("idf", [PP, 128], F32, ph0)
        arow = sb("arow", [1, DS], F32, ph0)
        crow = sb("crow", [1, 2], F32, ph0)
        wx_st = sb("wx_st", [PP, NK * 33], F32, ph0)
        qh_st = sb("qh_st", [64, H * L], F32, ph0)

        bld.dma("sync", conv_w_t[:], D["conv_w"])
        bld.dma("sync", conv_b_t[:], D["conv_b"])
        bld.dma("sync", w_dt_t[:], D["w_dt"])
        bld.dma("sync", b_dt_t[:], D["b_dt"])
        bld.dma("sync", d_par_t[:], D["d_param"])
        bld.dma("sync", ln_g_t[:], D["ln_g"])
        bld.dma("sync", ln_b_t[:], D["ln_b"])
        bld.dma("sync", wx_st[:], D["w_xT"])
        bld.dma("sync", qh_st[:], D["q_hT"])
        bld.dma("sync", cmask_t[:], D["cmask"])
        bld.dma("sync", idf[:], D["identf"])
        bld.dma("sync", arow[:], D["a_log"])
        bld.dma("sync", crow[0:1, 0:1], D["clip_x"])
        bld.dma("sync", crow[0:1, 1:2], D["clip_down"])
        bld.wait("vector", bld.dm)
        BO("vector", "memset", ones_f[:], 1.0)
        BO("vector", "memset", lmax[:], 0.0)
        BO("vector", "tensor_scalar", arow[:], arow[:], 5.0, None, ALU.min, sem=bld.vv)
        bld.wait("scalar", bld.vv)
        bld.wait("scalar", bld.dm)
        BO("scalar", "copy", ident_b[:], idf[:])
        BO("scalar", "copy", w_xT_b[:], wx_st[:])
        BO("scalar", "copy", qh_b[:], qh_st[:])
        BO("scalar", "activation", arow[:], arow[:], ACTF.Exp)
        BO("scalar", "activation", crow[:], crow[:], ACTF.Sigmoid, sem=bld.ev)
        bld.wait("vector", bld.ev)
        BO("vector", "tensor_scalar", arow[:], arow[:], -1.0, None, ALU.mult, sem=bld.vv)
        bld.wait("tensor", bld.vv)
        BO("tensor", "matmul", ps_util[:, 0:DS],
           ones_f[0:1, 0:1].broadcast_to([1, 128]), arow[0:1, :],
           start=True, stop=True)
        BO("tensor", "matmul", ps_util[:, 32:34],
           ones_f[0:1, 0:1].broadcast_to([1, 128]), crow[0:1, :],
           start=True, stop=True, sem=bld.mm)
        bld.wait("scalar", bld.mm)
        BO("scalar", "copy", a_cols[:], ps_util[:, 0:DS])
        BO("scalar", "copy", qs[:, 14:16], ps_util[:, 32:34], sem=bld.ev)
        bld.barrier()
        ph0.close()

        ph0b = contextlib.ExitStack()
        wst = [sb(f"wst{i}", [PP, 512], F32, ph0b) for i in range(2)]
        wbt = [sb(f"wbt{i}", [PP, 512], BF16, ph0b) for i in range(2)]
        s_wc = [bld.slot("wc0"), bld.slot("wc1")]
        step = 0

        def cast_stream(pairs):
            nonlocal step
            for (srcv, dstv) in pairs:
                sl = step % 2
                bld.wait("sync", s_wc[sl])
                bld.dma("sync", wst[sl][:, 0:512], srcv)
                bld.wait("scalar", bld.dm)
                BO("scalar", "copy", wbt[sl][:, 0:512], wst[sl][:, 0:512], sem=bld.ev)
                bld.wait("sync", bld.ev)
                bld.dma("sync", dstv, wbt[sl][:, 0:512], sem=s_wc[sl])
                step += 1

        pairs = []
        for eb in range(2 * DI // 128):
            for q in range(2):
                pairs.append((
                    D["w_inT_cb"][eb, q * 512:(q + 1) * 512, :].rearrange(
                        "(a p) m -> p a m", p=128),
                    S["winT"].ap()[eb, q * 512:(q + 1) * 512, :].rearrange(
                        "(a p) m -> p a m", p=128)))
        for (src2, dst2, K2, M2) in [("w_outT", "woutT", DI, C), ("w_kT", "wkT", C, C),
                                     ("w_vT", "wvT", C, C), ("w_gT", "wgT", C, C),
                                     ("w_pT", "wpT", C, C)]:
            for kt in range(K2 // 128):
                for mh in range(M2 // 512):
                    pairs.append((
                        D[src2][kt * 128:(kt + 1) * 128, mh * 512:(mh + 1) * 512],
                        S[dst2].ap()[kt * 128:(kt + 1) * 128, mh * 512:(mh + 1) * 512]))
        if phase_limit >= 2 and "ph0b" not in SKIP_STAGES:
            cast_stream(pairs)
        bld.barrier()
        ph0b.close()

        ph0c = contextlib.ExitStack()
        wq_st = sb("wq_st", [PP, I1], F32, ph0c)
        wq_i = sb("wq_i", [PP, I1], I32, ph0c)
        wq_b = sb("wq_b", [PP, I1], BF16, ph0c)
        wq_m = sb("wq_m", [PP, 96], F32, ph0c)
        wq_inv = sb("wq_inv", [PP, 96], F32, ph0c)
        tr_o = sb("tr_o", [PP, 512], BF16, ph0c)
        ps_tr = psb("ps_tr", [PP, 512], BF16, ph0c)
        s_q_store = bld.slot("q_store")

        def quant_weight(src2, dst2, rows, K2, lpbq, dst_cb):
            qmaxv = 7.0 if lpbq else 127.0
            qlov = -8.0 if lpbq else -128.0
            for rt in range(rows // 128):
                bld.wait("sync", bld.mm)
                bld.wait("sync", bld.vv)
                bld.dma("sync", wq_st[:, 0:K2], D[src2][rt * 128:(rt + 1) * 128, :])
                bld.wait("vector", bld.dm)
                if lpbq:
                    nb = K2 // 32
                    wv3 = wq_st[:, 0:K2].rearrange("p (b i) -> p b i", i=32)
                    if "qv_red" not in SKIP_STAGES:
                        BO("vector", "tensor_reduce", wq_m[:, 0:nb], wv3, AX.X,
                           ALU.max, apply_absolute_value=True, sem=bld.vv)
                        bld.wait("vector", bld.vv)
                        BO("vector", "tensor_scalar", wq_m[:, 0:nb], wq_m[:, 0:nb],
                           1.0 / qmaxv, 1e-12, ALU.mult, ALU.max, sem=bld.vv)
                        bld.wait("vector", bld.vv)
                        BO("vector", "reciprocal", wq_inv[:, 0:nb], wq_m[:, 0:nb],
                           sem=bld.vv)
                        bld.wait("vector", bld.vv)
                    iexp = wq_inv[:, 0:nb].unsqueeze(2).broadcast_to([PP, nb, 32])
                    mexp = wq_m[:, 0:nb].unsqueeze(2).broadcast_to([PP, nb, 32])
                    if "qv_tt1" not in SKIP_STAGES:
                        BO("vector", "tensor_tensor", wv3, wv3, iexp, ALU.mult)
                    if "qv_clip" not in SKIP_STAGES:
                        BO("vector", "tensor_scalar", wq_st[:, 0:K2], wq_st[:, 0:K2],
                           qmaxv, qlov, ALU.min, ALU.max)
                        BO("vector", "tensor_scalar", wq_st[:, 0:K2], wq_st[:, 0:K2],
                           MAGIC, MAGIC, ALU.add, ALU.subtract)
                    if "qv_tt2" not in SKIP_STAGES:
                        BO("vector", "tensor_tensor",
                           wq_b[:, 0:K2].rearrange("p (b i) -> p b i", i=32),
                           wv3, mexp, ALU.mult, sem=bld.vv)
                    else:
                        BO("vector", "tensor_copy", wq_b[:, 0:K2], wq_st[:, 0:K2],
                           sem=bld.vv)
                else:
                    BO("vector", "tensor_reduce", wq_m[:, 0:1], wq_st[:, 0:K2],
                       AX.X, ALU.max, apply_absolute_value=True, sem=bld.vv)
                    bld.wait("vector", bld.vv)
                    BO("vector", "tensor_scalar", wq_m[:, 0:1], wq_m[:, 0:1],
                       1.0 / qmaxv, None, ALU.mult, sem=bld.vv)
                    bld.wait("vector", bld.vv)
                    BO("vector", "reciprocal", wq_inv[:, 0:1], wq_m[:, 0:1],
                       sem=bld.vv)
                    bld.wait("vector", bld.vv)
                    BO("vector", "tensor_scalar", wq_st[:, 0:K2], wq_st[:, 0:K2],
                       wq_inv[:, 0:1], qmaxv, ALU.mult, ALU.min)
                    BO("vector", "tensor_scalar", wq_st[:, 0:K2], wq_st[:, 0:K2],
                       qlov, MAGIC, ALU.max, ALU.add)
                    BO("vector", "tensor_scalar", wq_st[:, 0:K2], wq_st[:, 0:K2],
                       MAGIC, None, ALU.subtract)
                    BO("vector", "tensor_scalar", wq_b[:, 0:K2], wq_st[:, 0:K2],
                       wq_m[:, 0:1], None, ALU.mult, sem=bld.vv)
                for kt in range(K2 // 128):
                    q = kt % 4
                    bld.wait("tensor", bld.vv)
                    bld.wait("tensor", bld.ev)
                    BO("tensor", "transpose", ps_tr[:, q * 128:(q + 1) * 128],
                       wq_b[:, kt * 128:(kt + 1) * 128], ident_b[:], sem=bld.mm)
                    bld.wait("scalar", bld.mm)
                    bld.wait("scalar", s_q_store)
                    BO("scalar", "copy", tr_o[:, q * 128:(q + 1) * 128],
                       ps_tr[:, q * 128:(q + 1) * 128], sem=bld.ev)
                    bld.wait("sync", bld.ev)
                    if dst_cb:
                        dview = S[dst2].ap()[rt, kt * 128:(kt + 1) * 128, :]
                    else:
                        dview = S[dst2].ap()[kt * 128:(kt + 1) * 128,
                                             rt * 128:(rt + 1) * 128]
                    bld.dma("sync", dview, tr_o[:, q * 128:(q + 1) * 128],
                            sem=s_q_store)

        if phase_limit >= 2 and "ph0c" not in SKIP_STAGES:
            qw_calls = [("w_gate0", "wq_g0T", I0, C, False, True),
                        ("w_up0", "wq_u0T", I0, C, False, True),
                        ("w_down0", "wq_d0T", C, I0, False, False),
                        ("w_gate1", "wq_g1T", I1, C, True, True),
                        ("w_up1", "wq_u1T", I1, C, True, True),
                        ("w_down1", "wq_d1T", C, I1, True, False)]
            for qi, qargs in enumerate(qw_calls):
                if f"qw{qi}" not in SKIP_STAGES:
                    quant_weight(*qargs)
        bld.barrier()
        if "lmax_ph0c" in dbg:
            sd = S["lmax_ph0c"].ap()
            bld.dma("sync", sd[:, 0:8], gmax[:, 0:8])
            bld.dma("sync", sd[:, 8:16], lmax[:, 0:8])
            bld.dma("sync", sd[:, 16:32], qs[:, 0:16], sem=bld.dm)
            bld.wait("sync", bld.dm)
        ph0c.close()

        if phase_limit >= 1:
            ph1 = contextlib.ExitStack()
            xt = sb("p1x", [PP, C], F32, ph1)
            xnt = sb("p1n", [PP, C], F32, ph1)
            xnb = sb("p1b", [PP, C], BF16, ph1)
            stat = sb("p1s", [PP, 8], F32, ph1)
            trb = sb("p1t", [PP, C], BF16, ph1)
            dmp = sb("p1d", [PP, C], F32, ph1)
            ps_t1 = psb("p1p", [PP, C], BF16, ph1)
            s_p1 = bld.slot("p1st")
            for it in range(T // 128):
                bld.wait("sync", s_p1)
                bld.wait("sync", bld.vv)
                bld.wait("sync", bld.ev)
                bld.dma("sync", xt[:], D["x_in"][it * 128:(it + 1) * 128, :])
                bld.wait("scalar", bld.dm)
                bld.wait("vector", bld.dm)
                emit_ln_tile(xt[:], xnt[:], xnb[:], stat, dmp[:], C, absmax_slot=0)
                bld.wait("sync", bld.vv)
                bld.dma("sync", S["xn"].ap()[it * 128:(it + 1) * 128, :], xnt[:],
                        sem=s_p1)
                bld.wait("tensor", bld.vv)
                bld.wait("tensor", bld.ev)
                for cb in range(8):
                    BO("tensor", "transpose", ps_t1[:, cb * 128:(cb + 1) * 128],
                       xnb[:, cb * 128:(cb + 1) * 128], ident_b[:],
                       sem=bld.mm if cb == 7 else None)
                bld.wait("scalar", bld.mm)
                BO("scalar", "copy", trb[:], ps_t1[:], sem=bld.ev)
                bld.wait("sync", bld.ev)
                for cb in range(8):
                    bld.dma("sync", S["xnT_bf"].ap()[cb * 128:(cb + 1) * 128,
                                                     it * 128:(it + 1) * 128],
                            trb[:, cb * 128:(cb + 1) * 128], sem=s_p1)
            global_max(0, 0)
            derive_scale(0, 127.0, 14, 0, 1)
            if "qs_dbg" in dbg:
                bld.wait("sync", bld.vv)
                bld.wait("sync", bld.ev)
                bld.dma("sync", S["qs_dbg"].ap()[:, 0:16], qs[:, 0:16])
                bld.dma("sync", S["qs_dbg"].ap()[:, 16:24], gmax[:, 0:8])
                bld.dma("sync", S["qs_dbg"].ap()[:, 24:32], lmax[:, 0:8])
            if "cc_dbg" in dbg:
                bld.wait("sync", bld.vv)
                bld.dma("sync", S["cc_dbg"].ap()[0:1, :], S["cc_in"].ap()[0:1, :])
                bld.dma("sync", S["cc_dbg"].ap()[1:2, :], S["cc_out"].ap()[0:1, :])
                bld.dma("sync", S["cc_dbg"].ap()[2:3, :], row128[0:1, 0:128])
            bld.barrier()
            ph1.close()


        if phase_limit >= 2 and "ph2a" not in SKIP_STAGES:
            ph2a = contextlib.ExitStack()
            wbuf = [sb(f"a2w{i}", [PP, 8 * 128], BF16, ph2a) for i in range(2)]
            xbuf = [sb(f"a2x{i}", [PP, 8 * 512], BF16, ph2a) for i in range(2)]
            est = [sb(f"a2e{i}", [PP, 512], F32, ph2a) for i in range(2)]
            ps2 = [psb(f"a2p{i}", [PP, 512], F32, ph2a) for i in range(2)]
            s_wb = [bld.slot("a2wb0"), bld.slot("a2wb1")]
            s_xb = [bld.slot("a2xb0"), bld.slot("a2xb1")]
            s_es = [bld.slot("a2es0"), bld.slot("a2es1")]
            s_pb = [bld.slot("a2pb0"), bld.slot("a2pb1")]
            for tt in range(T // 512):
                xsl = tt % 2
                bld.wait("sync", s_xb[xsl])
                bld.dma("sync", xbuf[xsl][:],
                        S["xnT_bf"].ap()[:, tt * 512:(tt + 1) * 512].rearrange(
                            "(q p) m -> p q m", p=128))
                for eb in range(2 * DI // 128):
                    wsl = eb % 2
                    psl = eb % 2
                    bld.wait("sync", s_wb[wsl])
                    bld.dma("sync", wbuf[wsl][:],
                            S["winT"].ap()[eb].rearrange("(q p) m -> p q m", p=128))
                    bld.wait("tensor", bld.dm)
                    bld.wait("tensor", s_pb[psl])
                    for cb in range(8):
                        BO("tensor", "matmul", ps2[psl][:],
                           wbuf[wsl][:, cb * 128:(cb + 1) * 128],
                           xbuf[xsl][:, cb * 512:(cb + 1) * 512],
                           start=(cb == 0), stop=(cb == 7),
                           sem=bld.mm if cb == 7 else None)
                    bld.tick("tensor", s_wb[wsl])
                    if eb == 2 * DI // 128 - 1:
                        bld.tick("tensor", s_xb[xsl])
                    esl = eb % 2
                    bld.wait("scalar", bld.mm)
                    bld.wait("scalar", s_es[esl])
                    BO("scalar", "copy", est[esl][:], ps2[psl][:], sem=bld.ev)
                    bld.tick("scalar", s_pb[psl])
                    bld.wait("sync", bld.ev)
                    dstb = S["xc"] if eb < NK else S["z"]
                    er = eb if eb < NK else eb - NK
                    bld.dma("sync", dstb.ap()[er * 128:(er + 1) * 128,
                                              tt * 512:(tt + 1) * 512],
                            est[esl][:], sem=s_es[esl])
            bld.barrier()
            ph2a.close()


        if phase_limit >= 3:
            ph2b = contextlib.ExitStack()
            xc_t = sb("xc_t", [PP, NK * (TCH + 3)], F32, ph2b)
            u_t = sb("u_t", [PP, NK * TCH], F32, ph2b)
            ub_t = sb("ub_t", [PP, NK * TCH], BF16, ph2b)
            zs_t = sb("zs_t", [PP, NK * TCH], F32, ph2b)
            dt_t = sb("dt_t", [PP, NK * TCH], F32, ph2b)
            xdT = sb("xdT", [33, TCH], F32, ph2b)
            xdf0 = sb("xdf0", [1, TCH], F32, ph2b)
            xdfs = sb("xdfs", [1, 33 * TS], F32, ph2b)
            dA2 = [sb(f"dA{i}", [PP, TS * 256], F32, ph2b) for i in range(2)]
            dBu = sb("dBu", [PP, 16 * 256], F32, ph2b)
            hist = sb("hist", [PP, 17 * 256], F32, ph2b)
            hcs = sb("hcs", [PP, 8 * 256], F32, ph2b)
            brep = sb("brep", [PP, DS * TS], F32, ph2b)
            crep = sb("crep", [PP, DS * TS], F32, ph2b)
            dus = sb("dus", [PP, NK * TS], F32, ph2b)
            ysub = sb("ysub", [PP, TS * NK], F32, ph2b)
            ysq = sb("ysq", [PP, TS * NK], F32, ph2b)
            yrow = sb("yrow", [1, 4 * TS], F32, ph2b)
            prodb = sb("prodb", [PP, NK * TCH], BF16, ph2b)
            cm100 = sb("cm100", [PP, 4], F32, ph2b)
            ps_xd = psb("ps_xd", [33, TCH], F32, ph2b)
            ps_dr = psb("ps_dr", [PP, TCH], F32, ph2b)
            ps_bc2 = psb("ps_bc2", [PP, DS * TS], F32, ph2b)
            ps_row = psb("ps_row", [1, 2 * TS], F32, ph2b)
            ps_mu = psb("ps_mu", [PP, 2 * TS], F32, ph2b)
            hist_last = hist[:, 16 * 256:17 * 256]

            s_gd = bld.slot("b2_dbu")
            s_vs = bld.slot("b2_stepsdone")
            s_chv = bld.slot("b2_chv")
            s_chs = bld.slot("b2_chs")
            s_prod = bld.slot("b2_prod")

            BO("vector", "memset", hist[:], 0.0)
            BO("vector", "memset", cm100[:, 0:1], -100.0)
            BO("vector", "memset", cm100[:, 1:2], 100.0)
            BO("vector", "memset", cm100[:, 2:3], EXP_NEG20, sem=bld.vv)
            cm100b = cm100[:, 0:1].broadcast_to([PP, 256])
            c100b = cm100[:, 1:2].broadcast_to([PP, 256])
            e20b = cm100[:, 2:3].broadcast_to([PP, TS * 256])

            xcv = lambda k: xc_t[:, k * (TCH + 3):(k + 1) * (TCH + 3)]
            uv = lambda k: u_t[:, k * TCH:(k + 1) * TCH]
            ubv = lambda k: ub_t[:, k * TCH:(k + 1) * TCH]
            zv = lambda k: zs_t[:, k * TCH:(k + 1) * TCH]
            dtv = lambda k: dt_t[:, k * TCH:(k + 1) * TCH]
            dt3 = dt_t[:].rearrange("p (c t) -> p t c", c=NK)
            u3 = u_t[:].rearrange("p (c t) -> p t c", c=NK)
            ysub3 = ysub[:].rearrange("p (t c) -> p t c", t=TS)
            ysq3 = ysq[:].rearrange("p (t c) -> p t c", t=TS)

            for ch in range(NCH2):
                t0 = ch * TCH
                bld.wait("sync", s_chv)
                bld.wait("sync", s_chs)
                for k in range(NK):
                    if ch == 0:
                        bld.dma("sync", xcv(k)[:, 3:],
                                S["xc"].ap()[k * 128:(k + 1) * 128, t0:t0 + TCH])
                    else:
                        bld.dma("sync", xcv(k)[:, :],
                                S["xc"].ap()[k * 128:(k + 1) * 128, t0 - 3:t0 + TCH])
                    bld.dma("sync", zv(k)[:, :],
                            S["z"].ap()[k * 128:(k + 1) * 128, t0:t0 + TCH])
                bld.wait("vector", bld.dm)
                if ch == 0:
                    for k in range(NK):
                        BO("vector", "memset", xcv(k)[:, 0:3], 0.0)
                for k in range(NK):
                    w = lambda j: conv_w_t[:, k * 4 + j:k * 4 + j + 1]
                    BO("vector", "tensor_scalar", uv(k), xcv(k)[:, 0:TCH], w(0), None,
                       ALU.mult)
                    for j in range(1, 4):
                        BO("vector", "scalar_tensor_tensor", uv(k), xcv(k)[:, j:j + TCH],
                           w(j), uv(k), ALU.mult, ALU.add,
                           sem=bld.vv if j == 3 else None)
                bld.wait("scalar", bld.vv)
                bld.wait("scalar", bld.dm)
                for k in range(NK):
                    BO("scalar", "activation", uv(k), uv(k), ACTF.Silu,
                       bias=conv_b_t[:, k:k + 1], scale=1.0)
                    BO("scalar", "copy", ubv(k), uv(k))
                    BO("scalar", "activation", zv(k), zv(k), ACTF.Silu,
                       sem=bld.ev if k == NK - 1 else None)
                bld.wait("tensor", bld.ev)
                for k in range(NK):
                    BO("tensor", "matmul", ps_xd[:, :],
                       w_xT_b[:, k * 33:(k + 1) * 33], ubv(k),
                       start=(k == 0), stop=(k == NK - 1),
                       sem=bld.mm if k == NK - 1 else None)
                bld.wait("scalar", bld.mm)
                BO("scalar", "copy", xdT[:, :], ps_xd[:, :], sem=bld.ev)
                bld.wait("sync", bld.ev)
                bld.dma("sync", xdf0[:], xdT[0:1, :])
                bld.wait("tensor", bld.dm)
                BO("tensor", "matmul", ps_dr[:, :],
                   ones_f[0:1, 0:1].broadcast_to([1, 128]), xdf0[0:1, :],
                   start=True, stop=True, sem=bld.mm)
                bld.wait("scalar", bld.mm)
                for k in range(NK):
                    BO("scalar", "activation", dtv(k), ps_dr[:, :], ACTF.Exp,
                       bias=b_dt_t[:, k:k + 1], scale=w_dt_t[:, k:k + 1])
                    BO("scalar", "activation", dtv(k), dtv(k), ACTF.Ln, bias=1.0,
                       sem=bld.ev if k == NK - 1 else None)
                bld.wait("vector", bld.ev)
                for k in range(NK):
                    BO("vector", "tensor_scalar", dtv(k), dtv(k), 1e-4, 10.0,
                       ALU.max, ALU.min, sem=bld.vv if k == NK - 1 else None)

                for s in range(NSUB):
                    ts0 = s * TS
                    gs = ch * NSUB + s
                    bld.wait("sync", bld.mm)
                    bld.dma("sync", xdfs[0:1, :], xdT[:, ts0:ts0 + TS],
                            allow_slow_non_contiguous=True)
                    bld.wait("tensor", bld.dm)
                    bld.wait("tensor", bld.ev)
                    for n in range(DS):
                        BO("tensor", "matmul", ps_bc2[:, n * TS:(n + 1) * TS],
                           ones_f[0:1, 0:1].broadcast_to([1, 128]),
                           xdfs[0:1, (1 + n) * TS:(2 + n) * TS],
                           start=True, stop=True,
                           sem=bld.mm if n == DS - 1 else None)
                    bld.wait("scalar", bld.mm)
                    bld.wait("scalar", s_gd)
                    if s == 0:
                        bld.wait("scalar", bld.vv)
                    BO("scalar", "copy", brep[:], ps_bc2[:], sem=bld.ev)
                    bld.wait("tensor", bld.ev)
                    for n in range(DS):
                        BO("tensor", "matmul", ps_bc2[:, n * TS:(n + 1) * TS],
                           ones_f[0:1, 0:1].broadcast_to([1, 128]),
                           xdfs[0:1, (17 + n) * TS:(18 + n) * TS],
                           start=True, stop=True,
                           sem=bld.mm if n == DS - 1 else None)
                    bld.wait("scalar", bld.mm)
                    BO("scalar", "copy", crep[:], ps_bc2[:], sem=bld.ev)
                    bld.wait("scalar", s_vs, at=max(0, s_vs.n - 1))
                    dA = dA2[gs % 2]
                    dA4 = dA[:].rearrange("p (t c n) -> p t c n", t=TS, c=NK)
                    for n in range(DS):
                        BO("scalar", "activation", dA4[:, :, :, n],
                           dt3[:, ts0:ts0 + TS, :], ACTF.Exp,
                           scale=a_cols[:, n:n + 1],
                           sem=bld.ev if n == DS - 1 else None)
                    if s == NSUB - 1:
                        bld.tick("scalar", s_chs)
                    bld.wait("vector", bld.ev)
                    BO("vector", "tensor_tensor",
                       dus[:].rearrange("p (t c) -> p t c", t=TS),
                       dt3[:, ts0:ts0 + TS, :], u3[:, ts0:ts0 + TS, :], ALU.mult)
                    BO("vector", "tensor_copy", hist[:, 0:256], hist_last)
                    for grp in range(TS // 8):
                        if grp % 2 == 0:
                            hh = grp // 2
                            BO("vector", "tensor_tensor",
                               dBu[:].rearrange("p (t c n) -> p t c n",
                                                t=16, c=NK),
                               dus[:].rearrange("p (t c) -> p t c", t=TS)
                                   [:, hh * 16:(hh + 1) * 16, :].unsqueeze(3)
                                   .broadcast_to([PP, 16, NK, DS]),
                               brep[:].rearrange("p (n t) -> p t n", n=DS)
                                   [:, hh * 16:(hh + 1) * 16, :].unsqueeze(2)
                                   .broadcast_to([PP, 16, NK, DS]), ALU.mult)
                            BO("vector", "tensor_scalar", dBu[:], dBu[:],
                               10.0, -10.0, ALU.min, ALU.max)
                            if hh == 1:
                                bld.tick("vector", s_gd)
                        for tl in range(grp * 8, grp * 8 + 8):
                            hpi = 0 if tl == 0 else 1 + ((tl - 1) % 16)
                            hci = 1 + (tl % 16)
                            hp = hist[:, hpi * 256:(hpi + 1) * 256]
                            hc = hist[:, hci * 256:(hci + 1) * 256]
                            av = dA[:, tl * 256:(tl + 1) * 256]
                            bv = dBu[:, (tl % 16) * 256:((tl % 16) + 1) * 256]
                            BO("vector", "scalar_tensor_tensor", hc, av, EXP_NEG20,
                               hp, ALU.max, ALU.mult)
                            BO("vector", "tensor_tensor", hc, hc, bv, ALU.add)
                            BO("vector", "scalar_tensor_tensor", hc, hc, 100.0,
                               cm100b, ALU.min, ALU.max)
                        hv = hist[:, (1 + (grp % 2) * 8) * 256:
                                  (1 + (grp % 2) * 8 + 8) * 256]
                        BO("vector", "tensor_tensor",
                           hcs[:].rearrange("p (t c n) -> p t c n", t=8, c=NK),
                           hv.rearrange("p (t c n) -> p t c n", t=8, c=NK),
                           crep[:].rearrange("p (n t) -> p t n", n=DS)[
                               :, grp * 8:grp * 8 + 8, :].unsqueeze(2)
                               .broadcast_to([PP, 8, NK, DS]), ALU.mult)
                        BO("vector", "tensor_reduce",
                           ysub3[:, grp * 8:grp * 8 + 8, :],
                           hcs[:].rearrange("p (t c n) -> p t c n", t=8, c=NK),
                           AX.X, ALU.add)
                    bld.tick("vector", s_vs)
                    if "y_dbg" in dbg:
                        bld.vv.n += 1
                        bld.raw("vector", lambda e: e.engine_nop().then_inc(bld.vv.h, 1))
                        bld.wait("sync", bld.vv)
                        for k in range(NK):
                            bld.dma("sync", S["y_dbg"].ap()[
                                k * 128:(k + 1) * 128,
                                t0 + ts0:t0 + ts0 + TS],
                                ysub3[:, :, k], sem=bld.dmb,
                                allow_slow_non_contiguous=True)
                        bld.wait("vector", bld.dmb)
                    BO("vector", "tensor_reduce", ysq[:, 0:TS], ysub3, AX.X, ALU.add,
                       sem=bld.vv)
                    bld.wait("tensor", bld.vv)
                    BO("tensor", "matmul", ps_row[0:1, 0:TS], ones_f[:, 0:1],
                       ysq[:, 0:TS], start=True, stop=True, sem=bld.mm)
                    bld.wait("vector", bld.mm)
                    BO("vector", "tensor_scalar", yrow[0:1, 0:TS], ps_row[0:1, 0:TS],
                       1.0 / DI, None, ALU.mult, sem=bld.vv)
                    bld.wait("tensor", bld.vv)
                    BO("tensor", "matmul", ps_mu[:, 0:TS],
                       ones_f[0:1, 0:1].broadcast_to([1, 128]), yrow[0:1, 0:TS],
                       start=True, stop=True, sem=bld.mm)
                    bld.wait("vector", bld.mm)
                    mu3 = ps_mu[:, 0:TS].unsqueeze(2).broadcast_to([PP, TS, NK])
                    BO("vector", "tensor_tensor", ysub3, ysub3, mu3, ALU.subtract,
                       sem=bld.vv)
                    bld.wait("scalar", bld.vv)
                    BO("scalar", "activation", ysq[:, 0:TS * NK], ysub[:], ACTF.Square,
                       sem=bld.ev)
                    bld.wait("vector", bld.ev)
                    BO("vector", "tensor_reduce", ysq[:, 0:TS], ysq3, AX.X, ALU.add,
                       sem=bld.vv)
                    bld.wait("tensor", bld.vv)
                    BO("tensor", "matmul", ps_row[0:1, TS:2 * TS], ones_f[:, 0:1],
                       ysq[:, 0:TS], start=True, stop=True, sem=bld.mm)
                    bld.wait("vector", bld.mm)
                    yv = yrow[0:1, TS:2 * TS]
                    BO("vector", "tensor_scalar", yv, ps_row[0:1, TS:2 * TS],
                       1.0 / DI, LN_EPS, ALU.mult, ALU.add, sem=bld.vv)
                    bld.wait("scalar", bld.vv)
                    BO("scalar", "activation", yrow[0:1, 2 * TS:3 * TS], yv, ACTF.Sqrt,
                       sem=bld.ev)
                    bld.wait("vector", bld.ev)
                    s0v = yrow[0:1, 2 * TS:3 * TS]
                    r0v = yrow[0:1, 3 * TS:4 * TS]
                    BO("vector", "reciprocal", r0v, s0v, sem=bld.vv)
                    bld.wait("vector", bld.vv)
                    bld.wait("tensor", bld.vv)
                    BO("tensor", "matmul", ps_mu[:, TS:2 * TS],
                       ones_f[0:1, 0:1].broadcast_to([1, 128]), r0v,
                       start=True, stop=True, sem=bld.mm)
                    bld.wait("vector", bld.mm)
                    rst3 = ps_mu[:, TS:2 * TS].unsqueeze(2).broadcast_to([PP, TS, NK])
                    lng3 = ln_g_t[:, 0:NK].unsqueeze(1).broadcast_to([PP, TS, NK])
                    lnb3 = ln_b_t[:, 0:NK].unsqueeze(1).broadcast_to([PP, TS, NK])
                    dpr3 = d_par_t[:, 0:NK].unsqueeze(1).broadcast_to([PP, TS, NK])
                    BO("vector", "tensor_tensor", ysub3, ysub3, rst3, ALU.mult)
                    BO("vector", "tensor_tensor", ysub3, ysub3, lng3, ALU.mult)
                    BO("vector", "tensor_tensor", ysub3, ysub3, lnb3, ALU.add)
                    usl = u3[:, ts0:ts0 + TS, :]
                    BO("vector", "tensor_tensor", ysq3, usl, dpr3, ALU.mult)
                    BO("vector", "tensor_tensor", ysub3, ysub3, ysq3, ALU.add)
                    if s == 0:
                        bld.wait("vector", s_prod)
                    zsl = zs_t[:].rearrange("p (c t) -> p t c", c=NK)[
                        :, ts0:ts0 + TS, :]
                    BO("vector", "tensor_tensor",
                       prodb[:].rearrange("p (c t) -> p t c", c=NK)[
                           :, ts0:ts0 + TS, :],
                       ysub3, zsl, ALU.mult, sem=bld.vv)
                bld.tick("vector", s_chv)
                bld.wait("sync", bld.vv)
                for k in range(NK):
                    bld.dma("sync", S["prod_bf"].ap()[k * 128:(k + 1) * 128,
                                                      t0:t0 + TCH],
                            prodb[:, k * TCH:(k + 1) * TCH], sem=s_prod)
            bld.barrier()
            ph2b.close()


        if phase_limit >= 4:
            ph2c = contextlib.ExitStack()
            pbuf = [sb(f"c2p{i}", [PP, NK * 512], BF16, ph2c) for i in range(2)]
            wob = sb("c2w", [PP, NK * 1024], BF16, ph2c)
            sst = [sb(f"c2s{i}", [PP, 1024], F32, ph2c) for i in range(2)]
            ps2c = [psb(f"c2ps{i}", [PP, 512], F32, ph2c) for i in range(4)]
            s_pb2 = [bld.slot("c2pb0"), bld.slot("c2pb1")]
            s_ss = [bld.slot("c2ss0"), bld.slot("c2ss1")]
            s_pe = [bld.slot(f"c2pe{i}") for i in range(4)]
            for k in range(NK):
                bld.dma("sync", wob[:, k * 1024:(k + 1) * 1024],
                        S["woutT"].ap()[k * 128:(k + 1) * 128, :])
            for tt in range(T // 512):
                psl = tt % 2
                bld.wait("sync", s_pb2[psl])
                for k in range(NK):
                    bld.dma("sync", pbuf[psl][:, k * 512:(k + 1) * 512],
                            S["prod_bf"].ap()[k * 128:(k + 1) * 128,
                                              tt * 512:(tt + 1) * 512])
                for ts4 in range(4):
                    for oh in range(2):
                        pb = 2 * (ts4 % 2) + oh
                        bld.wait("tensor", bld.dm)
                        bld.wait("tensor", s_pe[pb])
                        for k in range(NK):
                            BO("tensor", "matmul", ps2c[pb][:],
                               pbuf[psl][:, k * 512 + ts4 * 128:
                                         k * 512 + (ts4 + 1) * 128],
                               wob[:, k * 1024 + oh * 512:k * 1024 + (oh + 1) * 512],
                               start=(k == 0), stop=(k == NK - 1),
                               sem=bld.mm if k == NK - 1 else None)
                        if ts4 == 3 and oh == 1:
                            bld.tick("tensor", s_pb2[psl])
                        ssl = ts4 % 2
                        bld.wait("scalar", bld.mm)
                        bld.wait("scalar", s_ss[ssl])
                        BO("scalar", "copy", sst[ssl][:, oh * 512:(oh + 1) * 512],
                           ps2c[pb][:], sem=bld.ev)
                        bld.tick("scalar", s_pe[pb])
                    bld.wait("sync", bld.ev)
                    ssl = ts4 % 2
                    bld.dma("sync",
                            S["s0"].ap()[tt * 512 + ts4 * 128:
                                         tt * 512 + (ts4 + 1) * 128, :],
                            sst[ssl][:], sem=s_ss[ssl])
            bld.barrier()
            ph2c.close()

        if phase_limit >= 5:
            ph3a = contextlib.ExitStack()
            xqT = [sb(f"d3xq{i}", [PP, T], BF16, ph3a) for i in range(8)]
            x3t = sb("d3x", [PP, C], F32, ph3a)
            q3b = sb("d3q", [PP, C], BF16, ph3a)
            q3i = sb("d3i", [PP, C], I32, ph3a)
            trb3 = sb("d3t", [PP, C], BF16, ph3a)
            wg3 = [sb(f"d3wg{i}", [PP, 1024], BF16, ph3a) for i in range(2)]
            wu3 = [sb(f"d3wu{i}", [PP, 1024], BF16, ph3a) for i in range(2)]
            gsil = sb("d3gs", [PP, 512], F32, ph3a)
            guf = sb("d3gu", [PP, 512], F32, ph3a)
            gub = sb("d3gb", [PP, 512], BF16, ph3a)
            st3 = sb("d3st", [PP, 8], F32, ph3a)
            ps_g3 = psb("d3pg", [PP, 512], F32, ph3a)
            ps_u3 = psb("d3pu", [PP, 512], F32, ph3a)
            ps_t3 = psb("d3pt", [PP, C], BF16, ph3a)
            s_w3 = [bld.slot("d3w0"), bld.slot("d3w1")]
            s_gu = bld.slot("d3gu")
            for it in range(T // 128):
                bld.wait("sync", bld.vv)
                bld.wait("sync", bld.ev)
                bld.dma("sync", x3t[:], S["xn"].ap()[it * 128:(it + 1) * 128, :])
                bld.wait("vector", bld.dm)
                BO("vector", "tensor_scalar", x3t[:], x3t[:], qs[:, 0:1], 127.0,
                   ALU.mult, ALU.min)
                BO("vector", "tensor_scalar", x3t[:], x3t[:], -128.0, MAGIC,
                   ALU.max, ALU.add)
                BO("vector", "tensor_scalar", q3b[:], x3t[:], MAGIC, qs[:, 1:2],
                   ALU.subtract, ALU.mult, sem=bld.vv)
                if "xq_dbg" in dbg:
                    BO("vector", "tensor_scalar", x3t[:], x3t[:], MAGIC, None,
                       ALU.subtract, sem=bld.vv)
                    bld.wait("sync", bld.vv)
                    bld.dma("sync", S["xq_dbg"].ap()[it * 128:(it + 1) * 128, :],
                            x3t[:])
                bld.wait("tensor", bld.vv)
                bld.wait("tensor", bld.ev)
                for cb in range(8):
                    BO("tensor", "transpose", ps_t3[:, cb * 128:(cb + 1) * 128],
                       q3b[:, cb * 128:(cb + 1) * 128], ident_b[:],
                       sem=bld.mm if cb == 7 else None)
                bld.wait("scalar", bld.mm)
                BO("scalar", "copy", trb3[:], ps_t3[:], sem=bld.ev)
                bld.wait("vector", bld.ev)
                for cb in range(8):
                    BO("vector", "tensor_copy", xqT[cb][:, it * 128:(it + 1) * 128],
                       trb3[:, cb * 128:(cb + 1) * 128],
                       sem=bld.vv if cb == 7 else None)
            for ib in range(NB0):
                wsl = ib % 2
                bld.wait("sync", s_w3[wsl])
                bld.dma("sync", wg3[wsl][:],
                        S["wq_g0T"].ap()[ib].rearrange("(q p) m -> p q m", p=128))
                bld.dma("sync", wu3[wsl][:],
                        S["wq_u0T"].ap()[ib].rearrange("(q p) m -> p q m", p=128))
                if ib == 0 and "w0_dbg" in dbg:
                    bld.wait("vector", bld.dm)
                    BO("vector", "tensor_copy", gsil[:, 0:512], wg3[0][:, 0:512])
                    BO("vector", "tensor_copy", guf[:, 0:512], wg3[0][:, 512:1024],
                       sem=bld.vv)
                    bld.wait("sync", bld.vv)
                    bld.dma("sync", S["w0_dbg"].ap()[:, 0:512], gsil[:, 0:512])
                    bld.dma("sync", S["w0_dbg"].ap()[:, 512:1024], guf[:, 0:512])
                    bld.dma("sync", trb3[:, 0:128],
                            S["wq_g0T"].ap()[0, 0:128, :], sem=bld.dm)
                    bld.wait("vector", bld.dm)
                    BO("vector", "tensor_copy", gsil[:, 0:128], trb3[:, 0:128],
                       sem=bld.vv)
                    bld.wait("sync", bld.vv)
                    bld.dma("sync", S["w0_dbg"].ap()[:, 1024:1152], gsil[:, 0:128])
                for tt in range(T // 512):
                    bld.wait("tensor", bld.dm)
                    bld.wait("tensor", bld.vv)
                    if tt == 0 and ib == 0:
                        pass
                    bld.wait("tensor", bld.ev)
                    for cb in range(8):
                        BO("tensor", "matmul", ps_g3[:],
                           wg3[wsl][:, cb * 128:(cb + 1) * 128],
                           xqT[cb][:, tt * 512:(tt + 1) * 512],
                           start=(cb == 0), stop=(cb == 7))
                    for cb in range(8):
                        BO("tensor", "matmul", ps_u3[:],
                           wu3[wsl][:, cb * 128:(cb + 1) * 128],
                           xqT[cb][:, tt * 512:(tt + 1) * 512],
                           start=(cb == 0), stop=(cb == 7),
                           sem=bld.mm if cb == 7 else None)
                    if tt == T // 512 - 1:
                        bld.tick("tensor", s_w3[wsl])
                    bld.wait("scalar", bld.mm)
                    BO("scalar", "activation", gsil[:], ps_g3[:], ACTF.Silu,
                       sem=bld.ev)
                    bld.wait("vector", bld.ev)
                    bld.wait("vector", s_gu)
                    BO("vector", "tensor_tensor", guf[:], gsil[:], ps_u3[:], ALU.mult)
                    BO("vector", "tensor_reduce", st3[:, 0:1], guf[:], AX.X, ALU.max,
                       apply_absolute_value=True, sem=bld.vv)
                    bld.wait("vector", bld.vv)
                    BO("vector", "tensor_tensor", lmax[:, 1:2], lmax[:, 1:2],
                       st3[:, 0:1], ALU.max)
                    BO("vector", "tensor_copy", gub[:], guf[:], sem=bld.vv)
                    bld.tick("vector", bld.ev)
                    bld.wait("sync", bld.vv)
                    bld.dma("sync", S["gu_bf"].ap()[ib * 128:(ib + 1) * 128,
                                                    tt * 512:(tt + 1) * 512],
                            gub[:], sem=s_gu)
            global_max(1, 1)
            derive_scale(1, 127.0, 15, 2, 3)
            bld.barrier()
            ph3a.close()

        if phase_limit >= 6:
            ph3b = contextlib.ExitStack()
            wd3 = [sb(f"e3wd{i}", [PP, 1024], BF16, ph3b) for i in range(NB0)]
            gu3 = sb("e3gu", [PP, NB0 * 512], BF16, ph3b)
            hq3 = sb("e3hq", [PP, NB0 * 512], BF16, ph3b)
            hqi = sb("e3hi", [PP, 512], I32, ph3b)
            hqf = sb("e3hf", [PP, 512], F32, ph3b)
            g3 = sb("e3g", [PP, C], F32, ph3b)
            xx3 = sb("e3x", [PP, C], F32, ph3b)
            ss3 = sb("e3s", [PP, C], F32, ph3b)
            xn1t = sb("e3n", [PP, C], F32, ph3b)
            xn1b = sb("e3nb", [PP, C], BF16, ph3b)
            dmp3 = sb("e3d", [PP, C], F32, ph3b)
            st3b = sb("e3st", [PP, 8], F32, ph3b)
            trb3b = sb("e3t", [PP, C], BF16, ph3b)
            ps_d3 = [psb(f"e3p{i}", [PP, 512], F32, ph3b) for i in range(2)]
            ps_t3b = psb("e3pt", [PP, C], BF16, ph3b)
            for k in range(NB0):
                bld.dma("sync", wd3[k][:],
                        S["wq_d0T"].ap()[k * 128:(k + 1) * 128, :])
            for tt in range(T // 512):
                bld.wait("sync", bld.vv)
                for k in range(NB0):
                    bld.dma("sync", gu3[:, k * 512:(k + 1) * 512],
                            S["gu_bf"].ap()[k * 128:(k + 1) * 128,
                                            tt * 512:(tt + 1) * 512])
                bld.wait("vector", bld.dm)
                for k in range(NB0):
                    BO("vector", "tensor_scalar", hqf[:], gu3[:, k * 512:(k + 1) * 512],
                       qs[:, 2:3], 127.0, ALU.mult, ALU.min)
                    BO("vector", "tensor_scalar", hqf[:], hqf[:], -128.0, MAGIC,
                       ALU.max, ALU.add)
                    BO("vector", "tensor_scalar", hq3[:, k * 512:(k + 1) * 512],
                       hqf[:], MAGIC, qs[:, 3:4], ALU.subtract, ALU.mult,
                       sem=bld.vv if k == NB0 - 1 else None)
                for t4 in range(4):
                    it = tt * 4 + t4
                    bld.wait("tensor", bld.vv)
                    bld.wait("tensor", bld.ev)
                    for oh in range(2):
                        for k in range(NB0):
                            BO("tensor", "matmul", ps_d3[oh][:],
                               hq3[:, k * 512 + t4 * 128:k * 512 + (t4 + 1) * 128],
                               wd3[k][:, oh * 512:(oh + 1) * 512],
                               start=(k == 0), stop=(k == NB0 - 1),
                               sem=bld.mm if (k == NB0 - 1 and oh == 1) else None)
                    bld.wait("scalar", bld.mm)
                    bld.wait("scalar", bld.vv)
                    BO("scalar", "activation", g3[:, 0:512], ps_d3[0][:], ACTF.Silu)
                    BO("scalar", "activation", g3[:, 512:1024], ps_d3[1][:], ACTF.Silu,
                       sem=bld.ev)
                    bld.wait("sync", bld.ev)
                    if "g0_dbg" in dbg:
                        bld.dma("sync", S["g0_dbg"].ap()[it * 128:(it + 1) * 128, :],
                                g3[:])
                    bld.dma("sync", xx3[:], D["x_in"][it * 128:(it + 1) * 128, :])
                    bld.dma("sync", ss3[:], S["s0"].ap()[it * 128:(it + 1) * 128, :])
                    bld.wait("vector", bld.dm)
                    bld.wait("vector", bld.ev)
                    BO("vector", "tensor_tensor", g3[:], g3[:], ss3[:], ALU.mult)
                    BO("vector", "tensor_tensor", xx3[:], xx3[:], g3[:], ALU.add,
                       sem=bld.vv)
                    bld.tick("vector", bld.ev)
                    bld.wait("sync", bld.vv)
                    bld.dma("sync", S["x1"].ap()[it * 128:(it + 1) * 128, :], xx3[:])
                    bld.wait("scalar", bld.vv)
                    emit_ln_tile(xx3[:], xn1t[:], xn1b[:], st3b, dmp3[:], C,
                                 absmax_slot=2)
                    bld.wait("sync", bld.vv)
                    bld.dma("sync", S["xn1"].ap()[it * 128:(it + 1) * 128, :],
                            xn1t[:])
                    bld.wait("tensor", bld.vv)
                    bld.wait("tensor", bld.ev)
                    for cb in range(8):
                        BO("tensor", "transpose", ps_t3b[:, cb * 128:(cb + 1) * 128],
                           xn1b[:, cb * 128:(cb + 1) * 128], ident_b[:],
                           sem=bld.mm if cb == 7 else None)
                    bld.wait("scalar", bld.mm)
                    BO("scalar", "copy", trb3b[:], ps_t3b[:], sem=bld.ev)
                    bld.wait("sync", bld.ev)
                    for cb in range(8):
                        bld.dma("sync", S["xn1T_bf"].ap()[cb * 128:(cb + 1) * 128,
                                                          it * 128:(it + 1) * 128],
                                trb3b[:, cb * 128:(cb + 1) * 128])
            global_max(2, 2)
            derive_scale(2, 32767.0, None, 4, 5, floor=1e-12)
            if "qs2_dbg" in dbg:
                bld.wait("sync", bld.vv)
                bld.wait("sync", bld.ev)
                bld.dma("sync", S["qs2_dbg"].ap()[:, 0:16], qs[:, 0:16])
                bld.dma("sync", S["qs2_dbg"].ap()[:, 16:24], gmax[:, 0:8])
                bld.dma("sync", S["qs2_dbg"].ap()[:, 24:32], lmax[:, 0:8])
            bld.barrier()
            ph3b.close()

        attn_sb = sb("attn_sb", [64, H * DH], BF16)
        if phase_limit >= 7:
            ph4a = contextlib.ExitStack()
            wk4 = [sb(f"f4wk{i}", [PP, C], BF16, ph4a) for i in range(8)]
            wv4 = [sb(f"f4wv{i}", [PP, C], BF16, ph4a) for i in range(8)]
            wg4 = [sb(f"f4wg{i}", [PP, C], BF16, ph4a) for i in range(8)]
            xT4 = sb("f4x", [PP, 8 * 128], BF16, ph4a)
            kst = sb("f4k", [PP, C], F32, ph4a)
            krp = sb("f4kr", [PP, C], BF16, ph4a)
            rt1 = sb("f4r1", [PP, 512], F32, ph4a)
            rt2 = sb("f4r2", [PP, 512], F32, ph4a)
            cosT = sb("f4c", [PP, 32], F32, ph4a)
            sinT = sb("f4s", [PP, 32], F32, ph4a)
            kTh = sb("f4kt", [64, H * 128], BF16, ph4a)
            vb4 = sb("f4v", [PP, C], BF16, ph4a)
            gst = sb("f4g", [PP, C], F32, ph4a)
            gsb = sb("f4gb", [PP, C], BF16, ph4a)
            gmx = sb("f4gm", [PP, 20], F32, ph4a)
            trg = sb("f4tg", [PP, C], BF16, ph4a)
            sc4 = sb("f4sc", [64, CS], F32, ph4a)
            e4 = sb("f4e", [64, CS], F32, ph4a)
            pm4 = sb("f4pm", [64, 4], F32, ph4a)
            pb4 = sb("f4pb", [64, CS], BF16, ph4a)
            pT4 = sb("f4pt", [PP, 64], BF16, ph4a)
            attn_acc = sb("f4aa", [64, H * DH], F32, ph4a)
            ps_kv = psb("f4p", [PP, 512], F32, ph4a)
            ps_sc = psb("f4ps", [64, CS], F32, ph4a)
            ps_at = psb("f4pa", [64, DH], F32, ph4a)
            ps_kT = psb("f4pk", [64, H * 128], BF16, ph4a)
            ps_tg = psb("f4ptg", [PP, C], BF16, ph4a)
            ps_pt = ps_tg[:, 0:64]
            BO("vector", "memset", attn_acc[:, :], 0.0, sem=bld.vv)
            for cb in range(8):
                bld.dma("sync", wk4[cb][:], S["wkT"].ap()[cb * 128:(cb + 1) * 128, :])
                bld.dma("sync", wv4[cb][:], S["wvT"].ap()[cb * 128:(cb + 1) * 128, :])
                bld.dma("sync", wg4[cb][:], S["wgT"].ap()[cb * 128:(cb + 1) * 128, :])
            for it in range(T // 128):
                bld.wait("sync", bld.mm)
                bld.wait("sync", bld.vv)
                bld.wait("sync", bld.ev)
                for cb in range(8):
                    bld.dma("sync", xT4[:, cb * 128:(cb + 1) * 128],
                            S["xn1T_bf"].ap()[cb * 128:(cb + 1) * 128,
                                              it * 128:(it + 1) * 128])
                bld.dma("sync", cosT[:, :], D["cos_t"][it * 128:(it + 1) * 128, :])
                bld.dma("sync", sinT[:, :], D["sin_t"][it * 128:(it + 1) * 128, :])
                for (dst, wsrc) in [(kst, wk4), (vb4, wv4), (gst, wg4)]:
                    for oh in range(2):
                        bld.wait("tensor", bld.dm)
                        bld.wait("tensor", bld.ev)
                        for cb in range(8):
                            BO("tensor", "matmul", ps_kv[:],
                               xT4[:, cb * 128:(cb + 1) * 128],
                               wsrc[cb][:, oh * 512:(oh + 1) * 512],
                               start=(cb == 0), stop=(cb == 7),
                               sem=bld.mm if cb == 7 else None)
                        bld.wait("scalar", bld.mm)
                        BO("scalar", "copy", dst[:, oh * 512:(oh + 1) * 512],
                           ps_kv[:], sem=bld.ev)
                bld.wait("vector", bld.ev)
                av2 = kst[:].rearrange("p (h i two) -> p h i two", h=H, two=2)
                a4 = av2[:, :, :, 0]
                b4 = av2[:, :, :, 1]
                cexp = cosT[:].unsqueeze(1).broadcast_to([PP, H, 32])
                sexp = sinT[:].unsqueeze(1).broadcast_to([PP, H, 32])
                r13 = rt1[:].rearrange("p (h i) -> p h i", h=H)
                r23 = rt2[:].rearrange("p (h i) -> p h i", h=H)
                kv4 = krp[:].rearrange("p (h i two) -> p h i two", h=H, two=2)
                BO("vector", "tensor_tensor", r13, a4, cexp, ALU.mult)
                BO("vector", "tensor_tensor", r23, b4, sexp, ALU.mult)
                BO("vector", "tensor_tensor", kv4[:, :, :, 0], r13, r23, ALU.subtract)
                BO("vector", "tensor_tensor", r13, a4, sexp, ALU.mult)
                BO("vector", "tensor_tensor", r23, b4, cexp, ALU.mult)
                BO("vector", "tensor_tensor", kv4[:, :, :, 1], r13, r23, ALU.add,
                   sem=bld.vv)
                bld.wait("tensor", bld.vv)
                bld.wait("tensor", bld.ev)
                for h in range(H):
                    BO("tensor", "transpose", ps_kT[:, h * 128:(h + 1) * 128],
                       krp[:, h * 64:(h + 1) * 64], ident_b[:],
                       sem=bld.mm if h == H - 1 else None)
                bld.wait("scalar", bld.mm)
                BO("scalar", "copy", kTh[:, :], ps_kT[:, :], sem=bld.ev)
                bld.wait("vector", bld.ev)
                g3v = gst[:].rearrange("p (h l) -> p h l", h=H)
                BO("vector", "tensor_reduce", gmx[:, 0:16], g3v, AX.X, ALU.max,
                   sem=bld.vv)
                bld.wait("vector", bld.vv)
                mexp4 = gmx[:, 0:16].unsqueeze(2).broadcast_to([PP, H, L])
                BO("vector", "tensor_tensor", g3v, g3v, mexp4, ALU.subtract,
                   sem=bld.vv)
                bld.wait("scalar", bld.vv)
                BO("scalar", "activation", gst[:], gst[:], ACTF.Exp, scale=0.125,
                   sem=bld.ev)
                bld.wait("vector", bld.ev)
                BO("vector", "tensor_reduce", gmx[:, 0:16], g3v, AX.X, ALU.add,
                   sem=bld.vv)
                bld.wait("vector", bld.vv)
                BO("vector", "reciprocal", gmx[:, 0:16], gmx[:, 0:16], sem=bld.vv)
                bld.wait("vector", bld.vv)
                BO("vector", "tensor_tensor",
                   gsb[:].rearrange("p (h l) -> p h l", h=H), g3v,
                   gmx[:, 0:16].unsqueeze(2).broadcast_to([PP, H, L]), ALU.mult,
                   sem=bld.vv)
                bld.wait("tensor", bld.vv)
                for cb in range(8):
                    BO("tensor", "transpose", ps_tg[:, cb * 128:(cb + 1) * 128],
                       gsb[:, cb * 128:(cb + 1) * 128], ident_b[:],
                       sem=bld.mm if cb == 7 else None)
                bld.wait("scalar", bld.mm)
                BO("scalar", "copy", trg[:], ps_tg[:], sem=bld.ev)
                bld.wait("sync", bld.ev)
                for cb in range(8):
                    bld.dma("sync", S["gatesT_bf"].ap()[cb * 128:(cb + 1) * 128,
                                                        it * 128:(it + 1) * 128],
                            trg[:, cb * 128:(cb + 1) * 128])
                for h in range(H):
                    bld.wait("tensor", bld.vv)
                    BO("tensor", "matmul", ps_sc[:, :],
                       qh_b[:, h * L:(h + 1) * L],
                       kTh[:, h * 128:(h + 1) * 128],
                       start=True, stop=True, sem=bld.mm)
                    bld.wait("vector", bld.mm)
                    BO("vector", "tensor_tensor", sc4[:], ps_sc[:, :], cmask_t[:],
                       ALU.add)
                    BO("vector", "tensor_reduce", pm4[:, 0:1], sc4[:], AX.X, ALU.max,
                       sem=bld.vv)
                    bld.wait("vector", bld.vv)
                    BO("vector", "tensor_scalar", pm4[:, 1:2], pm4[:, 0:1], -0.125,
                       None, ALU.mult, sem=bld.vv)
                    bld.wait("scalar", bld.vv)
                    BO("scalar", "activation", e4[:], sc4[:], ACTF.Exp, scale=0.125,
                       bias=pm4[:, 1:2], sem=bld.ev)
                    bld.wait("vector", bld.ev)
                    BO("vector", "tensor_reduce", pm4[:, 2:3], e4[:], AX.X, ALU.add,
                       sem=bld.vv)
                    bld.wait("vector", bld.vv)
                    BO("vector", "reciprocal", pm4[:, 3:4], pm4[:, 2:3], sem=bld.vv)
                    bld.wait("vector", bld.vv)
                    BO("vector", "tensor_scalar", pb4[:], e4[:], pm4[:, 3:4], None,
                       ALU.mult, sem=bld.vv)
                    bld.wait("tensor", bld.vv)
                    BO("tensor", "transpose", ps_pt, pb4[:], ident_b[0:64, 0:64],
                       sem=bld.mm)
                    bld.wait("scalar", bld.mm)
                    BO("scalar", "copy", pT4[:, :], ps_pt, sem=bld.ev)
                    bld.wait("tensor", bld.ev)
                    bld.wait("tensor", bld.vv)
                    BO("tensor", "matmul", ps_at[:, :],
                       pT4[:, :], vb4[:, h * DH:(h + 1) * DH],
                       start=True, stop=True, sem=bld.mm)
                    bld.wait("vector", bld.mm)
                    BO("vector", "tensor_tensor", attn_acc[:, h * DH:(h + 1) * DH],
                       attn_acc[:, h * DH:(h + 1) * DH], ps_at[:, :], ALU.add,
                       sem=bld.vv)
            bld.wait("scalar", bld.vv)
            BO("scalar", "copy", attn_sb[:, :], attn_acc[:, :], sem=bld.ev)
            bld.barrier()
            ph4a.close()

        if phase_limit >= 8:
            ph4b = contextlib.ExitStack()
            wp4 = [sb(f"g4wp{i}", [PP, C], BF16, ph4b) for i in range(8)]
            gT4 = sb("g4g", [64, H * 128], BF16, ph4b)
            wgt = sb("g4w", [PP, C], BF16, ph4b)
            wgtT = sb("g4wt", [PP, 8 * 128], BF16, ph4b)
            s1st = sb("g4s", [PP, C], F32, ph4b)
            ps_w4 = psb("g4pw", [PP, C], F32, ph4b)
            ps_s4 = [psb(f"g4ps{i}", [PP, 512], F32, ph4b) for i in range(2)]
            ps_t4 = psb("g4pt", [PP, C], BF16, ph4b)
            for cb in range(8):
                bld.dma("sync", wp4[cb][:], S["wpT"].ap()[cb * 128:(cb + 1) * 128, :])
            for it in range(T // 128):
                bld.wait("sync", bld.mm)
                bld.wait("sync", bld.ev)
                for h in range(H):
                    bld.dma("sync", gT4[:, h * 128:(h + 1) * 128],
                            S["gatesT_bf"].ap()[h * 64:(h + 1) * 64,
                                                it * 128:(it + 1) * 128])
                bld.wait("tensor", bld.dm)
                bld.wait("tensor", bld.ev)
                for h in range(H):
                    BO("tensor", "matmul", ps_w4[:, h * DH:(h + 1) * DH],
                       gT4[:, h * 128:(h + 1) * 128],
                       attn_sb[:, h * DH:(h + 1) * DH],
                       start=True, stop=True,
                       sem=bld.mm if h == H - 1 else None)
                bld.wait("scalar", bld.mm)
                BO("scalar", "copy", wgt[:], ps_w4[:], sem=bld.ev)
                bld.wait("tensor", bld.ev)
                for cb in range(8):
                    BO("tensor", "transpose", ps_t4[:, cb * 128:(cb + 1) * 128],
                       wgt[:, cb * 128:(cb + 1) * 128], ident_b[:],
                       sem=bld.mm if cb == 7 else None)
                bld.wait("scalar", bld.mm)
                BO("scalar", "copy", wgtT[:], ps_t4[:], sem=bld.ev)
                bld.wait("tensor", bld.ev)
                for oh in range(2):
                    for cb in range(8):
                        BO("tensor", "matmul", ps_s4[oh][:],
                           wgtT[:, cb * 128:(cb + 1) * 128],
                           wp4[cb][:, oh * 512:(oh + 1) * 512],
                           start=(cb == 0), stop=(cb == 7),
                           sem=bld.mm if cb == 7 else None)
                bld.wait("scalar", bld.mm)
                BO("scalar", "copy", s1st[:, 0:512], ps_s4[0][:])
                BO("scalar", "copy", s1st[:, 512:1024], ps_s4[1][:], sem=bld.ev)
                bld.wait("sync", bld.ev)
                bld.dma("sync", S["s1"].ap()[it * 128:(it + 1) * 128, :], s1st[:])
            bld.barrier()
            ph4b.close()

        if phase_limit >= 9:
            ph5a = contextlib.ExitStack()
            xqT5 = [sb(f"h5xq{i}", [PP, T], BF16, ph5a) for i in range(8)]
            x5t = sb("h5x", [PP, C], F32, ph5a)
            q5b = sb("h5q", [PP, C], BF16, ph5a)
            q5i = sb("h5i", [PP, C], I32, ph5a)
            trb5 = sb("h5t", [PP, C], BF16, ph5a)
            wg5 = [sb(f"h5wg{i}", [PP, 1024], BF16, ph5a) for i in range(2)]
            wu5 = [sb(f"h5wu{i}", [PP, 1024], BF16, ph5a) for i in range(2)]
            ub5 = sb("h5ub", [PP, 512], BF16, ph5a)
            gb5 = sb("h5gb", [PP, 512], BF16, ph5a)
            st5 = sb("h5st", [PP, 8], F32, ph5a)
            ps_g5 = psb("h5pg", [PP, 512], F32, ph5a)
            ps_u5 = psb("h5pu", [PP, 512], F32, ph5a)
            ps_t5 = psb("h5pt", [PP, C], BF16, ph5a)
            s_w5 = [bld.slot("h5w0"), bld.slot("h5w1")]
            s_st5 = bld.slot("h5st")
            for it in range(T // 128):
                bld.wait("sync", bld.vv)
                bld.wait("sync", bld.ev)
                bld.dma("sync", x5t[:], S["xn1"].ap()[it * 128:(it + 1) * 128, :])
                bld.wait("vector", bld.dm)
                BO("vector", "tensor_scalar", x5t[:], x5t[:], qs[:, 4:5], 32767.0,
                   ALU.mult, ALU.min)
                BO("vector", "tensor_scalar", x5t[:], x5t[:], -32768.0, MAGIC,
                   ALU.max, ALU.add)
                BO("vector", "tensor_scalar", q5b[:], x5t[:], MAGIC, qs[:, 5:6],
                   ALU.subtract, ALU.mult, sem=bld.vv)
                bld.wait("tensor", bld.vv)
                bld.wait("tensor", bld.ev)
                for cb in range(8):
                    BO("tensor", "transpose", ps_t5[:, cb * 128:(cb + 1) * 128],
                       q5b[:, cb * 128:(cb + 1) * 128], ident_b[:],
                       sem=bld.mm if cb == 7 else None)
                bld.wait("scalar", bld.mm)
                BO("scalar", "copy", trb5[:], ps_t5[:], sem=bld.ev)
                bld.wait("vector", bld.ev)
                for cb in range(8):
                    BO("vector", "tensor_copy", xqT5[cb][:, it * 128:(it + 1) * 128],
                       trb5[:, cb * 128:(cb + 1) * 128],
                       sem=bld.vv if cb == 7 else None)
            for ib in range(NB1):
                wsl = ib % 2
                bld.wait("sync", s_w5[wsl])
                bld.dma("sync", wg5[wsl][:],
                        S["wq_g1T"].ap()[ib].rearrange("(q p) m -> p q m", p=128))
                bld.dma("sync", wu5[wsl][:],
                        S["wq_u1T"].ap()[ib].rearrange("(q p) m -> p q m", p=128))
                for tt in range(T // 512):
                    bld.wait("tensor", bld.dm)
                    bld.wait("tensor", bld.vv)
                    bld.wait("tensor", bld.ev)
                    for cb in range(8):
                        BO("tensor", "matmul", ps_g5[:],
                           wg5[wsl][:, cb * 128:(cb + 1) * 128],
                           xqT5[cb][:, tt * 512:(tt + 1) * 512],
                           start=(cb == 0), stop=(cb == 7))
                    for cb in range(8):
                        BO("tensor", "matmul", ps_u5[:],
                           wu5[wsl][:, cb * 128:(cb + 1) * 128],
                           xqT5[cb][:, tt * 512:(tt + 1) * 512],
                           start=(cb == 0), stop=(cb == 7),
                           sem=bld.mm if cb == 7 else None)
                    if tt == T // 512 - 1:
                        bld.tick("tensor", s_w5[wsl])
                    bld.wait("vector", bld.mm)
                    bld.wait("vector", s_st5)
                    BO("vector", "tensor_reduce", st5[:, 0:1], ps_g5[:], AX.X,
                       ALU.max, apply_absolute_value=True)
                    BO("vector", "tensor_reduce", st5[:, 1:2], ps_u5[:], AX.X,
                       ALU.max, apply_absolute_value=True, sem=bld.vv)
                    bld.wait("vector", bld.vv)
                    BO("vector", "tensor_tensor", lmax[:, 4:5], lmax[:, 4:5],
                       st5[:, 0:1], ALU.max)
                    BO("vector", "tensor_tensor", lmax[:, 3:4], lmax[:, 3:4],
                       st5[:, 1:2], ALU.max, sem=bld.vv)
                    bld.wait("scalar", bld.vv)
                    BO("scalar", "copy", gb5[:], ps_g5[:])
                    BO("scalar", "copy", ub5[:], ps_u5[:], sem=bld.ev)
                    bld.tick("scalar", bld.ev)
                    bld.wait("sync", bld.ev)
                    bld.dma("sync", S["gater_bf"].ap()[ib * 128:(ib + 1) * 128,
                                                       tt * 512:(tt + 1) * 512],
                            gb5[:], sem=s_st5)
                    bld.dma("sync", S["upr_bf"].ap()[ib * 128:(ib + 1) * 128,
                                                     tt * 512:(tt + 1) * 512],
                            ub5[:], sem=s_st5)
            global_max(3, 3)
            derive_scale(3, 32767.0, None, 6, 7, floor=1e-12)
            global_max(4, 4)
            derive_scale(4, 32767.0, None, 8, 9, floor=1e-12)
            bld.barrier()
            ph5a.close()

        if phase_limit >= 10:
            ph5b = contextlib.ExitStack()
            gr5 = sb("i5g", [PP, 512], BF16, ph5b)
            gq5 = sb("i5gq", [PP, 512], F32, ph5b)
            sg5 = sb("i5s", [PP, 512], F32, ph5b)
            t25 = sb("i5t", [PP, 512], F32, ph5b)
            t2b5 = sb("i5tb", [PP, 512], BF16, ph5b)
            i5 = sb("i5i", [PP, 512], I32, ph5b)
            st5b = sb("i5st", [PP, 4], F32, ph5b)
            s_st5b = bld.slot("i5st")
            for tt in range(T // 512):
                for ib in range(NB1):
                    bld.wait("sync", bld.vv)
                    bld.dma("sync", gr5[:], S["gater_bf"].ap()[
                        ib * 128:(ib + 1) * 128, tt * 512:(tt + 1) * 512])
                    bld.wait("vector", bld.dm)
                    BO("vector", "tensor_scalar", gq5[:], gr5[:], qs[:, 8:9], 32767.0,
                       ALU.mult, ALU.min)
                    BO("vector", "tensor_scalar", gq5[:], gq5[:], -32768.0, MAGIC,
                       ALU.max, ALU.add)
                    BO("vector", "tensor_scalar", gq5[:], gq5[:], MAGIC,
                       qs[:, 9:10], ALU.subtract, ALU.mult, sem=bld.vv)
                    bld.wait("scalar", bld.vv)
                    BO("scalar", "activation", sg5[:], gq5[:], ACTF.Sigmoid,
                       sem=bld.ev)
                    bld.wait("vector", bld.ev)
                    BO("vector", "tensor_scalar", sg5[:], sg5[:], 65536.0, 65535.0,
                       ALU.mult, ALU.min)
                    BO("vector", "tensor_scalar", sg5[:], sg5[:], 0.0, MAGIC,
                       ALU.max, ALU.add)
                    BO("vector", "tensor_scalar", sg5[:], sg5[:], MAGIC,
                       1.0 / 65536.0, ALU.subtract, ALU.mult)
                    BO("vector", "tensor_tensor", t25[:], gq5[:], sg5[:], ALU.mult)
                    BO("vector", "tensor_reduce", st5b[:, 0:1], t25[:], AX.X,
                       ALU.max, apply_absolute_value=True, sem=bld.vv)
                    bld.wait("vector", bld.vv)
                    BO("vector", "tensor_tensor", lmax[:, 5:6], lmax[:, 5:6],
                       st5b[:, 0:1], ALU.max)
                    bld.wait("vector", s_st5b)
                    BO("vector", "tensor_copy", t2b5[:], t25[:], sem=bld.vv)
                    bld.wait("sync", bld.vv)
                    bld.dma("sync", S["t2_bf"].ap()[ib * 128:(ib + 1) * 128,
                                                    tt * 512:(tt + 1) * 512],
                            t2b5[:], sem=s_st5b)
            global_max(5, 5)
            derive_scale(5, 32767.0, None, 10, 11, floor=1e-12)
            bld.barrier()
            ph5b.close()

        if phase_limit >= 11:
            ph5c = contextlib.ExitStack()
            a5 = sb("j5a", [PP, 512], BF16, ph5c)
            b5 = sb("j5b", [PP, 512], BF16, ph5c)
            aq5 = sb("j5aq", [PP, 512], F32, ph5c)
            bq5 = sb("j5bq", [PP, 512], F32, ph5c)
            t35 = sb("j5t", [PP, 512], F32, ph5c)
            t3b5 = sb("j5tb", [PP, 512], BF16, ph5c)
            i5c = sb("j5i", [PP, 512], I32, ph5c)
            st5c = sb("j5st", [PP, 4], F32, ph5c)
            s_st5c = bld.slot("j5st")
            for tt in range(T // 512):
                for ib in range(NB1):
                    bld.wait("sync", bld.vv)
                    bld.dma("sync", a5[:], S["t2_bf"].ap()[
                        ib * 128:(ib + 1) * 128, tt * 512:(tt + 1) * 512])
                    bld.dma("sync", b5[:], S["upr_bf"].ap()[
                        ib * 128:(ib + 1) * 128, tt * 512:(tt + 1) * 512])
                    bld.wait("vector", bld.dm)
                    BO("vector", "tensor_scalar", aq5[:], a5[:], qs[:, 10:11],
                       32767.0, ALU.mult, ALU.min)
                    BO("vector", "tensor_scalar", aq5[:], aq5[:], -32768.0, MAGIC,
                       ALU.max, ALU.add)
                    BO("vector", "tensor_scalar", aq5[:], aq5[:], MAGIC,
                       qs[:, 11:12], ALU.subtract, ALU.mult)
                    BO("vector", "tensor_scalar", bq5[:], b5[:], qs[:, 6:7],
                       32767.0, ALU.mult, ALU.min)
                    BO("vector", "tensor_scalar", bq5[:], bq5[:], -32768.0, MAGIC,
                       ALU.max, ALU.add)
                    BO("vector", "tensor_scalar", bq5[:], bq5[:], MAGIC,
                       qs[:, 7:8], ALU.subtract, ALU.mult)
                    BO("vector", "tensor_tensor", t35[:], aq5[:], bq5[:], ALU.mult)
                    BO("vector", "tensor_reduce", st5c[:, 0:1], t35[:], AX.X,
                       ALU.max, apply_absolute_value=True, sem=bld.vv)
                    bld.wait("vector", bld.vv)
                    BO("vector", "tensor_tensor", lmax[:, 6:7], lmax[:, 6:7],
                       st5c[:, 0:1], ALU.max)
                    bld.wait("vector", s_st5c)
                    BO("vector", "tensor_copy", t3b5[:], t35[:], sem=bld.vv)
                    bld.wait("sync", bld.vv)
                    bld.dma("sync", S["t3_bf"].ap()[ib * 128:(ib + 1) * 128,
                                                    tt * 512:(tt + 1) * 512],
                            t3b5[:], sem=s_st5c)
            global_max(6, 6)
            derive_scale(6, 32767.0, None, 12, 13, floor=1e-12)
            bld.barrier()
            ph5c.close()

        if phase_limit >= 12:
            ph5d = contextlib.ExitStack()
            wd5 = [sb(f"k5wd{i}", [PP, 1024], BF16, ph5d) for i in range(NB1)]
            t3l = sb("k5t", [PP, NB1 * 512], BF16, ph5d)
            oq5 = sb("k5o", [PP, NB1 * 512], BF16, ph5d)
            of5 = sb("k5f", [PP, 512], F32, ph5d)
            oi5 = sb("k5i", [PP, 512], I32, ph5d)
            g5d = sb("k5g", [PP, C], F32, ph5d)
            x5d = sb("k5x", [PP, C], F32, ph5d)
            s5d = sb("k5s", [PP, C], F32, ph5d)
            ps_d5 = [psb(f"k5p{i}", [PP, 512], F32, ph5d) for i in range(2)]
            s_out = bld.slot("k5out")
            for k in range(NB1):
                bld.dma("sync", wd5[k][:],
                        S["wq_d1T"].ap()[k * 128:(k + 1) * 128, :])
            for tt in range(T // 512):
                bld.wait("sync", bld.vv)
                for k in range(NB1):
                    bld.dma("sync", t3l[:, k * 512:(k + 1) * 512],
                            S["t3_bf"].ap()[k * 128:(k + 1) * 128,
                                            tt * 512:(tt + 1) * 512])
                bld.wait("vector", bld.dm)
                for k in range(NB1):
                    BO("vector", "tensor_scalar", of5[:], t3l[:, k * 512:(k + 1) * 512],
                       qs[:, 12:13], 32767.0, ALU.mult, ALU.min)
                    BO("vector", "tensor_scalar", of5[:], of5[:], -32768.0, MAGIC,
                       ALU.max, ALU.add)
                    BO("vector", "tensor_scalar", oq5[:, k * 512:(k + 1) * 512],
                       of5[:], MAGIC, qs[:, 13:14], ALU.subtract, ALU.mult,
                       sem=bld.vv if k == NB1 - 1 else None)
                for t4 in range(4):
                    it = tt * 4 + t4
                    bld.wait("tensor", bld.vv)
                    bld.wait("tensor", bld.ev)
                    for oh in range(2):
                        for k in range(NB1):
                            BO("tensor", "matmul", ps_d5[oh][:],
                               oq5[:, k * 512 + t4 * 128:k * 512 + (t4 + 1) * 128],
                               wd5[k][:, oh * 512:(oh + 1) * 512],
                               start=(k == 0), stop=(k == NB1 - 1),
                               sem=bld.mm if (k == NB1 - 1 and oh == 1) else None)
                    bld.wait("scalar", bld.mm)
                    BO("scalar", "activation", g5d[:, 0:512], ps_d5[0][:], ACTF.Silu)
                    BO("scalar", "activation", g5d[:, 512:1024], ps_d5[1][:],
                       ACTF.Silu, sem=bld.ev)
                    bld.wait("sync", bld.ev)
                    bld.wait("sync", s_out)
                    bld.dma("sync", x5d[:], S["x1"].ap()[it * 128:(it + 1) * 128, :])
                    bld.dma("sync", s5d[:], S["s1"].ap()[it * 128:(it + 1) * 128, :])
                    bld.wait("vector", bld.dm)
                    bld.wait("vector", bld.ev)
                    BO("vector", "tensor_tensor", g5d[:], g5d[:], s5d[:], ALU.mult)
                    BO("vector", "tensor_tensor", x5d[:], x5d[:], g5d[:], ALU.add,
                       sem=bld.vv)
                    bld.tick("vector", bld.ev)
                    bld.wait("sync", bld.vv)
                    bld.dma("sync", out_t[it * 128:(it + 1) * 128, :], x5d[:],
                            sem=s_out)
            bld.wait("sync", s_out)
            bld.barrier()
            ph5d.close()



        bld.flush()

        if dbg:
            for nm, dv in dbg.items():
                bld.dma("sync", dv, S[nm].ap(), sem=bld.dmb)
            bld.wait("sync", bld.dmb)
            bld.flush()

        if phase_limit < 12:
            bld.dma("sync", out_t, S["xn"].ap(), sem=bld.dmb)
            bld.wait("sync", bld.dmb)
            bld.flush()

    return nc


def _host_layout(inputs):
    g = lambda k: np.ascontiguousarray(np.asarray(inputs[k], dtype=np.float32))
    W_in = g("L0_attn_W_in")
    com = {}
    com["w_inT_cb"] = np.ascontiguousarray(
        W_in.reshape(32, 128, C).transpose(0, 2, 1))
    def perpart(v, f=1):
        return np.ascontiguousarray(
            v.reshape(NK, 128, f).transpose(1, 0, 2).reshape(128, NK * f))
    com["conv_w"] = perpart(g("L0_attn_conv_w").reshape(DI, 4), 4)
    com["conv_b"] = perpart(g("L0_attn_conv_b"))
    com["w_xT"] = perpart(np.ascontiguousarray(g("L0_attn_W_x").T), 33)
    com["w_dt"] = perpart(g("L0_attn_W_dt"))
    com["b_dt"] = perpart(g("L0_attn_b_dt"))
    com["a_log"] = g("L0_attn_A_log").reshape(1, DS)
    com["d_param"] = perpart(g("L0_attn_D_param"))
    com["ln_g"] = perpart(g("L0_attn_ln_g"))
    com["ln_b"] = perpart(g("L0_attn_ln_b"))
    com["w_outT"] = np.ascontiguousarray(g("L0_attn_W_out").T)
    com["clip_x"] = g("L0_ffn_clip_x").reshape(1, 1)
    com["clip_down"] = g("L0_ffn_clip_down").reshape(1, 1)
    com["w_gate0"] = g("L0_ffn_w_gate")
    com["w_up0"] = g("L0_ffn_w_up")
    com["w_down0"] = g("L0_ffn_w_down")
    com["q_hT"] = np.ascontiguousarray(
        g("L1_attn_latent_queries").reshape(L, H, DH).transpose(2, 1, 0).reshape(
            DH, H * L))
    com["w_kT"] = np.ascontiguousarray(g("L1_attn_W_k").T)
    com["w_vT"] = np.ascontiguousarray(g("L1_attn_W_v").T)
    com["w_gT"] = np.ascontiguousarray(g("L1_attn_W_g").T)
    com["w_pT"] = np.ascontiguousarray(g("L1_attn_W_p").T)
    com["w_gate1"] = g("L1_ffn_w_gate")
    com["w_up1"] = g("L1_ffn_w_up")
    com["w_down1"] = g("L1_ffn_w_down")
    com["identf"] = np.eye(128, dtype=np.float32)
    inv = (1.0 / (10000.0 ** (np.arange(0, DH, 2, dtype=np.float32) / DH)))
    ang = np.arange(T, dtype=np.float32)[:, None] * inv[None, :]
    com["cos_t"] = np.cos(ang).astype(np.float32)
    com["sin_t"] = np.sin(ang).astype(np.float32)
    causal = (np.arange(CS)[None, :] <= np.arange(L)[:, None])
    com["cmask"] = np.where(causal, 0.0, MASKVAL).astype(np.float32)
    return com


def kernel(**inputs):
    com = _host_layout(inputs)
    x = np.ascontiguousarray(np.asarray(inputs["L0_attn_x"], dtype=np.float32))
    nc = build_program()
    in_maps = []
    for c in range(NCORES):
        m = dict(com)
        m["x_in"] = np.ascontiguousarray(x[c])
        in_maps.append(m)
    res = run_bass_kernel_spmd(nc, in_maps, core_ids=list(range(NCORES)))
    return np.stack([np.asarray(r["out"], dtype=np.float32) for r in res.results])


if __name__ == "__main__":
    build_program(phase_limit=1)
    print("built")


# version 09
# speedup vs baseline: 1.1599x; 1.0558x over previous
import contextlib
import numpy as np

import concourse.bass as bass
import concourse.mybir as mybir
from concourse.bass_utils import run_bass_kernel_spmd

F32 = mybir.dt.float32
BF16 = mybir.dt.bfloat16
I32 = mybir.dt.int32
ALU = mybir.AluOpType
ACTF = mybir.ActivationFunctionType
AX = mybir.AxisListType

NCORES = 8
B, T, C = 8, 4096, 1024
DI, DS = 2048, 16
I0, I1 = 2816, 3072
H, L, DH, CS = 16, 64, 64, 128
NB0, NB1 = I0 // 128, I1 // 128
LN_EPS = 1e-5
EXP_NEG20 = float(np.float32(np.exp(np.float32(-20.0))))
NK = DI // 128
TCH = 256
TS = 32
NSUB = TCH // TS
NCH2 = T // TCH
PP = 128
ENGINES = ["sync", "vector", "scalar", "tensor", "gpsimd"]
MASKVAL = -1e30
MAGIC = 12582912.0
SKIP_STAGES = set()


class Sem:
    def __init__(self, nc, name):
        self.h = nc.semaphore(name).__enter__()
        self.n = 0


class Builder:
    def __init__(self, nc):
        self.nc = nc
        self.items = []
        self._slots = {}
        for nm in ["dm", "dmb", "mm", "ev", "vv", "gg", "cc", "bar"]:
            setattr(self, nm, Sem(nc, "s_" + nm))

    def slot(self, name):
        if name not in self._slots:
            self._slots[name] = Sem(self.nc, "sl_" + name)
        return self._slots[name]

    def tick(self, eng, sem):
        sem.n += 1

        def fn(e, sem=sem):
            e.nop().then_inc(sem.h, 1)
        self.items.append((eng, fn))

    def raw(self, eng, fn):
        self.items.append((eng, fn))

    def wait(self, eng, sem, at=None):
        v = sem.n if at is None else at
        if v > 0:
            self.items.append((eng, lambda e: e.wait_ge(sem.h, v)))

    def op(self, eng, meth, *args, sem=None, inc=1, **kw):
        s = sem
        if s is not None:
            s.n += inc

        def fn(e):
            i = getattr(e, meth)(*args, **kw)
            if s is not None:
                i.then_inc(s.h, inc)
        self.items.append((eng, fn))

    def dma(self, eng, out, in_, sem=None, **kw):
        s = sem if sem is not None else self.dm
        s.n += 16

        def fn(e):
            e.dma_start(out=out, in_=in_, **kw).then_inc(s.h, 16)
        self.items.append((eng, fn))

    def flush(self):
        nc = self.nc
        items, self.items = self.items, []
        if not items:
            return
        per = {e: [fn for (en, fn) in items if en == e] for e in ENGINES}
        with nc.Block() as blk:
            for e in ENGINES:
                fns = per[e]
                if not fns:
                    continue

                def mk(fns):
                    def emit(eng):
                        for fn in fns:
                            fn(eng)
                    return emit
                getattr(blk, e)(mk(fns))

    def barrier(self):
        self.bar.n += len(ENGINES)
        tgt = self.bar.n
        for e in ENGINES:
            def fn(eng, tgt=tgt):
                eng.nop().then_inc(self.bar.h, 1)
                eng.wait_ge(self.bar.h, tgt)
            self.items.append((e, fn))
        self.flush()


def build_program(phase_limit=99, debug_scr=()):
    nc = bass.Bass("TRN2", target_bir_lowering=False, detect_race_conditions=False)
    D, S = {}, {}

    def din(name, shape):
        D[name] = nc.dram_tensor(name, list(shape), F32, kind="ExternalInput").ap()

    din("x_in", (T, C))
    din("w_inT_cb", (2 * DI // 128, C, 128))
    din("conv_w", (128, NK * 4))
    din("conv_b", (128, NK))
    din("w_xT", (128, NK * 33))
    din("w_dt", (128, NK))
    din("b_dt", (128, NK))
    din("a_log", (1, DS))
    din("d_param", (128, NK))
    din("ln_g", (128, NK))
    din("ln_b", (128, NK))
    din("w_outT", (DI, C))
    din("clip_x", (1, 1))
    din("clip_down", (1, 1))
    din("w_gate0", (I0, C))
    din("w_up0", (I0, C))
    din("w_down0", (C, I0))
    din("q_hT", (64, H * L))
    din("w_kT", (C, C))
    din("w_vT", (C, C))
    din("w_gT", (C, C))
    din("w_pT", (C, C))
    din("w_gate1", (I1, C))
    din("w_up1", (I1, C))
    din("w_down1", (C, I1))
    din("identf", (128, 128))
    din("cos_t", (T, DH // 2))
    din("sin_t", (T, DH // 2))
    din("cmask", (L, CS))

    out_t = nc.dram_tensor("out", [T, C], F32, kind="ExternalOutput").ap()
    dbg = {}
    for nm in debug_scr:
        pass

    def scr(name, shape, dtype=F32):
        S[name] = nc.dram_tensor("scr_" + name, list(shape), dtype)

    scr("xn", (T, C))
    scr("xnT_bf", (C, T), BF16)
    scr("xc", (DI, T))
    scr("z", (DI, T))
    scr("prod_bf", (DI, T), BF16)
    scr("s0", (T, C))
    scr("gu_bf", (I0, T), BF16)
    scr("x1", (T, C))
    scr("xn1", (T, C))
    scr("xn1T_bf", (C, T), BF16)
    scr("gatesT_bf", (C, T), BF16)
    scr("s1", (T, C))
    scr("upr_bf", (I1, T), BF16)
    scr("gater_bf", (I1, T), BF16)
    scr("t2_bf", (I1, T), BF16)
    scr("t3_bf", (I1, T), BF16)
    scr("y_dbg", (DI, T))
    scr("xq_dbg", (T, C))
    scr("qs_dbg", (128, 32))
    scr("cc_dbg", (4, 128))
    scr("lmax_ph0c", (128, 32))
    scr("g0_dbg", (T, C))
    scr("qs2_dbg", (128, 32))
    scr("w0_dbg", (128, 2048))
    scr("winT", (2 * DI // 128, C, 128), BF16)
    scr("woutT", (DI, C), BF16)
    scr("wkT", (C, C), BF16)
    scr("wvT", (C, C), BF16)
    scr("wgT", (C, C), BF16)
    scr("wpT", (C, C), BF16)
    scr("wq_g0T", (NB0, C, 128), BF16)
    scr("wq_u0T", (NB0, C, 128), BF16)
    scr("wq_d0T", (I0, C), BF16)
    scr("wq_g1T", (NB1, C, 128), BF16)
    scr("wq_u1T", (NB1, C, 128), BF16)
    scr("wq_d1T", (I1, C), BF16)
    NCC = 7
    scr("cc_in", (NCC, 128))
    scr("cc_out", (NCC, 128))

    for nm in debug_scr:
        shp = list(S[nm].shape) if nm in S else None
        dt_ = S[nm].dtype if nm in S else F32
        dbg[nm] = nc.dram_tensor("dbg_" + nm, shp, dt_, kind="ExternalOutput").ap()

    bld = Builder(nc)
    es = contextlib.ExitStack()
    with es:
        def sb(name, shape, dtype=F32, st=None):
            return (st or es).enter_context(nc.sbuf_tensor(name, list(shape), dtype))

        def psb(name, shape, dtype=F32, st=None):
            return (st or es).enter_context(nc.psum_tensor(name, list(shape), dtype))

        ones_f = sb("ones_f", [PP, 1])
        ident_b = sb("ident_b", [PP, 128], BF16)
        conv_w_t = sb("conv_w_t", [PP, NK * 4])
        conv_b_t = sb("conv_b_t", [PP, NK])
        w_dt_t = sb("w_dt_t", [PP, NK])
        b_dt_t = sb("b_dt_t", [PP, NK])
        d_par_t = sb("d_par_t", [PP, NK])
        ln_g_t = sb("ln_g_t", [PP, NK])
        ln_b_t = sb("ln_b_t", [PP, NK])
        a_cols = sb("a_cols", [PP, DS])
        w_xT_b = sb("w_xT_b", [PP, NK * 33], BF16)
        qh_b = sb("qh_b", [64, H * L], BF16)
        cmask_t = sb("cmask_t", [L, CS])
        gmax = sb("gmax", [PP, 8])
        lmax = sb("lmax", [PP, 8])
        qs = sb("qs", [PP, 16])
        row128 = sb("row128", [1, 136])
        ps_util = psb("ps_util", [PP, 512])

        def BO(e, meth, *a, **kw):
            bld.op(e, meth, *a, **kw)

        def global_max(slot, cc_id):
            cin, cout = S["cc_in"].ap(), S["cc_out"].ap()
            bld.wait("gpsimd", bld.vv)
            bld.dma("gpsimd", cin[cc_id, :].unsqueeze(0), lmax[:, slot:slot + 1],
                    sem=bld.gg, allow_slow_non_contiguous=True)
            bld.wait("gpsimd", bld.gg)
            bld.op("gpsimd", "collective_compute", "AllReduce", ALU.max,
                   replica_groups=[list(range(NCORES))],
                   ins=[cin[cc_id:cc_id + 1, :]],
                   outs=[cout[cc_id:cc_id + 1, :]], sem=bld.cc)
            bld.wait("gpsimd", bld.cc)
            bld.dma("gpsimd", row128[0:1, 0:128], cout[cc_id:cc_id + 1, :],
                    sem=bld.gg)
            bld.wait("vector", bld.gg)
            BO("vector", "tensor_reduce", row128[0:1, 128:129], row128[0:1, 0:128],
               AX.X, ALU.max, sem=bld.vv)
            bld.wait("tensor", bld.vv)
            BO("tensor", "matmul", ps_util[:, 0:1],
               ones_f[0:1, 0:1].broadcast_to([1, 128]), row128[0:1, 128:129],
               start=True, stop=True, sem=bld.mm)
            bld.wait("scalar", bld.mm)
            BO("scalar", "copy", gmax[:, slot:slot + 1], ps_util[:, 0:1], sem=bld.ev)

        def derive_scale(slot, qmaxv, sig_col, inv_col, s_col, floor=None):
            bld.wait("vector", bld.ev)
            if sig_col is not None:
                BO("vector", "tensor_scalar", qs[:, s_col:s_col + 1],
                   gmax[:, slot:slot + 1], qs[:, sig_col:sig_col + 1], 1.0 / qmaxv,
                   ALU.mult, ALU.mult, sem=bld.vv)
            else:
                BO("vector", "tensor_scalar", qs[:, s_col:s_col + 1],
                   gmax[:, slot:slot + 1], 1.0 / qmaxv, None, ALU.mult, sem=bld.vv)
            bld.wait("vector", bld.vv)
            if floor is not None:
                BO("vector", "tensor_scalar", qs[:, s_col:s_col + 1],
                   qs[:, s_col:s_col + 1], floor, None, ALU.max, sem=bld.vv)
                bld.wait("vector", bld.vv)
            BO("vector", "reciprocal", qs[:, inv_col:inv_col + 1],
               qs[:, s_col:s_col + 1], sem=bld.vv)

        def emit_quant(dst_bf, src, inv_col, s_col, qhi, qlo, itmp, ftmp):
            BO("vector", "tensor_scalar", ftmp, src, qs[:, inv_col:inv_col + 1],
               float(qhi), ALU.mult, ALU.min)
            BO("vector", "tensor_scalar", itmp, ftmp, float(qlo), None, ALU.max)
            BO("vector", "tensor_scalar", dst_bf, itmp, qs[:, s_col:s_col + 1],
               None, ALU.mult)

        def emit_ln_tile(xin, xout_f32, xout_bf, stat, dump, F, absmax_slot=None):
            bld.wait("scalar", bld.vv)
            BO("scalar", "activation", xout_f32, xin, ACTF.Copy,
               accum_out=stat[:, 0:1], sem=bld.ev)
            bld.wait("vector", bld.ev)
            BO("vector", "tensor_scalar", stat[:, 0:1], stat[:, 0:1], 1.0 / F, None,
               ALU.mult, sem=bld.vv)
            bld.wait("vector", bld.vv)
            BO("vector", "tensor_scalar", xout_f32, xout_f32, stat[:, 0:1], None,
               ALU.subtract, sem=bld.vv)
            bld.wait("scalar", bld.vv)
            BO("scalar", "activation", dump, xout_f32, ACTF.Square,
               accum_out=stat[:, 1:2], sem=bld.ev)
            bld.wait("vector", bld.ev)
            BO("vector", "tensor_scalar", stat[:, 1:2], stat[:, 1:2], 1.0 / F, LN_EPS,
               ALU.mult, ALU.add, sem=bld.vv)
            bld.wait("scalar", bld.vv)
            BO("scalar", "activation", stat[:, 2:3], stat[:, 1:2], ACTF.Sqrt, sem=bld.ev)
            bld.wait("vector", bld.ev)
            BO("vector", "reciprocal", stat[:, 3:4], stat[:, 2:3], sem=bld.vv)
            bld.wait("vector", bld.vv)
            BO("vector", "tensor_scalar", xout_f32, xout_f32, stat[:, 3:4], None,
               ALU.mult, sem=bld.vv)
            if xout_bf is not None:
                BO("vector", "tensor_copy", xout_bf, xout_f32, sem=bld.vv)
            if absmax_slot is not None:
                BO("vector", "tensor_reduce", stat[:, 5:6], xout_f32, AX.X, ALU.max,
                   apply_absolute_value=True, sem=bld.vv)
                bld.wait("vector", bld.vv)
                BO("vector", "tensor_tensor", lmax[:, absmax_slot:absmax_slot + 1],
                   lmax[:, absmax_slot:absmax_slot + 1], stat[:, 5:6], ALU.max,
                   sem=bld.vv)

        ph0 = contextlib.ExitStack()
        idf = sb("idf", [PP, 128], F32, ph0)
        arow = sb("arow", [1, DS], F32, ph0)
        crow = sb("crow", [1, 2], F32, ph0)
        wx_st = sb("wx_st", [PP, NK * 33], F32, ph0)
        qh_st = sb("qh_st", [64, H * L], F32, ph0)

        bld.dma("sync", conv_w_t[:], D["conv_w"])
        bld.dma("sync", conv_b_t[:], D["conv_b"])
        bld.dma("sync", w_dt_t[:], D["w_dt"])
        bld.dma("sync", b_dt_t[:], D["b_dt"])
        bld.dma("sync", d_par_t[:], D["d_param"])
        bld.dma("sync", ln_g_t[:], D["ln_g"])
        bld.dma("sync", ln_b_t[:], D["ln_b"])
        bld.dma("sync", wx_st[:], D["w_xT"])
        bld.dma("sync", qh_st[:], D["q_hT"])
        bld.dma("sync", cmask_t[:], D["cmask"])
        bld.dma("sync", idf[:], D["identf"])
        bld.dma("sync", arow[:], D["a_log"])
        bld.dma("sync", crow[0:1, 0:1], D["clip_x"])
        bld.dma("sync", crow[0:1, 1:2], D["clip_down"])
        bld.wait("vector", bld.dm)
        BO("vector", "memset", ones_f[:], 1.0)
        BO("vector", "memset", lmax[:], 0.0)
        BO("vector", "tensor_scalar", arow[:], arow[:], 5.0, None, ALU.min, sem=bld.vv)
        bld.wait("scalar", bld.vv)
        bld.wait("scalar", bld.dm)
        BO("scalar", "copy", ident_b[:], idf[:])
        BO("scalar", "copy", w_xT_b[:], wx_st[:])
        BO("scalar", "copy", qh_b[:], qh_st[:])
        BO("scalar", "activation", arow[:], arow[:], ACTF.Exp)
        BO("scalar", "activation", crow[:], crow[:], ACTF.Sigmoid, sem=bld.ev)
        bld.wait("vector", bld.ev)
        BO("vector", "tensor_scalar", arow[:], arow[:], -1.0, None, ALU.mult, sem=bld.vv)
        bld.wait("tensor", bld.vv)
        BO("tensor", "matmul", ps_util[:, 0:DS],
           ones_f[0:1, 0:1].broadcast_to([1, 128]), arow[0:1, :],
           start=True, stop=True)
        BO("tensor", "matmul", ps_util[:, 32:34],
           ones_f[0:1, 0:1].broadcast_to([1, 128]), crow[0:1, :],
           start=True, stop=True, sem=bld.mm)
        bld.wait("scalar", bld.mm)
        BO("scalar", "copy", a_cols[:], ps_util[:, 0:DS])
        BO("scalar", "copy", qs[:, 14:16], ps_util[:, 32:34], sem=bld.ev)
        bld.barrier()
        ph0.close()

        ph0b = contextlib.ExitStack()
        wst = [sb(f"wst{i}", [PP, 512], F32, ph0b) for i in range(2)]
        wbt = [sb(f"wbt{i}", [PP, 512], BF16, ph0b) for i in range(2)]
        s_wc = [bld.slot("wc0"), bld.slot("wc1")]
        step = 0

        def cast_stream(pairs):
            nonlocal step
            for (srcv, dstv) in pairs:
                sl = step % 2
                bld.wait("sync", s_wc[sl])
                bld.dma("sync", wst[sl][:, 0:512], srcv)
                bld.wait("scalar", bld.dm)
                BO("scalar", "copy", wbt[sl][:, 0:512], wst[sl][:, 0:512], sem=bld.ev)
                bld.wait("sync", bld.ev)
                bld.dma("sync", dstv, wbt[sl][:, 0:512], sem=s_wc[sl])
                step += 1

        pairs = []
        for eb in range(2 * DI // 128):
            for q in range(2):
                pairs.append((
                    D["w_inT_cb"][eb, q * 512:(q + 1) * 512, :].rearrange(
                        "(a p) m -> p a m", p=128),
                    S["winT"].ap()[eb, q * 512:(q + 1) * 512, :].rearrange(
                        "(a p) m -> p a m", p=128)))
        for (src2, dst2, K2, M2) in [("w_outT", "woutT", DI, C), ("w_kT", "wkT", C, C),
                                     ("w_vT", "wvT", C, C), ("w_gT", "wgT", C, C),
                                     ("w_pT", "wpT", C, C)]:
            for kt in range(K2 // 128):
                for mh in range(M2 // 512):
                    pairs.append((
                        D[src2][kt * 128:(kt + 1) * 128, mh * 512:(mh + 1) * 512],
                        S[dst2].ap()[kt * 128:(kt + 1) * 128, mh * 512:(mh + 1) * 512]))
        if phase_limit >= 2 and "ph0b" not in SKIP_STAGES:
            cast_stream(pairs)
        bld.barrier()
        ph0b.close()

        ph0c = contextlib.ExitStack()
        wq_st = sb("wq_st", [PP, I1], F32, ph0c)
        wq_i = sb("wq_i", [PP, I1], I32, ph0c)
        wq_b = sb("wq_b", [PP, I1], BF16, ph0c)
        wq_m = sb("wq_m", [PP, 96], F32, ph0c)
        wq_inv = sb("wq_inv", [PP, 96], F32, ph0c)
        tr_o = sb("tr_o", [PP, 512], BF16, ph0c)
        ps_tr = psb("ps_tr", [PP, 512], BF16, ph0c)
        s_q_store = bld.slot("q_store")

        def quant_weight(src2, dst2, rows, K2, lpbq, dst_cb):
            qmaxv = 7.0 if lpbq else 127.0
            qlov = -8.0 if lpbq else -128.0
            for rt in range(rows // 128):
                bld.wait("sync", bld.mm)
                bld.wait("sync", bld.vv)
                bld.dma("sync", wq_st[:, 0:K2], D[src2][rt * 128:(rt + 1) * 128, :])
                bld.wait("vector", bld.dm)
                if lpbq:
                    nb = K2 // 32
                    wv3 = wq_st[:, 0:K2].rearrange("p (b i) -> p b i", i=32)
                    if "qv_red" not in SKIP_STAGES:
                        BO("vector", "tensor_reduce", wq_m[:, 0:nb], wv3, AX.X,
                           ALU.max, apply_absolute_value=True, sem=bld.vv)
                        bld.wait("vector", bld.vv)
                        BO("vector", "tensor_scalar", wq_m[:, 0:nb], wq_m[:, 0:nb],
                           1.0 / qmaxv, 1e-12, ALU.mult, ALU.max, sem=bld.vv)
                        bld.wait("vector", bld.vv)
                        BO("vector", "reciprocal", wq_inv[:, 0:nb], wq_m[:, 0:nb],
                           sem=bld.vv)
                        bld.wait("vector", bld.vv)
                    iexp = wq_inv[:, 0:nb].unsqueeze(2).broadcast_to([PP, nb, 32])
                    mexp = wq_m[:, 0:nb].unsqueeze(2).broadcast_to([PP, nb, 32])
                    if "qv_tt1" not in SKIP_STAGES:
                        BO("vector", "tensor_tensor", wv3, wv3, iexp, ALU.mult)
                    if "qv_clip" not in SKIP_STAGES:
                        BO("vector", "tensor_scalar", wq_st[:, 0:K2], wq_st[:, 0:K2],
                           qmaxv, qlov, ALU.min, ALU.max)
                        BO("vector", "tensor_scalar", wq_st[:, 0:K2], wq_st[:, 0:K2],
                           MAGIC, MAGIC, ALU.add, ALU.subtract)
                    if "qv_tt2" not in SKIP_STAGES:
                        BO("vector", "tensor_tensor",
                           wq_b[:, 0:K2].rearrange("p (b i) -> p b i", i=32),
                           wv3, mexp, ALU.mult, sem=bld.vv)
                    else:
                        BO("vector", "tensor_copy", wq_b[:, 0:K2], wq_st[:, 0:K2],
                           sem=bld.vv)
                else:
                    BO("vector", "tensor_reduce", wq_m[:, 0:1], wq_st[:, 0:K2],
                       AX.X, ALU.max, apply_absolute_value=True, sem=bld.vv)
                    bld.wait("vector", bld.vv)
                    BO("vector", "tensor_scalar", wq_m[:, 0:1], wq_m[:, 0:1],
                       1.0 / qmaxv, None, ALU.mult, sem=bld.vv)
                    bld.wait("vector", bld.vv)
                    BO("vector", "reciprocal", wq_inv[:, 0:1], wq_m[:, 0:1],
                       sem=bld.vv)
                    bld.wait("vector", bld.vv)
                    BO("vector", "tensor_scalar", wq_st[:, 0:K2], wq_st[:, 0:K2],
                       wq_inv[:, 0:1], qmaxv, ALU.mult, ALU.min)
                    BO("vector", "tensor_scalar", wq_st[:, 0:K2], wq_st[:, 0:K2],
                       qlov, MAGIC, ALU.max, ALU.add)
                    BO("vector", "tensor_scalar", wq_st[:, 0:K2], wq_st[:, 0:K2],
                       MAGIC, None, ALU.subtract)
                    BO("vector", "tensor_scalar", wq_b[:, 0:K2], wq_st[:, 0:K2],
                       wq_m[:, 0:1], None, ALU.mult, sem=bld.vv)
                for kt in range(K2 // 128):
                    q = kt % 4
                    bld.wait("tensor", bld.vv)
                    bld.wait("tensor", bld.ev)
                    BO("tensor", "transpose", ps_tr[:, q * 128:(q + 1) * 128],
                       wq_b[:, kt * 128:(kt + 1) * 128], ident_b[:], sem=bld.mm)
                    bld.wait("scalar", bld.mm)
                    bld.wait("scalar", s_q_store)
                    BO("scalar", "copy", tr_o[:, q * 128:(q + 1) * 128],
                       ps_tr[:, q * 128:(q + 1) * 128], sem=bld.ev)
                    bld.wait("sync", bld.ev)
                    if dst_cb:
                        dview = S[dst2].ap()[rt, kt * 128:(kt + 1) * 128, :]
                    else:
                        dview = S[dst2].ap()[kt * 128:(kt + 1) * 128,
                                             rt * 128:(rt + 1) * 128]
                    bld.dma("sync", dview, tr_o[:, q * 128:(q + 1) * 128],
                            sem=s_q_store)

        if phase_limit >= 2 and "ph0c" not in SKIP_STAGES:
            qw_calls = [("w_gate0", "wq_g0T", I0, C, False, True),
                        ("w_up0", "wq_u0T", I0, C, False, True),
                        ("w_down0", "wq_d0T", C, I0, False, False),
                        ("w_gate1", "wq_g1T", I1, C, True, True),
                        ("w_up1", "wq_u1T", I1, C, True, True),
                        ("w_down1", "wq_d1T", C, I1, True, False)]
            for qi, qargs in enumerate(qw_calls):
                if f"qw{qi}" not in SKIP_STAGES:
                    quant_weight(*qargs)
        bld.barrier()
        if "lmax_ph0c" in dbg:
            sd = S["lmax_ph0c"].ap()
            bld.dma("sync", sd[:, 0:8], gmax[:, 0:8])
            bld.dma("sync", sd[:, 8:16], lmax[:, 0:8])
            bld.dma("sync", sd[:, 16:32], qs[:, 0:16], sem=bld.dm)
            bld.wait("sync", bld.dm)
        ph0c.close()

        if phase_limit >= 1:
            ph1 = contextlib.ExitStack()
            xt = sb("p1x", [PP, C], F32, ph1)
            xnt = sb("p1n", [PP, C], F32, ph1)
            xnb = sb("p1b", [PP, C], BF16, ph1)
            stat = sb("p1s", [PP, 8], F32, ph1)
            trb = sb("p1t", [PP, C], BF16, ph1)
            dmp = sb("p1d", [PP, C], F32, ph1)
            ps_t1 = psb("p1p", [PP, C], BF16, ph1)
            s_p1 = bld.slot("p1st")
            for it in range(T // 128):
                bld.wait("sync", s_p1)
                bld.wait("sync", bld.vv)
                bld.wait("sync", bld.ev)
                bld.dma("sync", xt[:], D["x_in"][it * 128:(it + 1) * 128, :])
                bld.wait("scalar", bld.dm)
                bld.wait("vector", bld.dm)
                emit_ln_tile(xt[:], xnt[:], xnb[:], stat, dmp[:], C, absmax_slot=0)
                bld.wait("sync", bld.vv)
                bld.dma("sync", S["xn"].ap()[it * 128:(it + 1) * 128, :], xnt[:],
                        sem=s_p1)
                bld.wait("tensor", bld.vv)
                bld.wait("tensor", bld.ev)
                for cb in range(8):
                    BO("tensor", "transpose", ps_t1[:, cb * 128:(cb + 1) * 128],
                       xnb[:, cb * 128:(cb + 1) * 128], ident_b[:],
                       sem=bld.mm if cb == 7 else None)
                bld.wait("scalar", bld.mm)
                BO("scalar", "copy", trb[:], ps_t1[:], sem=bld.ev)
                bld.wait("sync", bld.ev)
                for cb in range(8):
                    bld.dma("sync", S["xnT_bf"].ap()[cb * 128:(cb + 1) * 128,
                                                     it * 128:(it + 1) * 128],
                            trb[:, cb * 128:(cb + 1) * 128], sem=s_p1)
            global_max(0, 0)
            derive_scale(0, 127.0, 14, 0, 1)
            if "qs_dbg" in dbg:
                bld.wait("sync", bld.vv)
                bld.wait("sync", bld.ev)
                bld.dma("sync", S["qs_dbg"].ap()[:, 0:16], qs[:, 0:16])
                bld.dma("sync", S["qs_dbg"].ap()[:, 16:24], gmax[:, 0:8])
                bld.dma("sync", S["qs_dbg"].ap()[:, 24:32], lmax[:, 0:8])
            if "cc_dbg" in dbg:
                bld.wait("sync", bld.vv)
                bld.dma("sync", S["cc_dbg"].ap()[0:1, :], S["cc_in"].ap()[0:1, :])
                bld.dma("sync", S["cc_dbg"].ap()[1:2, :], S["cc_out"].ap()[0:1, :])
                bld.dma("sync", S["cc_dbg"].ap()[2:3, :], row128[0:1, 0:128])
            bld.barrier()
            ph1.close()


        if phase_limit >= 2 and "ph2a" not in SKIP_STAGES:
            ph2a = contextlib.ExitStack()
            wbuf = [sb(f"a2w{i}", [PP, 8 * 128], BF16, ph2a) for i in range(2)]
            xbuf = [sb(f"a2x{i}", [PP, 8 * 512], BF16, ph2a) for i in range(2)]
            est = [sb(f"a2e{i}", [PP, 512], F32, ph2a) for i in range(2)]
            ps2 = [psb(f"a2p{i}", [PP, 512], F32, ph2a) for i in range(2)]
            s_wb = [bld.slot("a2wb0"), bld.slot("a2wb1")]
            s_xb = [bld.slot("a2xb0"), bld.slot("a2xb1")]
            s_es = [bld.slot("a2es0"), bld.slot("a2es1")]
            s_pb = [bld.slot("a2pb0"), bld.slot("a2pb1")]
            for tt in range(T // 512):
                xsl = tt % 2
                bld.wait("sync", s_xb[xsl])
                bld.dma("sync", xbuf[xsl][:],
                        S["xnT_bf"].ap()[:, tt * 512:(tt + 1) * 512].rearrange(
                            "(q p) m -> p q m", p=128))
                for eb in range(2 * DI // 128):
                    wsl = eb % 2
                    psl = eb % 2
                    bld.wait("sync", s_wb[wsl])
                    bld.dma("sync", wbuf[wsl][:],
                            S["winT"].ap()[eb].rearrange("(q p) m -> p q m", p=128))
                    bld.wait("tensor", bld.dm)
                    bld.wait("tensor", s_pb[psl])
                    for cb in range(8):
                        BO("tensor", "matmul", ps2[psl][:],
                           wbuf[wsl][:, cb * 128:(cb + 1) * 128],
                           xbuf[xsl][:, cb * 512:(cb + 1) * 512],
                           start=(cb == 0), stop=(cb == 7),
                           sem=bld.mm if cb == 7 else None)
                    bld.tick("tensor", s_wb[wsl])
                    if eb == 2 * DI // 128 - 1:
                        bld.tick("tensor", s_xb[xsl])
                    esl = eb % 2
                    bld.wait("scalar", bld.mm)
                    bld.wait("scalar", s_es[esl])
                    BO("scalar", "copy", est[esl][:], ps2[psl][:], sem=bld.ev)
                    bld.tick("scalar", s_pb[psl])
                    bld.wait("sync", bld.ev)
                    dstb = S["xc"] if eb < NK else S["z"]
                    er = eb if eb < NK else eb - NK
                    bld.dma("sync", dstb.ap()[er * 128:(er + 1) * 128,
                                              tt * 512:(tt + 1) * 512],
                            est[esl][:], sem=s_es[esl])
            bld.barrier()
            ph2a.close()


        if phase_limit >= 3:
            ph2b = contextlib.ExitStack()
            xc_t = sb("xc_t", [PP, NK * (TCH + 3)], F32, ph2b)
            u_t = sb("u_t", [PP, NK * TCH], F32, ph2b)
            ub_t = sb("ub_t", [PP, NK * TCH], BF16, ph2b)
            zs_t = sb("zs_t", [PP, NK * TCH], F32, ph2b)
            dt_t = sb("dt_t", [PP, NK * TCH], F32, ph2b)
            xdT = sb("xdT", [33, TCH], F32, ph2b)
            xdf0 = sb("xdf0", [1, TCH], F32, ph2b)
            xdfs = sb("xdfs", [1, 33 * TS], F32, ph2b)
            dA2 = [sb(f"dA{i}", [PP, TS * 256], F32, ph2b) for i in range(2)]
            dBu = sb("dBu", [PP, 16 * 256], F32, ph2b)
            hist = sb("hist", [PP, 17 * 256], F32, ph2b)
            hcs = sb("hcs", [PP, 8 * 256], F32, ph2b)
            brep = sb("brep", [PP, DS * TS], F32, ph2b)
            crep2 = [sb(f"crep{i}", [PP, DS * TS], F32, ph2b) for i in range(2)]
            dus = sb("dus", [PP, NK * TS], F32, ph2b)
            ysub = sb("ysub", [PP, TS * NK], F32, ph2b)
            yrow = sb("yrow", [1, 4 * TS], F32, ph2b)
            prodb = sb("prodb", [PP, NK * TCH], BF16, ph2b)
            cm100 = sb("cm100", [PP, 4], F32, ph2b)
            ps_xd = psb("ps_xd", [33, TCH], F32, ph2b)
            ps_dr = psb("ps_dr", [PP, TCH], F32, ph2b)
            ps_bc2 = psb("ps_bc2", [PP, DS * TS], F32, ph2b)
            ps_row = psb("ps_row", [1, 2 * TS], F32, ph2b)
            ps_mu = psb("ps_mu", [PP, 2 * TS], F32, ph2b)
            hist_last = hist[:, 16 * 256:17 * 256]

            s_gd = bld.slot("b2_dbu")
            s_vs = bld.slot("b2_stepsdone")
            s_chv = bld.slot("b2_chv")
            s_chs = bld.slot("b2_chs")
            s_prod = bld.slot("b2_prod")

            BO("vector", "memset", hist[:], 0.0)
            BO("vector", "memset", cm100[:, 0:1], -100.0)
            BO("vector", "memset", cm100[:, 1:2], 100.0)
            BO("vector", "memset", cm100[:, 2:3], EXP_NEG20, sem=bld.vv)
            cm100b = cm100[:, 0:1].broadcast_to([PP, 256])
            c100b = cm100[:, 1:2].broadcast_to([PP, 256])
            e20b = cm100[:, 2:3].broadcast_to([PP, TS * 256])

            xcv = lambda k: xc_t[:, k * (TCH + 3):(k + 1) * (TCH + 3)]
            uv = lambda k: u_t[:, k * TCH:(k + 1) * TCH]
            ubv = lambda k: ub_t[:, k * TCH:(k + 1) * TCH]
            zv = lambda k: zs_t[:, k * TCH:(k + 1) * TCH]
            dtv = lambda k: dt_t[:, k * TCH:(k + 1) * TCH]
            dt3 = dt_t[:].rearrange("p (c t) -> p t c", c=NK)
            u3 = u_t[:].rearrange("p (c t) -> p t c", c=NK)
            ysub3 = ysub[:].rearrange("p (t c) -> p t c", t=TS)
            ysq = hcs[:, 0:TS * NK]
            ysq3 = ysq.rearrange("p (t c) -> p t c", t=TS)

            for ch in range(NCH2):
                t0 = ch * TCH
                bld.wait("sync", s_chv)
                bld.wait("sync", s_chs)
                for k in range(NK):
                    if ch == 0:
                        bld.dma("sync", xcv(k)[:, 3:],
                                S["xc"].ap()[k * 128:(k + 1) * 128, t0:t0 + TCH])
                    else:
                        bld.dma("sync", xcv(k)[:, :],
                                S["xc"].ap()[k * 128:(k + 1) * 128, t0 - 3:t0 + TCH])
                    bld.dma("sync", zv(k)[:, :],
                            S["z"].ap()[k * 128:(k + 1) * 128, t0:t0 + TCH])
                bld.wait("vector", bld.dm)
                if ch == 0:
                    for k in range(NK):
                        BO("vector", "memset", xcv(k)[:, 0:3], 0.0)
                for k in range(NK):
                    w = lambda j: conv_w_t[:, k * 4 + j:k * 4 + j + 1]
                    BO("vector", "tensor_scalar", uv(k), xcv(k)[:, 0:TCH], w(0), None,
                       ALU.mult)
                    for j in range(1, 4):
                        BO("vector", "scalar_tensor_tensor", uv(k), xcv(k)[:, j:j + TCH],
                           w(j), uv(k), ALU.mult, ALU.add,
                           sem=bld.vv if j == 3 else None)
                bld.wait("scalar", bld.vv)
                bld.wait("scalar", bld.dm)
                for k in range(NK):
                    BO("scalar", "activation", uv(k), uv(k), ACTF.Silu,
                       bias=conv_b_t[:, k:k + 1], scale=1.0)
                    BO("scalar", "copy", ubv(k), uv(k))
                    BO("scalar", "activation", zv(k), zv(k), ACTF.Silu,
                       sem=bld.ev if k == NK - 1 else None)
                bld.wait("tensor", bld.ev)
                for k in range(NK):
                    BO("tensor", "matmul", ps_xd[:, :],
                       w_xT_b[:, k * 33:(k + 1) * 33], ubv(k),
                       start=(k == 0), stop=(k == NK - 1),
                       sem=bld.mm if k == NK - 1 else None)
                bld.wait("scalar", bld.mm)
                BO("scalar", "copy", xdT[:, :], ps_xd[:, :], sem=bld.ev)
                bld.wait("sync", bld.ev)
                bld.dma("sync", xdf0[:], xdT[0:1, :])
                bld.wait("tensor", bld.dm)
                BO("tensor", "matmul", ps_dr[:, :],
                   ones_f[0:1, 0:1].broadcast_to([1, 128]), xdf0[0:1, :],
                   start=True, stop=True, sem=bld.mm)
                bld.wait("scalar", bld.mm)
                for k in range(NK):
                    BO("scalar", "activation", dtv(k), ps_dr[:, :], ACTF.Exp,
                       bias=b_dt_t[:, k:k + 1], scale=w_dt_t[:, k:k + 1])
                    BO("scalar", "activation", dtv(k), dtv(k), ACTF.Ln, bias=1.0,
                       sem=bld.ev if k == NK - 1 else None)
                bld.wait("vector", bld.ev)
                for k in range(NK):
                    BO("vector", "tensor_scalar", dtv(k), dtv(k), 1e-4, 10.0,
                       ALU.max, ALU.min, sem=bld.vv if k == NK - 1 else None)

                def emit_prep(s):
                    emit_prep_bc(s)
                    emit_prep_da(s)

                def emit_prep_bc(s):
                    ts0 = s * TS
                    gs = ch * NSUB + s
                    bld.wait("sync", bld.mm)
                    bld.dma("sync", xdfs[0:1, :], xdT[:, ts0:ts0 + TS],
                            allow_slow_non_contiguous=True, sem=bld.dm)
                    bld.wait("tensor", bld.dm)
                    bld.wait("tensor", bld.ev)
                    for n in range(DS):
                        BO("tensor", "matmul", ps_bc2[:, n * TS:(n + 1) * TS],
                           ones_f[0:1, 0:1].broadcast_to([1, 128]),
                           xdfs[0:1, (1 + n) * TS:(2 + n) * TS],
                           start=True, stop=True,
                           sem=bld.mm if n == DS - 1 else None)
                    bld.wait("scalar", bld.mm)
                    bld.wait("scalar", s_gd)
                    if s == 0:
                        bld.wait("scalar", bld.vv)
                    BO("scalar", "copy", brep[:], ps_bc2[:], sem=bld.ev)
                    bld.wait("tensor", bld.ev)
                    for n in range(DS):
                        BO("tensor", "matmul", ps_bc2[:, n * TS:(n + 1) * TS],
                           ones_f[0:1, 0:1].broadcast_to([1, 128]),
                           xdfs[0:1, (17 + n) * TS:(18 + n) * TS],
                           start=True, stop=True,
                           sem=bld.mm if n == DS - 1 else None)
                    bld.wait("scalar", bld.mm)
                    BO("scalar", "copy", crep2[s % 2][:], ps_bc2[:], sem=bld.ev)

                def emit_prep_da(s):
                    ts0 = s * TS
                    gs = ch * NSUB + s
                    bld.wait("scalar", s_vs, at=max(0, s_vs.n - 1))
                    dA = dA2[gs % 2]
                    dA4 = dA[:].rearrange("p (t c n) -> p t c n", t=TS, c=NK)
                    for n in range(DS):
                        BO("scalar", "activation", dA4[:, :, :, n],
                           dt3[:, ts0:ts0 + TS, :], ACTF.Exp,
                           scale=a_cols[:, n:n + 1],
                           sem=bld.ev if n == DS - 1 else None)
                    if s == NSUB - 1:
                        bld.tick("scalar", s_chs)

                def emit_vwork(s):
                    ts0 = s * TS
                    gs = ch * NSUB + s
                    dA = dA2[gs % 2]
                    bld.wait("vector", bld.ev)
                    BO("vector", "tensor_tensor",
                       dus[:].rearrange("p (t c) -> p t c", t=TS),
                       dt3[:, ts0:ts0 + TS, :], u3[:, ts0:ts0 + TS, :], ALU.mult)
                    BO("vector", "tensor_copy", hist[:, 0:256], hist_last)
                    for grp in range(TS // 8):
                        if grp % 2 == 0:
                            hh = grp // 2
                            BO("vector", "tensor_tensor",
                               dBu[:].rearrange("p (t c n) -> p t c n",
                                                t=16, c=NK),
                               dus[:].rearrange("p (t c) -> p t c", t=TS)
                                   [:, hh * 16:(hh + 1) * 16, :].unsqueeze(3)
                                   .broadcast_to([PP, 16, NK, DS]),
                               brep[:].rearrange("p (n t) -> p t n", n=DS)
                                   [:, hh * 16:(hh + 1) * 16, :].unsqueeze(2)
                                   .broadcast_to([PP, 16, NK, DS]), ALU.mult)
                            BO("vector", "tensor_scalar", dBu[:], dBu[:],
                               10.0, -10.0, ALU.min, ALU.max)
                            if hh == 1:
                                bld.tick("vector", s_gd)
                        for tl in range(grp * 8, grp * 8 + 8):
                            hpi = 0 if tl == 0 else 1 + ((tl - 1) % 16)
                            hci = 1 + (tl % 16)
                            hp = hist[:, hpi * 256:(hpi + 1) * 256]
                            hc = hist[:, hci * 256:(hci + 1) * 256]
                            av = dA[:, tl * 256:(tl + 1) * 256]
                            bv = dBu[:, (tl % 16) * 256:((tl % 16) + 1) * 256]
                            BO("vector", "scalar_tensor_tensor", hc, av, EXP_NEG20,
                               hp, ALU.max, ALU.mult)
                            BO("vector", "tensor_tensor", hc, hc, bv, ALU.add)
                            BO("vector", "scalar_tensor_tensor", hc, hc, 100.0,
                               cm100b, ALU.min, ALU.max)
                        hv = hist[:, (1 + (grp % 2) * 8) * 256:
                                  (1 + (grp % 2) * 8 + 8) * 256]
                        BO("vector", "tensor_tensor",
                           hcs[:].rearrange("p (t c n) -> p t c n", t=8, c=NK),
                           hv.rearrange("p (t c n) -> p t c n", t=8, c=NK),
                           crep2[s % 2][:].rearrange("p (n t) -> p t n", n=DS)[
                               :, grp * 8:grp * 8 + 8, :].unsqueeze(2)
                               .broadcast_to([PP, 8, NK, DS]), ALU.mult)
                        BO("vector", "tensor_reduce",
                           ysub3[:, grp * 8:grp * 8 + 8, :],
                           hcs[:].rearrange("p (t c n) -> p t c n", t=8, c=NK),
                           AX.X, ALU.add)
                    bld.tick("vector", s_vs)
                    if "y_dbg" in dbg:
                        bld.vv.n += 1
                        bld.raw("vector", lambda e: e.engine_nop().then_inc(bld.vv.h, 1))
                        bld.wait("sync", bld.vv)
                        for k in range(NK):
                            bld.dma("sync", S["y_dbg"].ap()[
                                k * 128:(k + 1) * 128,
                                t0 + ts0:t0 + ts0 + TS],
                                ysub3[:, :, k], sem=bld.dmb,
                                allow_slow_non_contiguous=True)
                        bld.wait("vector", bld.dmb)

                def emit_yln(s):
                    ts0 = s * TS
                    BO("vector", "tensor_reduce", ysq[:, 0:TS], ysub3, AX.X, ALU.add,
                       sem=bld.vv)
                    bld.wait("tensor", bld.vv)
                    BO("tensor", "matmul", ps_row[0:1, 0:TS], ones_f[:, 0:1],
                       ysq[:, 0:TS], start=True, stop=True, sem=bld.mm)
                    bld.wait("vector", bld.mm)
                    BO("vector", "tensor_scalar", yrow[0:1, 0:TS], ps_row[0:1, 0:TS],
                       1.0 / DI, None, ALU.mult, sem=bld.vv)
                    bld.wait("tensor", bld.vv)
                    BO("tensor", "matmul", ps_mu[:, 0:TS],
                       ones_f[0:1, 0:1].broadcast_to([1, 128]), yrow[0:1, 0:TS],
                       start=True, stop=True, sem=bld.mm)
                    bld.wait("vector", bld.mm)
                    mu3 = ps_mu[:, 0:TS].unsqueeze(2).broadcast_to([PP, TS, NK])
                    BO("vector", "tensor_tensor", ysub3, ysub3, mu3, ALU.subtract,
                       sem=bld.vv)
                    bld.wait("scalar", bld.vv)
                    BO("scalar", "activation", ysq[:, 0:TS * NK], ysub[:],
                       ACTF.Square, sem=bld.ev)
                    bld.wait("vector", bld.ev)
                    BO("vector", "tensor_reduce", ysq[:, 0:TS], ysq3, AX.X, ALU.add,
                       sem=bld.vv)
                    bld.wait("tensor", bld.vv)
                    BO("tensor", "matmul", ps_row[0:1, TS:2 * TS], ones_f[:, 0:1],
                       ysq[:, 0:TS], start=True, stop=True, sem=bld.mm)
                    bld.wait("vector", bld.mm)
                    yv = yrow[0:1, TS:2 * TS]
                    BO("vector", "tensor_scalar", yv, ps_row[0:1, TS:2 * TS],
                       1.0 / DI, LN_EPS, ALU.mult, ALU.add, sem=bld.vv)
                    bld.wait("scalar", bld.vv)
                    BO("scalar", "activation", yrow[0:1, 2 * TS:3 * TS], yv, ACTF.Sqrt,
                       sem=bld.ev)
                    bld.wait("vector", bld.ev)
                    s0v = yrow[0:1, 2 * TS:3 * TS]
                    r0v = yrow[0:1, 3 * TS:4 * TS]
                    BO("vector", "reciprocal", r0v, s0v, sem=bld.vv)
                    bld.wait("vector", bld.vv)
                    bld.wait("tensor", bld.vv)
                    BO("tensor", "matmul", ps_mu[:, TS:2 * TS],
                       ones_f[0:1, 0:1].broadcast_to([1, 128]), r0v,
                       start=True, stop=True, sem=bld.mm)
                    bld.wait("vector", bld.mm)
                    rst3 = ps_mu[:, TS:2 * TS].unsqueeze(2).broadcast_to([PP, TS, NK])
                    lng3 = ln_g_t[:, 0:NK].unsqueeze(1).broadcast_to([PP, TS, NK])
                    lnb3 = ln_b_t[:, 0:NK].unsqueeze(1).broadcast_to([PP, TS, NK])
                    dpr3 = d_par_t[:, 0:NK].unsqueeze(1).broadcast_to([PP, TS, NK])
                    BO("vector", "tensor_tensor", ysub3, ysub3, rst3, ALU.mult)
                    BO("vector", "tensor_tensor", ysub3, ysub3, lng3, ALU.mult)
                    BO("vector", "tensor_tensor", ysub3, ysub3, lnb3, ALU.add)
                    usl = u3[:, ts0:ts0 + TS, :]
                    BO("vector", "tensor_tensor", ysq3, usl, dpr3, ALU.mult)
                    BO("vector", "tensor_tensor", ysub3, ysub3, ysq3, ALU.add)
                    if s == 0:
                        bld.wait("vector", s_prod)
                    zsl = zs_t[:].rearrange("p (c t) -> p t c", c=NK)[
                        :, ts0:ts0 + TS, :]
                    BO("vector", "tensor_tensor",
                       prodb[:].rearrange("p (c t) -> p t c", c=NK)[
                           :, ts0:ts0 + TS, :],
                       ysub3, zsl, ALU.mult, sem=bld.vv)

                emit_prep(0)
                for s in range(NSUB):
                    emit_vwork(s)
                    if s + 1 < NSUB:
                        emit_prep_da(s + 1)
                    emit_yln(s)
                    if s + 1 < NSUB:
                        emit_prep_bc(s + 1)
                bld.tick("vector", s_chv)
                bld.wait("sync", bld.vv)
                for k in range(NK):
                    bld.dma("sync", S["prod_bf"].ap()[k * 128:(k + 1) * 128,
                                                      t0:t0 + TCH],
                            prodb[:, k * TCH:(k + 1) * TCH], sem=s_prod)
            bld.barrier()
            ph2b.close()


        if phase_limit >= 4:
            ph2c = contextlib.ExitStack()
            pbuf = [sb(f"c2p{i}", [PP, NK * 512], BF16, ph2c) for i in range(2)]
            wob = sb("c2w", [PP, NK * 1024], BF16, ph2c)
            sst = [sb(f"c2s{i}", [PP, 1024], F32, ph2c) for i in range(2)]
            ps2c = [psb(f"c2ps{i}", [PP, 512], F32, ph2c) for i in range(4)]
            s_pb2 = [bld.slot("c2pb0"), bld.slot("c2pb1")]
            s_ss = [bld.slot("c2ss0"), bld.slot("c2ss1")]
            s_pe = [bld.slot(f"c2pe{i}") for i in range(4)]
            for k in range(NK):
                bld.dma("sync", wob[:, k * 1024:(k + 1) * 1024],
                        S["woutT"].ap()[k * 128:(k + 1) * 128, :])
            for tt in range(T // 512):
                psl = tt % 2
                bld.wait("sync", s_pb2[psl])
                for k in range(NK):
                    bld.dma("sync", pbuf[psl][:, k * 512:(k + 1) * 512],
                            S["prod_bf"].ap()[k * 128:(k + 1) * 128,
                                              tt * 512:(tt + 1) * 512])
                for ts4 in range(4):
                    for oh in range(2):
                        pb = 2 * (ts4 % 2) + oh
                        bld.wait("tensor", bld.dm)
                        bld.wait("tensor", s_pe[pb])
                        for k in range(NK):
                            BO("tensor", "matmul", ps2c[pb][:],
                               pbuf[psl][:, k * 512 + ts4 * 128:
                                         k * 512 + (ts4 + 1) * 128],
                               wob[:, k * 1024 + oh * 512:k * 1024 + (oh + 1) * 512],
                               start=(k == 0), stop=(k == NK - 1),
                               sem=bld.mm if k == NK - 1 else None)
                        if ts4 == 3 and oh == 1:
                            bld.tick("tensor", s_pb2[psl])
                        ssl = ts4 % 2
                        bld.wait("scalar", bld.mm)
                        bld.wait("scalar", s_ss[ssl])
                        BO("scalar", "copy", sst[ssl][:, oh * 512:(oh + 1) * 512],
                           ps2c[pb][:], sem=bld.ev)
                        bld.tick("scalar", s_pe[pb])
                    bld.wait("sync", bld.ev)
                    ssl = ts4 % 2
                    bld.dma("sync",
                            S["s0"].ap()[tt * 512 + ts4 * 128:
                                         tt * 512 + (ts4 + 1) * 128, :],
                            sst[ssl][:], sem=s_ss[ssl])
            bld.barrier()
            ph2c.close()

        if phase_limit >= 5:
            ph3a = contextlib.ExitStack()
            xqT = [sb(f"d3xq{i}", [PP, T], BF16, ph3a) for i in range(8)]
            x3t = sb("d3x", [PP, C], F32, ph3a)
            q3b = sb("d3q", [PP, C], BF16, ph3a)
            q3i = sb("d3i", [PP, C], I32, ph3a)
            trb3 = sb("d3t", [PP, C], BF16, ph3a)
            wg3 = [sb(f"d3wg{i}", [PP, 1024], BF16, ph3a) for i in range(2)]
            wu3 = [sb(f"d3wu{i}", [PP, 1024], BF16, ph3a) for i in range(2)]
            gsil = sb("d3gs", [PP, 512], F32, ph3a)
            guf = sb("d3gu", [PP, 512], F32, ph3a)
            gub = sb("d3gb", [PP, 512], BF16, ph3a)
            st3 = sb("d3st", [PP, 8], F32, ph3a)
            ps_g3 = psb("d3pg", [PP, 512], F32, ph3a)
            ps_u3 = psb("d3pu", [PP, 512], F32, ph3a)
            ps_t3 = psb("d3pt", [PP, C], BF16, ph3a)
            s_w3 = [bld.slot("d3w0"), bld.slot("d3w1")]
            s_gu = bld.slot("d3gu")
            for it in range(T // 128):
                bld.wait("sync", bld.vv)
                bld.wait("sync", bld.ev)
                bld.dma("sync", x3t[:], S["xn"].ap()[it * 128:(it + 1) * 128, :])
                bld.wait("vector", bld.dm)
                BO("vector", "tensor_scalar", x3t[:], x3t[:], qs[:, 0:1], 127.0,
                   ALU.mult, ALU.min)
                BO("vector", "tensor_scalar", x3t[:], x3t[:], -128.0, MAGIC,
                   ALU.max, ALU.add)
                BO("vector", "tensor_scalar", q3b[:], x3t[:], MAGIC, qs[:, 1:2],
                   ALU.subtract, ALU.mult, sem=bld.vv)
                if "xq_dbg" in dbg:
                    BO("vector", "tensor_scalar", x3t[:], x3t[:], MAGIC, None,
                       ALU.subtract, sem=bld.vv)
                    bld.wait("sync", bld.vv)
                    bld.dma("sync", S["xq_dbg"].ap()[it * 128:(it + 1) * 128, :],
                            x3t[:])
                bld.wait("tensor", bld.vv)
                bld.wait("tensor", bld.ev)
                for cb in range(8):
                    BO("tensor", "transpose", ps_t3[:, cb * 128:(cb + 1) * 128],
                       q3b[:, cb * 128:(cb + 1) * 128], ident_b[:],
                       sem=bld.mm if cb == 7 else None)
                bld.wait("scalar", bld.mm)
                BO("scalar", "copy", trb3[:], ps_t3[:], sem=bld.ev)
                bld.wait("vector", bld.ev)
                for cb in range(8):
                    BO("vector", "tensor_copy", xqT[cb][:, it * 128:(it + 1) * 128],
                       trb3[:, cb * 128:(cb + 1) * 128],
                       sem=bld.vv if cb == 7 else None)
            for ib in range(NB0):
                wsl = ib % 2
                bld.wait("sync", s_w3[wsl])
                bld.dma("sync", wg3[wsl][:],
                        S["wq_g0T"].ap()[ib].rearrange("(q p) m -> p q m", p=128))
                bld.dma("sync", wu3[wsl][:],
                        S["wq_u0T"].ap()[ib].rearrange("(q p) m -> p q m", p=128))
                if ib == 0 and "w0_dbg" in dbg:
                    bld.wait("vector", bld.dm)
                    BO("vector", "tensor_copy", gsil[:, 0:512], wg3[0][:, 0:512])
                    BO("vector", "tensor_copy", guf[:, 0:512], wg3[0][:, 512:1024],
                       sem=bld.vv)
                    bld.wait("sync", bld.vv)
                    bld.dma("sync", S["w0_dbg"].ap()[:, 0:512], gsil[:, 0:512])
                    bld.dma("sync", S["w0_dbg"].ap()[:, 512:1024], guf[:, 0:512])
                    bld.dma("sync", trb3[:, 0:128],
                            S["wq_g0T"].ap()[0, 0:128, :], sem=bld.dm)
                    bld.wait("vector", bld.dm)
                    BO("vector", "tensor_copy", gsil[:, 0:128], trb3[:, 0:128],
                       sem=bld.vv)
                    bld.wait("sync", bld.vv)
                    bld.dma("sync", S["w0_dbg"].ap()[:, 1024:1152], gsil[:, 0:128])
                for tt in range(T // 512):
                    bld.wait("tensor", bld.dm)
                    bld.wait("tensor", bld.vv)
                    if tt == 0 and ib == 0:
                        pass
                    bld.wait("tensor", bld.ev)
                    for cb in range(8):
                        BO("tensor", "matmul", ps_g3[:],
                           wg3[wsl][:, cb * 128:(cb + 1) * 128],
                           xqT[cb][:, tt * 512:(tt + 1) * 512],
                           start=(cb == 0), stop=(cb == 7))
                    for cb in range(8):
                        BO("tensor", "matmul", ps_u3[:],
                           wu3[wsl][:, cb * 128:(cb + 1) * 128],
                           xqT[cb][:, tt * 512:(tt + 1) * 512],
                           start=(cb == 0), stop=(cb == 7),
                           sem=bld.mm if cb == 7 else None)
                    if tt == T // 512 - 1:
                        bld.tick("tensor", s_w3[wsl])
                    bld.wait("scalar", bld.mm)
                    BO("scalar", "activation", gsil[:], ps_g3[:], ACTF.Silu,
                       sem=bld.ev)
                    bld.wait("vector", bld.ev)
                    bld.wait("vector", s_gu)
                    BO("vector", "tensor_tensor", guf[:], gsil[:], ps_u3[:], ALU.mult)
                    BO("vector", "tensor_reduce", st3[:, 0:1], guf[:], AX.X, ALU.max,
                       apply_absolute_value=True, sem=bld.vv)
                    bld.wait("vector", bld.vv)
                    BO("vector", "tensor_tensor", lmax[:, 1:2], lmax[:, 1:2],
                       st3[:, 0:1], ALU.max)
                    BO("vector", "tensor_copy", gub[:], guf[:], sem=bld.vv)
                    bld.tick("vector", bld.ev)
                    bld.wait("sync", bld.vv)
                    bld.dma("sync", S["gu_bf"].ap()[ib * 128:(ib + 1) * 128,
                                                    tt * 512:(tt + 1) * 512],
                            gub[:], sem=s_gu)
            global_max(1, 1)
            derive_scale(1, 127.0, 15, 2, 3)
            bld.barrier()
            ph3a.close()

        if phase_limit >= 6:
            ph3b = contextlib.ExitStack()
            wd3 = [sb(f"e3wd{i}", [PP, 1024], BF16, ph3b) for i in range(NB0)]
            gu3 = sb("e3gu", [PP, NB0 * 512], BF16, ph3b)
            hq3 = sb("e3hq", [PP, NB0 * 512], BF16, ph3b)
            hqi = sb("e3hi", [PP, 512], I32, ph3b)
            hqf = sb("e3hf", [PP, 512], F32, ph3b)
            g3 = sb("e3g", [PP, C], F32, ph3b)
            xx3 = sb("e3x", [PP, C], F32, ph3b)
            ss3 = sb("e3s", [PP, C], F32, ph3b)
            xn1t = sb("e3n", [PP, C], F32, ph3b)
            xn1b = sb("e3nb", [PP, C], BF16, ph3b)
            dmp3 = sb("e3d", [PP, C], F32, ph3b)
            st3b = sb("e3st", [PP, 8], F32, ph3b)
            trb3b = sb("e3t", [PP, C], BF16, ph3b)
            ps_d3 = [psb(f"e3p{i}", [PP, 512], F32, ph3b) for i in range(2)]
            ps_t3b = psb("e3pt", [PP, C], BF16, ph3b)
            for k in range(NB0):
                bld.dma("sync", wd3[k][:],
                        S["wq_d0T"].ap()[k * 128:(k + 1) * 128, :])
            for tt in range(T // 512):
                bld.wait("sync", bld.vv)
                for k in range(NB0):
                    bld.dma("sync", gu3[:, k * 512:(k + 1) * 512],
                            S["gu_bf"].ap()[k * 128:(k + 1) * 128,
                                            tt * 512:(tt + 1) * 512])
                bld.wait("vector", bld.dm)
                for k in range(NB0):
                    BO("vector", "tensor_scalar", hqf[:], gu3[:, k * 512:(k + 1) * 512],
                       qs[:, 2:3], 127.0, ALU.mult, ALU.min)
                    BO("vector", "tensor_scalar", hqf[:], hqf[:], -128.0, MAGIC,
                       ALU.max, ALU.add)
                    BO("vector", "tensor_scalar", hq3[:, k * 512:(k + 1) * 512],
                       hqf[:], MAGIC, qs[:, 3:4], ALU.subtract, ALU.mult,
                       sem=bld.vv if k == NB0 - 1 else None)
                for t4 in range(4):
                    it = tt * 4 + t4
                    bld.wait("tensor", bld.vv)
                    bld.wait("tensor", bld.ev)
                    for oh in range(2):
                        for k in range(NB0):
                            BO("tensor", "matmul", ps_d3[oh][:],
                               hq3[:, k * 512 + t4 * 128:k * 512 + (t4 + 1) * 128],
                               wd3[k][:, oh * 512:(oh + 1) * 512],
                               start=(k == 0), stop=(k == NB0 - 1),
                               sem=bld.mm if (k == NB0 - 1 and oh == 1) else None)
                    bld.wait("scalar", bld.mm)
                    bld.wait("scalar", bld.vv)
                    BO("scalar", "activation", g3[:, 0:512], ps_d3[0][:], ACTF.Silu)
                    BO("scalar", "activation", g3[:, 512:1024], ps_d3[1][:], ACTF.Silu,
                       sem=bld.ev)
                    bld.wait("sync", bld.ev)
                    if "g0_dbg" in dbg:
                        bld.dma("sync", S["g0_dbg"].ap()[it * 128:(it + 1) * 128, :],
                                g3[:])
                    bld.dma("sync", xx3[:], D["x_in"][it * 128:(it + 1) * 128, :])
                    bld.dma("sync", ss3[:], S["s0"].ap()[it * 128:(it + 1) * 128, :])
                    bld.wait("vector", bld.dm)
                    bld.wait("vector", bld.ev)
                    BO("vector", "tensor_tensor", g3[:], g3[:], ss3[:], ALU.mult)
                    BO("vector", "tensor_tensor", xx3[:], xx3[:], g3[:], ALU.add,
                       sem=bld.vv)
                    bld.tick("vector", bld.ev)
                    bld.wait("sync", bld.vv)
                    bld.dma("sync", S["x1"].ap()[it * 128:(it + 1) * 128, :], xx3[:])
                    bld.wait("scalar", bld.vv)
                    emit_ln_tile(xx3[:], xn1t[:], xn1b[:], st3b, dmp3[:], C,
                                 absmax_slot=2)
                    bld.wait("sync", bld.vv)
                    bld.dma("sync", S["xn1"].ap()[it * 128:(it + 1) * 128, :],
                            xn1t[:])
                    bld.wait("tensor", bld.vv)
                    bld.wait("tensor", bld.ev)
                    for cb in range(8):
                        BO("tensor", "transpose", ps_t3b[:, cb * 128:(cb + 1) * 128],
                           xn1b[:, cb * 128:(cb + 1) * 128], ident_b[:],
                           sem=bld.mm if cb == 7 else None)
                    bld.wait("scalar", bld.mm)
                    BO("scalar", "copy", trb3b[:], ps_t3b[:], sem=bld.ev)
                    bld.wait("sync", bld.ev)
                    for cb in range(8):
                        bld.dma("sync", S["xn1T_bf"].ap()[cb * 128:(cb + 1) * 128,
                                                          it * 128:(it + 1) * 128],
                                trb3b[:, cb * 128:(cb + 1) * 128])
            global_max(2, 2)
            derive_scale(2, 32767.0, None, 4, 5, floor=1e-12)
            if "qs2_dbg" in dbg:
                bld.wait("sync", bld.vv)
                bld.wait("sync", bld.ev)
                bld.dma("sync", S["qs2_dbg"].ap()[:, 0:16], qs[:, 0:16])
                bld.dma("sync", S["qs2_dbg"].ap()[:, 16:24], gmax[:, 0:8])
                bld.dma("sync", S["qs2_dbg"].ap()[:, 24:32], lmax[:, 0:8])
            bld.barrier()
            ph3b.close()

        attn_sb = sb("attn_sb", [64, H * DH], BF16)
        if phase_limit >= 7:
            ph4a = contextlib.ExitStack()
            wk4 = [sb(f"f4wk{i}", [PP, C], BF16, ph4a) for i in range(8)]
            wv4 = [sb(f"f4wv{i}", [PP, C], BF16, ph4a) for i in range(8)]
            wg4 = [sb(f"f4wg{i}", [PP, C], BF16, ph4a) for i in range(8)]
            xT4 = sb("f4x", [PP, 8 * 128], BF16, ph4a)
            kst = sb("f4k", [PP, C], F32, ph4a)
            krp = sb("f4kr", [PP, C], BF16, ph4a)
            rt1 = sb("f4r1", [PP, 512], F32, ph4a)
            rt2 = sb("f4r2", [PP, 512], F32, ph4a)
            cosT = sb("f4c", [PP, 32], F32, ph4a)
            sinT = sb("f4s", [PP, 32], F32, ph4a)
            kTh = sb("f4kt", [64, H * 128], BF16, ph4a)
            vb4 = sb("f4v", [PP, C], BF16, ph4a)
            gst = sb("f4g", [PP, C], F32, ph4a)
            gsb = sb("f4gb", [PP, C], BF16, ph4a)
            gmx = sb("f4gm", [PP, 20], F32, ph4a)
            trg = sb("f4tg", [PP, C], BF16, ph4a)
            sc4 = sb("f4sc", [64, CS], F32, ph4a)
            e4 = sb("f4e", [64, CS], F32, ph4a)
            pm4 = sb("f4pm", [64, 4], F32, ph4a)
            pb4 = sb("f4pb", [64, CS], BF16, ph4a)
            pT4 = sb("f4pt", [PP, 64], BF16, ph4a)
            attn_acc = sb("f4aa", [64, H * DH], F32, ph4a)
            ps_kv = psb("f4p", [PP, 512], F32, ph4a)
            ps_sc = psb("f4ps", [64, CS], F32, ph4a)
            ps_at = psb("f4pa", [64, DH], F32, ph4a)
            ps_kT = psb("f4pk", [64, H * 128], BF16, ph4a)
            ps_tg = psb("f4ptg", [PP, C], BF16, ph4a)
            ps_pt = ps_tg[:, 0:64]
            BO("vector", "memset", attn_acc[:, :], 0.0, sem=bld.vv)
            for cb in range(8):
                bld.dma("sync", wk4[cb][:], S["wkT"].ap()[cb * 128:(cb + 1) * 128, :])
                bld.dma("sync", wv4[cb][:], S["wvT"].ap()[cb * 128:(cb + 1) * 128, :])
                bld.dma("sync", wg4[cb][:], S["wgT"].ap()[cb * 128:(cb + 1) * 128, :])
            for it in range(T // 128):
                bld.wait("sync", bld.mm)
                bld.wait("sync", bld.vv)
                bld.wait("sync", bld.ev)
                for cb in range(8):
                    bld.dma("sync", xT4[:, cb * 128:(cb + 1) * 128],
                            S["xn1T_bf"].ap()[cb * 128:(cb + 1) * 128,
                                              it * 128:(it + 1) * 128])
                bld.dma("sync", cosT[:, :], D["cos_t"][it * 128:(it + 1) * 128, :])
                bld.dma("sync", sinT[:, :], D["sin_t"][it * 128:(it + 1) * 128, :])
                for (dst, wsrc) in [(kst, wk4), (vb4, wv4), (gst, wg4)]:
                    for oh in range(2):
                        bld.wait("tensor", bld.dm)
                        bld.wait("tensor", bld.ev)
                        for cb in range(8):
                            BO("tensor", "matmul", ps_kv[:],
                               xT4[:, cb * 128:(cb + 1) * 128],
                               wsrc[cb][:, oh * 512:(oh + 1) * 512],
                               start=(cb == 0), stop=(cb == 7),
                               sem=bld.mm if cb == 7 else None)
                        bld.wait("scalar", bld.mm)
                        BO("scalar", "copy", dst[:, oh * 512:(oh + 1) * 512],
                           ps_kv[:], sem=bld.ev)
                bld.wait("vector", bld.ev)
                av2 = kst[:].rearrange("p (h i two) -> p h i two", h=H, two=2)
                a4 = av2[:, :, :, 0]
                b4 = av2[:, :, :, 1]
                cexp = cosT[:].unsqueeze(1).broadcast_to([PP, H, 32])
                sexp = sinT[:].unsqueeze(1).broadcast_to([PP, H, 32])
                r13 = rt1[:].rearrange("p (h i) -> p h i", h=H)
                r23 = rt2[:].rearrange("p (h i) -> p h i", h=H)
                kv4 = krp[:].rearrange("p (h i two) -> p h i two", h=H, two=2)
                BO("vector", "tensor_tensor", r13, a4, cexp, ALU.mult)
                BO("vector", "tensor_tensor", r23, b4, sexp, ALU.mult)
                BO("vector", "tensor_tensor", kv4[:, :, :, 0], r13, r23, ALU.subtract)
                BO("vector", "tensor_tensor", r13, a4, sexp, ALU.mult)
                BO("vector", "tensor_tensor", r23, b4, cexp, ALU.mult)
                BO("vector", "tensor_tensor", kv4[:, :, :, 1], r13, r23, ALU.add,
                   sem=bld.vv)
                bld.wait("tensor", bld.vv)
                bld.wait("tensor", bld.ev)
                for h in range(H):
                    BO("tensor", "transpose", ps_kT[:, h * 128:(h + 1) * 128],
                       krp[:, h * 64:(h + 1) * 64], ident_b[:],
                       sem=bld.mm if h == H - 1 else None)
                bld.wait("scalar", bld.mm)
                BO("scalar", "copy", kTh[:, :], ps_kT[:, :], sem=bld.ev)
                bld.wait("vector", bld.ev)
                g3v = gst[:].rearrange("p (h l) -> p h l", h=H)
                BO("vector", "tensor_reduce", gmx[:, 0:16], g3v, AX.X, ALU.max,
                   sem=bld.vv)
                bld.wait("vector", bld.vv)
                mexp4 = gmx[:, 0:16].unsqueeze(2).broadcast_to([PP, H, L])
                BO("vector", "tensor_tensor", g3v, g3v, mexp4, ALU.subtract,
                   sem=bld.vv)
                bld.wait("scalar", bld.vv)
                BO("scalar", "activation", gst[:], gst[:], ACTF.Exp, scale=0.125,
                   sem=bld.ev)
                bld.wait("vector", bld.ev)
                BO("vector", "tensor_reduce", gmx[:, 0:16], g3v, AX.X, ALU.add,
                   sem=bld.vv)
                bld.wait("vector", bld.vv)
                BO("vector", "reciprocal", gmx[:, 0:16], gmx[:, 0:16], sem=bld.vv)
                bld.wait("vector", bld.vv)
                BO("vector", "tensor_tensor",
                   gsb[:].rearrange("p (h l) -> p h l", h=H), g3v,
                   gmx[:, 0:16].unsqueeze(2).broadcast_to([PP, H, L]), ALU.mult,
                   sem=bld.vv)
                bld.wait("tensor", bld.vv)
                for cb in range(8):
                    BO("tensor", "transpose", ps_tg[:, cb * 128:(cb + 1) * 128],
                       gsb[:, cb * 128:(cb + 1) * 128], ident_b[:],
                       sem=bld.mm if cb == 7 else None)
                bld.wait("scalar", bld.mm)
                BO("scalar", "copy", trg[:], ps_tg[:], sem=bld.ev)
                bld.wait("sync", bld.ev)
                for cb in range(8):
                    bld.dma("sync", S["gatesT_bf"].ap()[cb * 128:(cb + 1) * 128,
                                                        it * 128:(it + 1) * 128],
                            trg[:, cb * 128:(cb + 1) * 128])
                for h in range(H):
                    bld.wait("tensor", bld.vv)
                    BO("tensor", "matmul", ps_sc[:, :],
                       qh_b[:, h * L:(h + 1) * L],
                       kTh[:, h * 128:(h + 1) * 128],
                       start=True, stop=True, sem=bld.mm)
                    bld.wait("vector", bld.mm)
                    BO("vector", "tensor_tensor", sc4[:], ps_sc[:, :], cmask_t[:],
                       ALU.add)
                    BO("vector", "tensor_reduce", pm4[:, 0:1], sc4[:], AX.X, ALU.max,
                       sem=bld.vv)
                    bld.wait("vector", bld.vv)
                    BO("vector", "tensor_scalar", pm4[:, 1:2], pm4[:, 0:1], -0.125,
                       None, ALU.mult, sem=bld.vv)
                    bld.wait("scalar", bld.vv)
                    BO("scalar", "activation", e4[:], sc4[:], ACTF.Exp, scale=0.125,
                       bias=pm4[:, 1:2], sem=bld.ev)
                    bld.wait("vector", bld.ev)
                    BO("vector", "tensor_reduce", pm4[:, 2:3], e4[:], AX.X, ALU.add,
                       sem=bld.vv)
                    bld.wait("vector", bld.vv)
                    BO("vector", "reciprocal", pm4[:, 3:4], pm4[:, 2:3], sem=bld.vv)
                    bld.wait("vector", bld.vv)
                    BO("vector", "tensor_scalar", pb4[:], e4[:], pm4[:, 3:4], None,
                       ALU.mult, sem=bld.vv)
                    bld.wait("tensor", bld.vv)
                    BO("tensor", "transpose", ps_pt, pb4[:], ident_b[0:64, 0:64],
                       sem=bld.mm)
                    bld.wait("scalar", bld.mm)
                    BO("scalar", "copy", pT4[:, :], ps_pt, sem=bld.ev)
                    bld.wait("tensor", bld.ev)
                    bld.wait("tensor", bld.vv)
                    BO("tensor", "matmul", ps_at[:, :],
                       pT4[:, :], vb4[:, h * DH:(h + 1) * DH],
                       start=True, stop=True, sem=bld.mm)
                    bld.wait("vector", bld.mm)
                    BO("vector", "tensor_tensor", attn_acc[:, h * DH:(h + 1) * DH],
                       attn_acc[:, h * DH:(h + 1) * DH], ps_at[:, :], ALU.add,
                       sem=bld.vv)
            bld.wait("scalar", bld.vv)
            BO("scalar", "copy", attn_sb[:, :], attn_acc[:, :], sem=bld.ev)
            bld.barrier()
            ph4a.close()

        if phase_limit >= 8:
            ph4b = contextlib.ExitStack()
            wp4 = [sb(f"g4wp{i}", [PP, C], BF16, ph4b) for i in range(8)]
            gT4 = sb("g4g", [64, H * 128], BF16, ph4b)
            wgt = sb("g4w", [PP, C], BF16, ph4b)
            wgtT = sb("g4wt", [PP, 8 * 128], BF16, ph4b)
            s1st = sb("g4s", [PP, C], F32, ph4b)
            ps_w4 = psb("g4pw", [PP, C], F32, ph4b)
            ps_s4 = [psb(f"g4ps{i}", [PP, 512], F32, ph4b) for i in range(2)]
            ps_t4 = psb("g4pt", [PP, C], BF16, ph4b)
            for cb in range(8):
                bld.dma("sync", wp4[cb][:], S["wpT"].ap()[cb * 128:(cb + 1) * 128, :])
            for it in range(T // 128):
                bld.wait("sync", bld.mm)
                bld.wait("sync", bld.ev)
                for h in range(H):
                    bld.dma("sync", gT4[:, h * 128:(h + 1) * 128],
                            S["gatesT_bf"].ap()[h * 64:(h + 1) * 64,
                                                it * 128:(it + 1) * 128])
                bld.wait("tensor", bld.dm)
                bld.wait("tensor", bld.ev)
                for h in range(H):
                    BO("tensor", "matmul", ps_w4[:, h * DH:(h + 1) * DH],
                       gT4[:, h * 128:(h + 1) * 128],
                       attn_sb[:, h * DH:(h + 1) * DH],
                       start=True, stop=True,
                       sem=bld.mm if h == H - 1 else None)
                bld.wait("scalar", bld.mm)
                BO("scalar", "copy", wgt[:], ps_w4[:], sem=bld.ev)
                bld.wait("tensor", bld.ev)
                for cb in range(8):
                    BO("tensor", "transpose", ps_t4[:, cb * 128:(cb + 1) * 128],
                       wgt[:, cb * 128:(cb + 1) * 128], ident_b[:],
                       sem=bld.mm if cb == 7 else None)
                bld.wait("scalar", bld.mm)
                BO("scalar", "copy", wgtT[:], ps_t4[:], sem=bld.ev)
                bld.wait("tensor", bld.ev)
                for oh in range(2):
                    for cb in range(8):
                        BO("tensor", "matmul", ps_s4[oh][:],
                           wgtT[:, cb * 128:(cb + 1) * 128],
                           wp4[cb][:, oh * 512:(oh + 1) * 512],
                           start=(cb == 0), stop=(cb == 7),
                           sem=bld.mm if cb == 7 else None)
                bld.wait("scalar", bld.mm)
                BO("scalar", "copy", s1st[:, 0:512], ps_s4[0][:])
                BO("scalar", "copy", s1st[:, 512:1024], ps_s4[1][:], sem=bld.ev)
                bld.wait("sync", bld.ev)
                bld.dma("sync", S["s1"].ap()[it * 128:(it + 1) * 128, :], s1st[:])
            bld.barrier()
            ph4b.close()

        if phase_limit >= 9:
            ph5a = contextlib.ExitStack()
            xqT5 = [sb(f"h5xq{i}", [PP, T], BF16, ph5a) for i in range(8)]
            x5t = sb("h5x", [PP, C], F32, ph5a)
            q5b = sb("h5q", [PP, C], BF16, ph5a)
            q5i = sb("h5i", [PP, C], I32, ph5a)
            trb5 = sb("h5t", [PP, C], BF16, ph5a)
            wg5 = [sb(f"h5wg{i}", [PP, 1024], BF16, ph5a) for i in range(2)]
            wu5 = [sb(f"h5wu{i}", [PP, 1024], BF16, ph5a) for i in range(2)]
            ub5 = sb("h5ub", [PP, 512], BF16, ph5a)
            gb5 = sb("h5gb", [PP, 512], BF16, ph5a)
            st5 = sb("h5st", [PP, 8], F32, ph5a)
            ps_g5 = psb("h5pg", [PP, 512], F32, ph5a)
            ps_u5 = psb("h5pu", [PP, 512], F32, ph5a)
            ps_t5 = psb("h5pt", [PP, C], BF16, ph5a)
            s_w5 = [bld.slot("h5w0"), bld.slot("h5w1")]
            s_st5 = bld.slot("h5st")
            for it in range(T // 128):
                bld.wait("sync", bld.vv)
                bld.wait("sync", bld.ev)
                bld.dma("sync", x5t[:], S["xn1"].ap()[it * 128:(it + 1) * 128, :])
                bld.wait("vector", bld.dm)
                BO("vector", "tensor_scalar", x5t[:], x5t[:], qs[:, 4:5], 32767.0,
                   ALU.mult, ALU.min)
                BO("vector", "tensor_scalar", x5t[:], x5t[:], -32768.0, MAGIC,
                   ALU.max, ALU.add)
                BO("vector", "tensor_scalar", q5b[:], x5t[:], MAGIC, qs[:, 5:6],
                   ALU.subtract, ALU.mult, sem=bld.vv)
                bld.wait("tensor", bld.vv)
                bld.wait("tensor", bld.ev)
                for cb in range(8):
                    BO("tensor", "transpose", ps_t5[:, cb * 128:(cb + 1) * 128],
                       q5b[:, cb * 128:(cb + 1) * 128], ident_b[:],
                       sem=bld.mm if cb == 7 else None)
                bld.wait("scalar", bld.mm)
                BO("scalar", "copy", trb5[:], ps_t5[:], sem=bld.ev)
                bld.wait("vector", bld.ev)
                for cb in range(8):
                    BO("vector", "tensor_copy", xqT5[cb][:, it * 128:(it + 1) * 128],
                       trb5[:, cb * 128:(cb + 1) * 128],
                       sem=bld.vv if cb == 7 else None)
            for ib in range(NB1):
                wsl = ib % 2
                bld.wait("sync", s_w5[wsl])
                bld.dma("sync", wg5[wsl][:],
                        S["wq_g1T"].ap()[ib].rearrange("(q p) m -> p q m", p=128))
                bld.dma("sync", wu5[wsl][:],
                        S["wq_u1T"].ap()[ib].rearrange("(q p) m -> p q m", p=128))
                for tt in range(T // 512):
                    bld.wait("tensor", bld.dm)
                    bld.wait("tensor", bld.vv)
                    bld.wait("tensor", bld.ev)
                    for cb in range(8):
                        BO("tensor", "matmul", ps_g5[:],
                           wg5[wsl][:, cb * 128:(cb + 1) * 128],
                           xqT5[cb][:, tt * 512:(tt + 1) * 512],
                           start=(cb == 0), stop=(cb == 7))
                    for cb in range(8):
                        BO("tensor", "matmul", ps_u5[:],
                           wu5[wsl][:, cb * 128:(cb + 1) * 128],
                           xqT5[cb][:, tt * 512:(tt + 1) * 512],
                           start=(cb == 0), stop=(cb == 7),
                           sem=bld.mm if cb == 7 else None)
                    if tt == T // 512 - 1:
                        bld.tick("tensor", s_w5[wsl])
                    bld.wait("vector", bld.mm)
                    bld.wait("vector", s_st5)
                    BO("vector", "tensor_reduce", st5[:, 0:1], ps_g5[:], AX.X,
                       ALU.max, apply_absolute_value=True)
                    BO("vector", "tensor_reduce", st5[:, 1:2], ps_u5[:], AX.X,
                       ALU.max, apply_absolute_value=True, sem=bld.vv)
                    bld.wait("vector", bld.vv)
                    BO("vector", "tensor_tensor", lmax[:, 4:5], lmax[:, 4:5],
                       st5[:, 0:1], ALU.max)
                    BO("vector", "tensor_tensor", lmax[:, 3:4], lmax[:, 3:4],
                       st5[:, 1:2], ALU.max, sem=bld.vv)
                    bld.wait("scalar", bld.vv)
                    BO("scalar", "copy", gb5[:], ps_g5[:])
                    BO("scalar", "copy", ub5[:], ps_u5[:], sem=bld.ev)
                    bld.tick("scalar", bld.ev)
                    bld.wait("sync", bld.ev)
                    bld.dma("sync", S["gater_bf"].ap()[ib * 128:(ib + 1) * 128,
                                                       tt * 512:(tt + 1) * 512],
                            gb5[:], sem=s_st5)
                    bld.dma("sync", S["upr_bf"].ap()[ib * 128:(ib + 1) * 128,
                                                     tt * 512:(tt + 1) * 512],
                            ub5[:], sem=s_st5)
            global_max(3, 3)
            derive_scale(3, 32767.0, None, 6, 7, floor=1e-12)
            global_max(4, 4)
            derive_scale(4, 32767.0, None, 8, 9, floor=1e-12)
            bld.barrier()
            ph5a.close()

        if phase_limit >= 10:
            ph5b = contextlib.ExitStack()
            gr5 = sb("i5g", [PP, 512], BF16, ph5b)
            gq5 = sb("i5gq", [PP, 512], F32, ph5b)
            sg5 = sb("i5s", [PP, 512], F32, ph5b)
            t25 = sb("i5t", [PP, 512], F32, ph5b)
            t2b5 = sb("i5tb", [PP, 512], BF16, ph5b)
            i5 = sb("i5i", [PP, 512], I32, ph5b)
            st5b = sb("i5st", [PP, 4], F32, ph5b)
            s_st5b = bld.slot("i5st")
            for tt in range(T // 512):
                for ib in range(NB1):
                    bld.wait("sync", bld.vv)
                    bld.dma("sync", gr5[:], S["gater_bf"].ap()[
                        ib * 128:(ib + 1) * 128, tt * 512:(tt + 1) * 512])
                    bld.wait("vector", bld.dm)
                    BO("vector", "tensor_scalar", gq5[:], gr5[:], qs[:, 8:9], 32767.0,
                       ALU.mult, ALU.min)
                    BO("vector", "tensor_scalar", gq5[:], gq5[:], -32768.0, MAGIC,
                       ALU.max, ALU.add)
                    BO("vector", "tensor_scalar", gq5[:], gq5[:], MAGIC,
                       qs[:, 9:10], ALU.subtract, ALU.mult, sem=bld.vv)
                    bld.wait("scalar", bld.vv)
                    BO("scalar", "activation", sg5[:], gq5[:], ACTF.Sigmoid,
                       sem=bld.ev)
                    bld.wait("vector", bld.ev)
                    BO("vector", "tensor_scalar", sg5[:], sg5[:], 65536.0, 65535.0,
                       ALU.mult, ALU.min)
                    BO("vector", "tensor_scalar", sg5[:], sg5[:], 0.0, MAGIC,
                       ALU.max, ALU.add)
                    BO("vector", "tensor_scalar", sg5[:], sg5[:], MAGIC,
                       1.0 / 65536.0, ALU.subtract, ALU.mult)
                    BO("vector", "tensor_tensor", t25[:], gq5[:], sg5[:], ALU.mult)
                    BO("vector", "tensor_reduce", st5b[:, 0:1], t25[:], AX.X,
                       ALU.max, apply_absolute_value=True, sem=bld.vv)
                    bld.wait("vector", bld.vv)
                    BO("vector", "tensor_tensor", lmax[:, 5:6], lmax[:, 5:6],
                       st5b[:, 0:1], ALU.max)
                    bld.wait("vector", s_st5b)
                    BO("vector", "tensor_copy", t2b5[:], t25[:], sem=bld.vv)
                    bld.wait("sync", bld.vv)
                    bld.dma("sync", S["t2_bf"].ap()[ib * 128:(ib + 1) * 128,
                                                    tt * 512:(tt + 1) * 512],
                            t2b5[:], sem=s_st5b)
            global_max(5, 5)
            derive_scale(5, 32767.0, None, 10, 11, floor=1e-12)
            bld.barrier()
            ph5b.close()

        if phase_limit >= 11:
            ph5c = contextlib.ExitStack()
            a5 = sb("j5a", [PP, 512], BF16, ph5c)
            b5 = sb("j5b", [PP, 512], BF16, ph5c)
            aq5 = sb("j5aq", [PP, 512], F32, ph5c)
            bq5 = sb("j5bq", [PP, 512], F32, ph5c)
            t35 = sb("j5t", [PP, 512], F32, ph5c)
            t3b5 = sb("j5tb", [PP, 512], BF16, ph5c)
            i5c = sb("j5i", [PP, 512], I32, ph5c)
            st5c = sb("j5st", [PP, 4], F32, ph5c)
            s_st5c = bld.slot("j5st")
            for tt in range(T // 512):
                for ib in range(NB1):
                    bld.wait("sync", bld.vv)
                    bld.dma("sync", a5[:], S["t2_bf"].ap()[
                        ib * 128:(ib + 1) * 128, tt * 512:(tt + 1) * 512])
                    bld.dma("sync", b5[:], S["upr_bf"].ap()[
                        ib * 128:(ib + 1) * 128, tt * 512:(tt + 1) * 512])
                    bld.wait("vector", bld.dm)
                    BO("vector", "tensor_scalar", aq5[:], a5[:], qs[:, 10:11],
                       32767.0, ALU.mult, ALU.min)
                    BO("vector", "tensor_scalar", aq5[:], aq5[:], -32768.0, MAGIC,
                       ALU.max, ALU.add)
                    BO("vector", "tensor_scalar", aq5[:], aq5[:], MAGIC,
                       qs[:, 11:12], ALU.subtract, ALU.mult)
                    BO("vector", "tensor_scalar", bq5[:], b5[:], qs[:, 6:7],
                       32767.0, ALU.mult, ALU.min)
                    BO("vector", "tensor_scalar", bq5[:], bq5[:], -32768.0, MAGIC,
                       ALU.max, ALU.add)
                    BO("vector", "tensor_scalar", bq5[:], bq5[:], MAGIC,
                       qs[:, 7:8], ALU.subtract, ALU.mult)
                    BO("vector", "tensor_tensor", t35[:], aq5[:], bq5[:], ALU.mult)
                    BO("vector", "tensor_reduce", st5c[:, 0:1], t35[:], AX.X,
                       ALU.max, apply_absolute_value=True, sem=bld.vv)
                    bld.wait("vector", bld.vv)
                    BO("vector", "tensor_tensor", lmax[:, 6:7], lmax[:, 6:7],
                       st5c[:, 0:1], ALU.max)
                    bld.wait("vector", s_st5c)
                    BO("vector", "tensor_copy", t3b5[:], t35[:], sem=bld.vv)
                    bld.wait("sync", bld.vv)
                    bld.dma("sync", S["t3_bf"].ap()[ib * 128:(ib + 1) * 128,
                                                    tt * 512:(tt + 1) * 512],
                            t3b5[:], sem=s_st5c)
            global_max(6, 6)
            derive_scale(6, 32767.0, None, 12, 13, floor=1e-12)
            bld.barrier()
            ph5c.close()

        if phase_limit >= 12:
            ph5d = contextlib.ExitStack()
            wd5 = [sb(f"k5wd{i}", [PP, 1024], BF16, ph5d) for i in range(NB1)]
            t3l = sb("k5t", [PP, NB1 * 512], BF16, ph5d)
            oq5 = sb("k5o", [PP, NB1 * 512], BF16, ph5d)
            of5 = sb("k5f", [PP, 512], F32, ph5d)
            oi5 = sb("k5i", [PP, 512], I32, ph5d)
            g5d = sb("k5g", [PP, C], F32, ph5d)
            x5d = sb("k5x", [PP, C], F32, ph5d)
            s5d = sb("k5s", [PP, C], F32, ph5d)
            ps_d5 = [psb(f"k5p{i}", [PP, 512], F32, ph5d) for i in range(2)]
            s_out = bld.slot("k5out")
            for k in range(NB1):
                bld.dma("sync", wd5[k][:],
                        S["wq_d1T"].ap()[k * 128:(k + 1) * 128, :])
            for tt in range(T // 512):
                bld.wait("sync", bld.vv)
                for k in range(NB1):
                    bld.dma("sync", t3l[:, k * 512:(k + 1) * 512],
                            S["t3_bf"].ap()[k * 128:(k + 1) * 128,
                                            tt * 512:(tt + 1) * 512])
                bld.wait("vector", bld.dm)
                for k in range(NB1):
                    BO("vector", "tensor_scalar", of5[:], t3l[:, k * 512:(k + 1) * 512],
                       qs[:, 12:13], 32767.0, ALU.mult, ALU.min)
                    BO("vector", "tensor_scalar", of5[:], of5[:], -32768.0, MAGIC,
                       ALU.max, ALU.add)
                    BO("vector", "tensor_scalar", oq5[:, k * 512:(k + 1) * 512],
                       of5[:], MAGIC, qs[:, 13:14], ALU.subtract, ALU.mult,
                       sem=bld.vv if k == NB1 - 1 else None)
                for t4 in range(4):
                    it = tt * 4 + t4
                    bld.wait("tensor", bld.vv)
                    bld.wait("tensor", bld.ev)
                    for oh in range(2):
                        for k in range(NB1):
                            BO("tensor", "matmul", ps_d5[oh][:],
                               oq5[:, k * 512 + t4 * 128:k * 512 + (t4 + 1) * 128],
                               wd5[k][:, oh * 512:(oh + 1) * 512],
                               start=(k == 0), stop=(k == NB1 - 1),
                               sem=bld.mm if (k == NB1 - 1 and oh == 1) else None)
                    bld.wait("scalar", bld.mm)
                    BO("scalar", "activation", g5d[:, 0:512], ps_d5[0][:], ACTF.Silu)
                    BO("scalar", "activation", g5d[:, 512:1024], ps_d5[1][:],
                       ACTF.Silu, sem=bld.ev)
                    bld.wait("sync", bld.ev)
                    bld.wait("sync", s_out)
                    bld.dma("sync", x5d[:], S["x1"].ap()[it * 128:(it + 1) * 128, :])
                    bld.dma("sync", s5d[:], S["s1"].ap()[it * 128:(it + 1) * 128, :])
                    bld.wait("vector", bld.dm)
                    bld.wait("vector", bld.ev)
                    BO("vector", "tensor_tensor", g5d[:], g5d[:], s5d[:], ALU.mult)
                    BO("vector", "tensor_tensor", x5d[:], x5d[:], g5d[:], ALU.add,
                       sem=bld.vv)
                    bld.tick("vector", bld.ev)
                    bld.wait("sync", bld.vv)
                    bld.dma("sync", out_t[it * 128:(it + 1) * 128, :], x5d[:],
                            sem=s_out)
            bld.wait("sync", s_out)
            bld.barrier()
            ph5d.close()



        bld.flush()

        if dbg:
            for nm, dv in dbg.items():
                bld.dma("sync", dv, S[nm].ap(), sem=bld.dmb)
            bld.wait("sync", bld.dmb)
            bld.flush()

        if phase_limit < 12:
            bld.dma("sync", out_t, S["xn"].ap(), sem=bld.dmb)
            bld.wait("sync", bld.dmb)
            bld.flush()

    return nc


def _host_layout(inputs):
    g = lambda k: np.ascontiguousarray(np.asarray(inputs[k], dtype=np.float32))
    W_in = g("L0_attn_W_in")
    com = {}
    com["w_inT_cb"] = np.ascontiguousarray(
        W_in.reshape(32, 128, C).transpose(0, 2, 1))
    def perpart(v, f=1):
        return np.ascontiguousarray(
            v.reshape(NK, 128, f).transpose(1, 0, 2).reshape(128, NK * f))
    com["conv_w"] = perpart(g("L0_attn_conv_w").reshape(DI, 4), 4)
    com["conv_b"] = perpart(g("L0_attn_conv_b"))
    com["w_xT"] = perpart(np.ascontiguousarray(g("L0_attn_W_x").T), 33)
    com["w_dt"] = perpart(g("L0_attn_W_dt"))
    com["b_dt"] = perpart(g("L0_attn_b_dt"))
    com["a_log"] = g("L0_attn_A_log").reshape(1, DS)
    com["d_param"] = perpart(g("L0_attn_D_param"))
    com["ln_g"] = perpart(g("L0_attn_ln_g"))
    com["ln_b"] = perpart(g("L0_attn_ln_b"))
    com["w_outT"] = np.ascontiguousarray(g("L0_attn_W_out").T)
    com["clip_x"] = g("L0_ffn_clip_x").reshape(1, 1)
    com["clip_down"] = g("L0_ffn_clip_down").reshape(1, 1)
    com["w_gate0"] = g("L0_ffn_w_gate")
    com["w_up0"] = g("L0_ffn_w_up")
    com["w_down0"] = g("L0_ffn_w_down")
    com["q_hT"] = np.ascontiguousarray(
        g("L1_attn_latent_queries").reshape(L, H, DH).transpose(2, 1, 0).reshape(
            DH, H * L))
    com["w_kT"] = np.ascontiguousarray(g("L1_attn_W_k").T)
    com["w_vT"] = np.ascontiguousarray(g("L1_attn_W_v").T)
    com["w_gT"] = np.ascontiguousarray(g("L1_attn_W_g").T)
    com["w_pT"] = np.ascontiguousarray(g("L1_attn_W_p").T)
    com["w_gate1"] = g("L1_ffn_w_gate")
    com["w_up1"] = g("L1_ffn_w_up")
    com["w_down1"] = g("L1_ffn_w_down")
    com["identf"] = np.eye(128, dtype=np.float32)
    inv = (1.0 / (10000.0 ** (np.arange(0, DH, 2, dtype=np.float32) / DH)))
    ang = np.arange(T, dtype=np.float32)[:, None] * inv[None, :]
    com["cos_t"] = np.cos(ang).astype(np.float32)
    com["sin_t"] = np.sin(ang).astype(np.float32)
    causal = (np.arange(CS)[None, :] <= np.arange(L)[:, None])
    com["cmask"] = np.where(causal, 0.0, MASKVAL).astype(np.float32)
    return com


def kernel(**inputs):
    com = _host_layout(inputs)
    x = np.ascontiguousarray(np.asarray(inputs["L0_attn_x"], dtype=np.float32))
    nc = build_program()
    in_maps = []
    for c in range(NCORES):
        m = dict(com)
        m["x_in"] = np.ascontiguousarray(x[c])
        in_maps.append(m)
    res = run_bass_kernel_spmd(nc, in_maps, core_ids=list(range(NCORES)))
    return np.stack([np.asarray(r["out"], dtype=np.float32) for r in res.results])


if __name__ == "__main__":
    build_program(phase_limit=1)
    print("built")
